# version 05
# baseline (speedup 1.0000x reference)
import numpy as np
import ml_dtypes
import concourse.bass as bass
import concourse.mybir as mybir
from concourse.bass_utils import run_bass_kernel_spmd

F32 = mybir.dt.float32
BF16 = mybir.dt.bfloat16
AF = mybir.ActivationFunctionType
ALU = mybir.AluOpType
AX = mybir.AxisListType
BF_NP = ml_dtypes.bfloat16

D = 2048
NT = 16
TP = 512
NCORES = 8

OWNER = {
    "cx": "sync", "cst": "sync", "x1l": "sync", "cc": "sync", "cr": "sync",
    "cm": "sync", "ctr": "sync", "cts": "sync",
    "cl": "gpsimd", "wa": "gpsimd", "wb": "gpsimd", "wc": "gpsimd",
    "wd": "gpsimd",
    "pe": "tensor", "dve": "vector", "act": "scalar", "gp": "gpsimd",
}
DMA_LBL = {"cx", "cst", "x1l", "cl", "wa", "wb", "wc", "wd", "cc", "cr", "cm", "ctr", "cts"}

DEBUG_SPEC = []
PHASES = 99
ATTN_SUB = 3
TR_LIMIT = 9999
TR_EVAC = True
ATTN_BARRIER = False
HEADS_ONLY = None
SCORES_RAW = False


class Sch:
    ENGS = ("sync", "gpsimd", "tensor", "vector", "scalar")

    def __init__(self):
        self.ops = {e: [] for e in self.ENGS}
        self.cnt = {k: 0 for k in OWNER}

    def tok(self, lbl):
        return (lbl, self.cnt[lbl])

    def emit(self, eng, fn, waits=(), inc=None, force=()):
        wmax = {}
        for t in waits:
            if t is None:
                continue
            lbl, val = t[0], t[1]
            if val <= 0:
                continue
            if lbl in DMA_LBL:
                val = self.cnt[lbl]
                if val <= 0:
                    continue
            elif OWNER[lbl] == eng and lbl not in force:
                continue
            if wmax.get(lbl, 0) < val:
                wmax[lbl] = val
        amt = 0
        if inc is not None:
            amt = 16 if inc in DMA_LBL else 1
            self.cnt[inc] += amt
        self.ops[eng].append((sorted(wmax.items()), fn, inc, amt))
        return (inc, self.cnt[inc]) if inc is not None else None


class PBuf:
    def __init__(self):
        self.w = None
        self.r = []
        self.ap = None

    def begin_write(self):
        w = list(self.r)
        self.r = []
        return w

    def set_w(self, tok):
        self.w = tok

    def add_r(self, tok):
        if tok is not None:
            self.r.append(tok)



def _popcount_mat():
    i = np.arange(128)
    a = i[:, None] & i[None, :]
    pc = np.zeros_like(a)
    v = a.copy()
    while v.any():
        pc += v & 1
        v >>= 1
    return pc


def _h128_bf16():
    return np.where(_popcount_mat() % 2 == 0, 1.0, -1.0).astype(BF_NP)


def _masks_bf16(gamma):
    a = np.arange(128)[:, None]
    j = np.arange(512)[None, :]
    std = (j >= a + 1)
    out = np.zeros((128, 2688), np.float32)
    for qt in range(4):
        m = std.copy()
        if gamma == 0:
            m &= (j >= 512 - qt * 128)
        out[:, qt * 512:(qt + 1) * 512] = m
    b = np.arange(128)[None, :]
    out[:, 2048:2176] = (b <= a)
    out[:, 2176:2688] = std
    return out.astype(BF_NP)


def _ropes_f32(gamma):
    half = 64
    theta = 1.0 / 10000.0 ** (np.arange(half, dtype=np.float64) / half)
    th = np.concatenate([theta, theta])[:, None]
    scale = 128.0 ** -0.25
    out = np.zeros((128, 3072), np.float32)
    g0 = gamma * 1024
    t0 = np.arange(g0 - 512, g0 + 512, dtype=np.float64)[None, :]
    c0 = np.cos(th * t0) * scale
    s0 = np.sin(th * t0) * scale
    neg = (t0[0] < 0)
    c0[:, neg] = 0.0
    s0[:, neg] = 0.0
    out[:, 0:1024] = c0
    out[:, 1024:2048] = s0
    t1 = np.arange(g0 + 512, g0 + 1024, dtype=np.float64)[None, :]
    out[:, 2048:2560] = np.cos(th * t1) * scale
    out[:, 2560:3072] = np.sin(th * t1) * scale
    return out


def _pack_cols(v, n):
    return np.ascontiguousarray(v.reshape(n, 128).T)


def _pack_loraA(qA, kA, vA, oA):
    outs = []
    for A in (qA, kA, vA, oA):
        AT = np.asarray(A, np.float32).T.reshape(16, 128, 16).transpose(1, 0, 2).reshape(128, 256)
        outs.append(AT)
    return np.ascontiguousarray(np.concatenate(outs, axis=1))


def _pack_loraB(qB, kB, vB, oB):
    return np.ascontiguousarray(np.concatenate(
        [np.asarray(m, np.float32).T for m in (qB, kB, vB, oB)], axis=1))



def build_nc():
    nc = bass.Bass("TRN2", target_bir_lowering=False)

    xs_d = nc.dram_tensor("xs", [1536, D], F32, kind="ExternalInput").ap()
    wqT_d = nc.dram_tensor("wqT", [1024, 4096], F32, kind="ExternalInput").ap()
    wkT_d = nc.dram_tensor("wkT", [1024, 4096], F32, kind="ExternalInput").ap()
    wvT_d = nc.dram_tensor("wvT", [1024, 4096], F32, kind="ExternalInput").ap()
    woT_d = nc.dram_tensor("woT", [1024, 4096], F32, kind="ExternalInput").ap()
    wupT_d = nc.dram_tensor("wupT", [4096, 4096], F32, kind="ExternalInput").ap()
    wgT_d = nc.dram_tensor("wgT", [4096, 4096], F32, kind="ExternalInput").ap()
    wdT_d = nc.dram_tensor("wdT", [4096, 4096], F32, kind="ExternalInput").ap()
    loraA_d = nc.dram_tensor("loraA", [128, 1024], F32, kind="ExternalInput").ap()
    loraB_d = nc.dram_tensor("loraB", [16, 8192], F32, kind="ExternalInput").ap()
    masks_d = nc.dram_tensor("masks", [128, 2688], BF16, kind="ExternalInput").ap()
    ropes_d = nc.dram_tensor("ropes", [128, 3072], F32, kind="ExternalInput").ap()
    vecs_d = nc.dram_tensor("vecs", [128, 240], F32, kind="ExternalInput").ap()
    h128_d = nc.dram_tensor("h128", [128, 128], BF16, kind="ExternalInput").ap()
    identb_d = nc.dram_tensor("identb", [128, 128], BF16, kind="ExternalInput").ap()
    identf_d = nc.dram_tensor("identf", [128, 128], F32, kind="ExternalInput").ap()
    out_d = nc.dram_tensor("out", [1024, D], F32, kind="ExternalOutput").ap()
    x1scr_d = nc.dram_tensor("x1scr", [128, 8192], BF16, kind="Internal").ap()
    dbg_d = {}
    for name, shape, dt in DEBUG_SPEC:
        dbg_d[name] = nc.dram_tensor("dbg_" + name, list(shape), dt,
                                     kind="ExternalOutput").ap()

    S = Sch()
    T = S.tok

    from contextlib import ExitStack
    es = ExitStack()
    with es:
        acc_t = es.enter_context(nc.sbuf_tensor("acc", [128, 8192], F32))
        kfm_t = es.enter_context(nc.sbuf_tensor("kfm", [128, 16384], BF16))
        vtm_t = es.enter_context(nc.sbuf_tensor("vtm", [128, 16384], BF16))
        bigA_t = es.enter_context(nc.sbuf_tensor("bigA", [128, 32768], BF16))
        par_t = es.enter_context(nc.sbuf_tensor("parena", [128, 12288], BF16))
        war_t = es.enter_context(nc.sbuf_tensor("warena", [128, 8192], BF16))
        loraA_s = es.enter_context(nc.sbuf_tensor("loraA_s", [128, 1024], BF16))
        h128_s = es.enter_context(nc.sbuf_tensor("h128_s", [128, 128], BF16))
        identb_s = es.enter_context(nc.sbuf_tensor("identb_s", [128, 128], BF16))
        identf_s = es.enter_context(nc.sbuf_tensor("identf_s", [128, 128], F32))
        vecs_s = es.enter_context(nc.sbuf_tensor("vecs_s", [128, 240], F32))
        dvecs_s = es.enter_context(nc.sbuf_tensor("dvecs", [128, 144], F32))
        stats_s = es.enter_context(nc.sbuf_tensor("stats", [128, 32], F32))
        smax_s = es.enter_context(nc.sbuf_tensor("smax", [128, 8], F32))
        pA_t = es.enter_context(nc.psum_tensor("pA", [128, 512], F32))
        pB_t = es.enter_context(nc.psum_tensor("pB", [128, 512], F32))
        pC_t = es.enter_context(nc.psum_tensor("pC", [128, 512], F32))
        pD_t = es.enter_context(nc.psum_tensor("pD", [128, 512], F32))
        pT_t = es.enter_context(nc.psum_tensor("pT", [128, 2048], BF16))
        pE_t = es.enter_context(nc.psum_tensor("pE", [128, 512], F32))
        pF_t = es.enter_context(nc.psum_tensor("pF", [128, 512], F32))
        s_cx = es.enter_context(nc.semaphore("s_cx"))
        s_cc = es.enter_context(nc.semaphore("s_cc"))
        s_cr = es.enter_context(nc.semaphore("s_cr"))
        s_cm = es.enter_context(nc.semaphore("s_cm"))
        s_ctr = es.enter_context(nc.semaphore("s_ctr"))
        s_cts = es.enter_context(nc.semaphore("s_cts"))
        s_cst = es.enter_context(nc.semaphore("s_cst"))
        s_x1l = es.enter_context(nc.semaphore("s_x1l"))
        s_cl = es.enter_context(nc.semaphore("s_cl"))
        s_wa = es.enter_context(nc.semaphore("s_wa"))
        s_wb = es.enter_context(nc.semaphore("s_wb"))
        s_wc = es.enter_context(nc.semaphore("s_wc"))
        s_wd = es.enter_context(nc.semaphore("s_wd"))
        s_pe = es.enter_context(nc.semaphore("s_pe"))
        s_dve = es.enter_context(nc.semaphore("s_dve"))
        s_act = es.enter_context(nc.semaphore("s_act"))
        s_gp = es.enter_context(nc.semaphore("s_gp"))
        block = es.enter_context(nc.Block())

        sems = {"cx": s_cx, "cst": s_cst, "x1l": s_x1l, "cl": s_cl, "wa": s_wa,
                "wb": s_wb, "wc": s_wc, "wd": s_wd, "pe": s_pe, "dve": s_dve, "act": s_act, "gp": s_gp,
                "cc": s_cc, "cr": s_cr, "cm": s_cm, "ctr": s_ctr, "cts": s_cts}

        acc = acc_t[:]
        kfm = kfm_t[:]
        vtm = vtm_t[:]
        bigA = bigA_t[:]
        par = par_t[:]
        war = war_t[:]

        def h_tile(f, c0, w):
            return bigA[:, f * 1024 + c0: f * 1024 + c0 + w]

        def q_tile(o, c0=0, w=TP):
            return bigA[:, 16384 + o * 512 + c0: 16384 + o * 512 + c0 + w]

        def ao_tile(o, c0=0, w=TP):
            return bigA[:, 24576 + o * 512 + c0: 24576 + o * 512 + c0 + w]

        def z_tile(m, c0=0, w=TP):
            return bigA[:, m * 512 + c0: m * 512 + c0 + w]

        ropes_f = bigA[:, 24576:32768].bitcast(F32)
        x0_tile = q_tile
        x1b_tile = ao_tile

        xstage = par[:, 0:4096].bitcast(F32)

        def xn_buf(p):
            return par[:, 4096 + p * 2048: 6144 + p * 2048]

        def hst_buf(p):
            return par[:, 8192 + p * 2048: 10240 + p * 2048]

        lbq = par[0:16, 0:2048]
        lbk = par[0:16, 2048:4096]
        lbv = par[0:16, 4096:6144]
        s1q = par[0:16, 6144:6656]
        s1k = par[0:16, 6656:7680]
        s1v = par[0:16, 7680:8704]
        rotb = par[:, 8704:9216]
        masks_s = par[:, 0:2688]

        def ebuf(p):
            return par[:, 2688 + p * 640: 3328 + p * 640]

        def atbuf(p):
            return par[:, 3968 + p * 640: 4608 + p * 640]

        s1o = par[0:16, 5248:5760]
        oBt = par[0:16, 5760:7808]
        def ostage_p(pp):
            return par[:, 10240 + pp * 1024: 11264 + pp * 1024].bitcast(F32)
        x1 = par[:, 0:8192]

        def x1_tile(f):
            return par[:, f * 512:(f + 1) * 512]

        def ubuf(p):
            return par[:, 8192 + p * 512: 8704 + p * 512]

        def gbuf(p):
            return par[:, 9216 + p * 512: 9728 + p * 512]

        spare_bf_ap = par[:, 10240:10752]
        spare_bf_ap2 = par[:, 10752:11264]

        normw_v = vecs_s[:, 0:16]
        suug_v = vecs_s[:, 16:32]
        wscup_v = vecs_s[:, 32:96]
        wscg_v = vecs_s[:, 96:160]
        sudp_v = vecs_s[:, 160:224]
        wscdp_v = vecs_s[:, 224:240]
        su_a = dvecs_s[:, 0:16]
        su_b = dvecs_s[:, 16:80]
        wscup_su = dvecs_s[:, 80:144]

        ST = {n: i for i, n in enumerate(
            ["ssum", "ssq", "m", "q2", "mm", "var", "varp", "rp", "qq", "qq2",
             "rp2", "rstd", "rstd2", "nmr", "onep"])}

        def st(n, p):
            c = ST[n] * 2 + p
            return stats_s[:, c:c + 1]

        pA, pB, pC, pD = PBuf(), PBuf(), PBuf(), PBuf()
        pE, pF = PBuf(), PBuf()
        pT0, pT1 = PBuf(), PBuf()
        pA.ap, pB.ap, pC.ap, pD.ap = pA_t[:], pB_t[:], pC_t[:], pD_t[:]
        pE.ap, pF.ap = pE_t[:, 0:128], pF_t[:, 0:128]
        pE.full, pF.full = pE_t[:], pF_t[:]
        pT0.ap, pT1.ap = pT_t[:, 0:1024], pT_t[:, 1024:2048]

        xstage_b = PBuf()
        xnb = [PBuf(), PBuf()]
        hstb = [PBuf(), PBuf()]
        wslot = [PBuf(), PBuf(), PBuf(), PBuf()]
        wslot_lbl = ["wa", "wb", "wc", "wd"]
        rot_b = PBuf()
        eb = [PBuf(), PBuf()]
        atb = [PBuf(), PBuf()]
        ub = [PBuf(), PBuf()]
        gb = [PBuf(), PBuf()]
        stat_par = [PBuf(), PBuf()]
        smax_par = [PBuf(), PBuf()]
        ost_b2 = [PBuf(), PBuf()]

        wload_i = [0]
        wring = [2]

        def wslot_ap(s, a, b2):
            if s < 2:
                return war[:, s * 4096 + a: s * 4096 + b2]
            return vtm[:, (s - 2) * 4096 + a: (s - 2) * 4096 + b2]

        def wload(in_ap, nelem=4096, extra=()):
            s = wload_i[0] % wring[0]
            wload_i[0] += 1
            b = wslot[s]
            out_ap = wslot_ap(s, 0, nelem)
            waits = b.begin_write() + list(extra)
            t = S.emit("gpsimd",
                       lambda g, o=out_ap, i=in_ap: g.dma_start(out=o, in_=i),
                       waits=waits, inc=wslot_lbl[s])
            b.set_w(t)
            return s, t

        def wchunk(s, d, w, off=0):
            return wslot_ap(s, d * w + off, d * w + w)

        taps = {}

        t_vecs = S.emit("sync", lambda sy: sy.dma_start(out=vecs_s[:], in_=vecs_d), inc="cc")
        t_h128 = S.emit("sync", lambda sy: sy.dma_start(out=h128_s[:], in_=h128_d), inc="cc")
        t_idb = S.emit("sync", lambda sy: sy.dma_start(out=identb_s[:], in_=identb_d), inc="cc")
        t_idf = S.emit("sync", lambda sy: sy.dma_start(out=identf_s[:], in_=identf_d), inc="cc")
        t_loraA = S.emit("gpsimd", lambda g: g.dma_start(out=loraA_s[:], in_=loraA_d), inc="cl")

        t1 = S.emit("vector", lambda v: v.tensor_scalar(
            su_a, suug_v, float(1.0 / np.sqrt(2048.0)), None, op0=ALU.mult),
            waits=[t_vecs], inc="dve")
        t2 = S.emit("vector", lambda v: v.tensor_scalar(
            su_b, sudp_v, float(1.0 / np.sqrt(8192.0)), None, op0=ALU.mult),
            inc="dve")
        t_dv = S.emit("vector", lambda v: v.tensor_mul(wscup_su, wscup_v, su_b),
                      waits=[t2], force=("dve",), inc="dve")


        acc_last = [[None] * 4, [None] * 4]
        store_toks = [[None] * 4, [None] * 4]
        phase_end = {}
        t_x1l_prev = [None]

        for p in range(2):
            n_tiles = 8 if p == 0 else 4
            tile_rows0 = 0 if p == 0 else 1024
            h_ready = []
            x0_ready = []
            gate_bigA = []
            gate_par = []
            gate_x = []
            if p == 1:
                gate_bigA = [phase_end["ffn_pe"], phase_end["ffn_dve"],
                             phase_end["ffn_act"]]
                gate_par = gate_bigA
                gate_x = [phase_end["ug_pe"]]

            ptq_rot = [pT0, pT1]
            ptq_i = [0]

            for i in range(n_tiles):
                par2 = i % 2
                row0 = tile_rows0 + i * 128
                own = (p == 1) or (i >= 4)
                hc = row0 - tile_rows0
                waits = xstage_b.begin_write() + (gate_x if i == 0 else [])
                tx = S.emit("sync", lambda sy, r=row0: sy.dma_start(
                    out=xstage, in_=xs_d[r:r + 128, :]), waits=waits, inc="cx")
                xstage_b.set_w(tx)

                sb = stat_par[par2]
                wstat = sb.begin_write()
                t_sum = S.emit("vector", lambda v, pp=par2: v.reduce_sum(
                    st("ssum", pp), xstage, axis=AX.X),
                    waits=[tx] + wstat, inc="dve", force=("dve",))
                t_sq = S.emit("scalar", lambda s, pp=par2: s.activation(
                    hst_buf(pp), xstage, AF.Square, accum_out=st("ssq", pp)),
                    waits=[tx] + wstat + hstb[par2].begin_write() +
                          (gate_par if i < 2 else []),
                    inc="act", force=("act",))

                def dvstep(fn, waits=()):
                    return S.emit("vector", fn, waits=list(waits), inc="dve",
                                  force=("dve",))

                td = dvstep(lambda v, pp=par2: v.tensor_scalar(
                    st("m", pp), st("ssum", pp), 1.0 / 2048.0, None, op0=ALU.mult),
                    [t_sum])
                td = dvstep(lambda v, pp=par2: v.tensor_scalar(
                    st("q2", pp), st("ssq", pp), 1.0 / 2048.0, None, op0=ALU.mult),
                    [td, t_sq])
                td = dvstep(lambda v, pp=par2: v.tensor_mul(
                    st("mm", pp), st("m", pp), st("m", pp)), [td])
                td = dvstep(lambda v, pp=par2: v.tensor_sub(
                    st("var", pp), st("q2", pp), st("mm", pp)), [td])
                td = dvstep(lambda v, pp=par2: v.tensor_scalar(
                    st("varp", pp), st("var", pp), 1e-5, None, op0=ALU.add), [td])
                td = dvstep(lambda v, pp=par2: v.reciprocal(
                    st("rp", pp), st("varp", pp)), [td])
                td = dvstep(lambda v, pp=par2: v.tensor_mul(
                    st("qq", pp), st("var", pp), st("rp", pp)), [td])
                td = dvstep(lambda v, pp=par2: v.tensor_scalar(
                    st("qq2", pp), st("qq", pp), 1e-6, None, op0=ALU.add), [td])
                td = dvstep(lambda v, pp=par2: v.reciprocal(
                    st("rp2", pp), st("qq2", pp)), [td])
                tsq1 = S.emit("scalar", lambda s, pp=par2: s.sqrt(
                    st("rstd", pp), st("rp", pp)), waits=[td], inc="act",
                    force=("act",))
                tsq2 = S.emit("scalar", lambda s, pp=par2: s.sqrt(
                    st("rstd2", pp), st("rp2", pp)), waits=[td], inc="act",
                    force=("act",))
                t_nmr = dvstep(lambda v, pp=par2: v.tensor_scalar(
                    st("nmr", pp), st("m", pp), st("rstd", pp), -1.0,
                    op0=ALU.mult, op1=ALU.mult), [tsq1])
                t_on = dvstep(lambda v, pp=par2: v.tensor_scalar(
                    st("onep", pp), st("rstd", pp), 1.0, None, op0=ALU.add), [tsq1])

                wxn = xnb[par2].begin_write()
                txn = S.emit("scalar", lambda s, pp=par2: s.activation(
                    xn_buf(pp), xstage, AF.Identity, bias=st("nmr", pp),
                    scale=st("rstd", pp)),
                    waits=[t_nmr, tsq1] + wxn + (gate_par if i < 2 else []),
                    inc="act", force=("act",))
                xnb[par2].set_w(txn)
                t_xlast = txn
                if own:
                    tt = (row0 - (512 + p * 512)) // 128
                    tacc = S.emit("scalar", lambda s, pp=par2, t_=tt: s.activation(
                        acc[:, t_ * 2048:(t_ + 1) * 2048], xstage, AF.Identity,
                        bias=st("nmr", pp), scale=st("onep", pp)),
                        waits=[t_on, t_nmr,
                               store_toks[0][tt] if p == 1 else None],
                        inc="act", force=("act",))
                    acc_last[p][tt] = tacc
                    t_xlast = tacc
                xstage_b.add_r(td)
                xstage_b.add_r(t_xlast)
                th = S.emit("scalar", lambda s, pp=par2: s.activation(
                    hst_buf(pp), xn_buf(pp), AF.Identity, scale=st("rstd2", pp)),
                    waits=[txn, tsq2], inc="act", force=("act",))
                hstb[par2].set_w(th)

                for half in range(2):
                    q = ptq_rot[ptq_i[0] % 2]
                    ptq_i[0] += 1
                    wq_ = q.begin_write()
                    tpe = None
                    for k in range(8):
                        f = half * 8 + k
                        tpe = S.emit("tensor", lambda t, f_=f, q_=q, k_=k, pp=par2:
                                     t.transpose(
                                         q_.ap[:, k_ * 128:(k_ + 1) * 128],
                                         hst_buf(pp)[:, f_ * 128:(f_ + 1) * 128],
                                         identb_s[:]),
                                     waits=([th, t_idb] + wq_ + gate_bigA)
                                     if k == 0 else [],
                                     inc="pe" if k == 7 else None)
                    q.set_w(tpe)
                    hstb[par2].add_r(tpe)
                    tev = None
                    for k in range(8):
                        f = half * 8 + k
                        tev = S.emit("scalar", lambda s, f_=f, q_=q, k_=k, c_=hc:
                                     s.mul(h_tile(f_, c_, 128),
                                           q_.ap[:, k_ * 128:(k_ + 1) * 128],
                                           normw_v[:, f_:f_ + 1]),
                                     waits=[tpe] if k == 0 else [],
                                     inc="act" if k == 7 else None)
                    q.add_r(tev)
                    h_ready.append(tev)
                if own:
                    oc = hc - (512 if p == 0 else 0)
                    for half in range(2):
                        q = ptq_rot[ptq_i[0] % 2]
                        ptq_i[0] += 1
                        wq_ = q.begin_write()
                        tpe = None
                        for k in range(8):
                            f = half * 8 + k
                            tpe = S.emit("tensor", lambda t, f_=f, q_=q, k_=k, pp=par2:
                                         t.transpose(
                                             q_.ap[:, k_ * 128:(k_ + 1) * 128],
                                             xn_buf(pp)[:, f_ * 128:(f_ + 1) * 128],
                                             identb_s[:]),
                                         waits=([txn, t_idb] + wq_ + gate_bigA)
                                         if k == 0 else [],
                                         inc="pe" if k == 7 else None)
                        q.set_w(tpe)
                        xnb[par2].add_r(tpe)
                        tev = None
                        for k in range(8):
                            f = half * 8 + k
                            tev = S.emit("vector", lambda v, f_=f, q_=q, k_=k, o_=oc:
                                         v.tensor_scalar(
                                             x0_tile(f_, o_, 128),
                                             q_.ap[:, k_ * 128:(k_ + 1) * 128],
                                             su_a[:, f_:f_ + 1], None, op0=ALU.mult),
                                         waits=[tpe, t1] if k == 0 else [],
                                         inc="dve" if k == 7 else None)
                        q.add_r(tev)
                        x0_ready.append(tev)

            x1_toks = []
            for f in range(NT):
                q = [pA, pB][f % 2]
                wq_ = q.begin_write()
                tpe = S.emit("tensor", lambda t, f_=f, q_=q: t.matmul(
                    q_.ap, h128_s[:], x0_tile(f_), start=True, stop=True),
                    waits=wq_ + x0_ready + [t_h128], inc="pe")
                q.set_w(tpe)
                tev = S.emit("vector", lambda v, f_=f, q_=q: v.tensor_copy(
                    x1b_tile(f_), q_.ap), waits=[tpe] + (gate_bigA if f < 2 else []),
                    inc="dve")
                q.add_r(tev)
                x1_toks.append(tev)
            cur_in, cur_out = x1b_tile, x0_tile
            stg_tok = x1_toks
            for s_i in range(4):
                newtoks = []
                for ho in range(2):
                    def bf_fn(v, s_=s_i, h_=ho, ci=cur_in, co=cur_out):
                        restv = (1 << s_) * 512
                        iv = ci(0, 0, 8192).rearrange("p (g t r) -> p g t r",
                                                      t=2, r=restv)
                        ov = co(0, 0, 8192).rearrange("p (g t r) -> p g t r",
                                                      t=2, r=restv)
                        if h_ == 0:
                            return v.tensor_add(ov[:, :, 0, :], iv[:, :, 0, :],
                                                iv[:, :, 1, :])
                        return v.tensor_sub(ov[:, :, 1, :], iv[:, :, 0, :],
                                            iv[:, :, 1, :])
                    tb = S.emit("vector", bf_fn,
                                waits=(stg_tok if ho == 0 else []),
                                inc="dve", force=("dve",))
                    newtoks.append(tb)
                stg_tok = newtoks
                cur_in, cur_out = cur_out, cur_in
            t_x1st = S.emit("sync", lambda sy: sy.dma_start(
                out=x1scr_d, in_=bigA[:, 24576:32768]),
                waits=stg_tok + [t_x1l_prev[0]], inc="cst")

            if p == 0:
                t_ropes = S.emit("sync", lambda sy: sy.dma_start(
                    out=ropes_f[:, 0:2048], in_=ropes_d[:, 0:2048]),
                    waits=[t_x1st], inc="cr")

                def cosv(c0, w):
                    return ropes_f[:, c0:c0 + w]

                def sinv(c0, w):
                    return ropes_f[:, 1024 + c0:1024 + c0 + w]
            else:
                t_ropes = S.emit("sync", lambda sy: sy.dma_start(
                    out=ropes_f[:, 0:1024], in_=ropes_d[:, 2048:3072]),
                    waits=[t_x1st], inc="cr")

                def cosv(c0, w):
                    return ropes_f[:, c0:c0 + w]

                def sinv(c0, w):
                    return ropes_f[:, 512 + c0:512 + c0 + w]

            if PHASES < 2:
                break
            own_c0 = 512 if p == 0 else 0
            ln_end = [T("act"), T("dve"), T("pe")]
            tlb = []
            for mi, (dst, c0) in enumerate([(lbq, 0), (lbk, 2048), (lbv, 4096)]):
                tt_ = S.emit("gpsimd", lambda g, d_=dst, c_=c0: g.dma_start(
                    out=d_, in_=loraB_d[:, c_:c_ + 2048]),
                    waits=ln_end if mi == 0 else [], inc="cl")
                tlb.append(tt_)
            t_lb = tlb[-1]

            s1_rot = [pA, pB, pC, pD]
            s1_jobs = [("q", s1q, 0, [(own_c0, 512, 0)]),
                       ("k", s1k, 256,
                        [(0, 512, 0), (512, 512, 512)] if p == 0 else [(0, 512, 0)]),
                       ("v", s1v, 512,
                        [(0, 512, 0), (512, 512, 512)] if p == 0 else [(0, 512, 0)])]
            s1_toks = {}
            ji = 0
            for name, dst, acol, groups in s1_jobs:
                for (hc0, w, dcol) in groups:
                    q = s1_rot[ji % 4]
                    ji += 1
                    wq_ = q.begin_write()
                    tpe = None
                    for d in range(NT):
                        tpe = S.emit("tensor", lambda t, d_=d, q_=q, a_=acol, h0=hc0,
                                     w_=w: t.matmul(
                                         q_.ap[0:16, 0:w_],
                                         loraA_s[:, a_ + d_ * 16: a_ + d_ * 16 + 16],
                                         h_tile(d_, h0, w_),
                                         start=(d_ == 0), stop=(d_ == 15)),
                                     waits=(wq_ + h_ready + [t_loraA])
                                     if d == 0 else [],
                                     inc="pe" if d == 15 else None)
                    q.set_w(tpe)
                    tev = S.emit("vector", lambda v, q_=q, d_=dst, c_=dcol, w_=w:
                                 v.tensor_scalar(d_[:, c_:c_ + w_],
                                                 q_.ap[0:16, 0:w_],
                                                 1.0 / 16.0, None, op0=ALU.mult),
                                 waits=[tpe] + (gate_par if ji <= 4 else []),
                                 inc="dve")
                    q.add_r(tev)
                    s1_toks.setdefault(name, []).append(tev)

            def rope_ops(buf_ap, cos_ap, sin_ap, t_ev):
                tr1 = S.emit("scalar", lambda s: s.mul(
                    rotb[0:64, :], buf_ap[64:128, :], -1.0),
                    waits=[t_ev, t_ropes] + rot_b.begin_write(),
                    inc="act", force=("act",))
                tr2 = S.emit("scalar", lambda s: s.copy(
                    rotb[64:128, :], buf_ap[0:64, :]), inc="act")
                rot_b.set_w(tr2)
                tm1 = S.emit("vector", lambda v: v.tensor_mul(buf_ap, buf_ap, cos_ap),
                             waits=[tr2], inc="dve")
                tm2 = S.emit("vector", lambda v: v.tensor_mul(rotb, rotb, sin_ap),
                             inc="dve", force=("dve",))
                tm3 = S.emit("vector", lambda v: v.tensor_add(buf_ap, buf_ap, rotb),
                             waits=[tm1, tm2], inc="dve", force=("dve",))
                rot_b.add_r(tm3)
                return tm3

            q_rope = [None] * 16
            k_rope = [[] for _ in range(16)]

            for mat_d, isQ in [(wqT_d, True), (wkT_d, False)]:
                for g in range(8):
                    sl, tw = wload(mat_d[g * 128:(g + 1) * 128, :])
                    for sub in range(2):
                        o = g * 2 + sub
                        if isQ:
                            segs = [(own_c0, 512, 0)]
                        else:
                            segs = ([(0, 512, 0), (512, 512, 512)]
                                    if p == 0 else [(0, 512, 0)])
                        for si, (hc0, w, kc0) in enumerate(segs):
                            q = [pA, pB][(o + si) % 2]
                            wq_ = q.begin_write()
                            tpe = None
                            for d in range(NT):
                                tpe = S.emit("tensor", lambda t, d_=d, q_=q, s_=sl,
                                             h0=hc0, w_=w, sub_=sub: t.matmul(
                                                 q_.ap[:, 0:w_],
                                                 wchunk(s_, d_, 256, sub_ * 128)[:, 0:128],
                                                 h_tile(d_, h0, w_),
                                                 start=(d_ == 0), stop=False),
                                             waits=(wq_ + [tw] + h_ready)
                                             if d == 0 else [], inc=None)
                            lb = lbq if isQ else lbk
                            s1buf = s1q if isQ else s1k
                            tpe = S.emit("tensor", lambda t, q_=q, o_=o, lb_=lb,
                                         s1_=s1buf, w_=w, kc=kc0: t.matmul(
                                             q_.ap[:, 0:w_],
                                             lb_[:, o_ * 128:(o_ + 1) * 128],
                                             s1_[:, kc:kc + w_],
                                             start=False, stop=True),
                                         waits=[t_lb] + s1_toks["q" if isQ else "k"],
                                         inc="pe")
                            q.set_w(tpe)
                            wslot[sl].add_r(tpe)
                            if isQ:
                                dst = q_tile(o)
                                tev = S.emit("vector", lambda v, d_=dst, q_=q:
                                             v.tensor_copy(d_, q_.ap),
                                             waits=[tpe, t_x1st], inc="dve")
                                q.add_r(tev)
                                q_rope[o] = rope_ops(dst, cosv(own_c0, 512),
                                                     sinv(own_c0, 512), tev)
                            else:
                                dst = kfm[:, o * 1024 + kc0: o * 1024 + kc0 + w]
                                war_gate = ([phase_end["attn_pe"]]
                                            if p == 1 else [])
                                tev = S.emit("vector", lambda v, d_=dst, q_=q:
                                             v.tensor_copy(d_, q_.ap),
                                             waits=[tpe] + war_gate, inc="dve")
                                q.add_r(tev)
                                k_rope[o].append(rope_ops(dst, cosv(kc0, w),
                                                          sinv(kc0, w), tev))

            nchunks = 8 if p == 0 else 4
            vp = [pA, pB, pC, pD]
            v_evac = {}
            vj = 0
            for g in range(8):
                sl, tw = wload(wvT_d[g * 128:(g + 1) * 128, :])
                for c in range(nchunks):
                    q = vp[vj % 4]
                    vj += 1
                    wq_ = q.begin_write()
                    col = 0
                    tpe = None
                    for d in range(NT):
                        tpe = S.emit("tensor", lambda t, d_=d, q_=q, s_=sl, c_=c,
                                     col_=col: t.matmul(
                                         q_.ap[:, col_:col_ + 256],
                                         h_tile(d_, c_ * 128, 128),
                                         wchunk(s_, d_, 256),
                                         start=(d_ == 0), stop=False),
                                     waits=(wq_ + [tw] + h_ready)
                                     if d == 0 else [], inc=None)
                    tpe = S.emit("tensor", lambda t, q_=q, g_=g, c_=c, col_=col:
                                 t.matmul(q_.ap[:, col_:col_ + 256],
                                          s1v[:, c_ * 128:(c_ + 1) * 128],
                                          lbv[:, g_ * 256:(g_ + 1) * 256],
                                          start=False, stop=True),
                                 waits=[t_lb] + s1_toks["v"], inc="pe")
                    q.set_w(tpe)
                    wslot[sl].add_r(tpe)
                    war_gate = [phase_end["attn_pe"]] if p == 1 else []
                    tev = S.emit("scalar", lambda s, q_=q, c_=c, g_=g, col_=col:
                                 s.copy(vtm[:, c_ * 2048 + g_ * 256:
                                            c_ * 2048 + (g_ + 1) * 256],
                                        q_.ap[:, col_:col_ + 256]),
                                 waits=[tpe] + war_gate, inc="act")
                    q.add_r(tev)
                    v_evac.setdefault(c, []).append(tev)

            qkv_pe_end = T("pe")
            if PHASES < 3:
                break

            t_masks = S.emit("sync", lambda sy: sy.dma_start(out=masks_s, in_=masks_d),
                             waits=[qkv_pe_end, T("dve"), T("act")], inc="cm")

            def kcols_for(qt):
                if p == 0:
                    sa = [(0, 512, qt * 128)]
                    sb = (512, 128, qt * 128 + 512)
                else:
                    jb = 512 - qt * 128
                    sa = [(0, jb, 512 + qt * 128)]
                    if qt > 0:
                        sa.append((jb, qt * 128, 0))
                    sb = (512, 128, qt * 128)
                return sa, sb

            def vchunk_for(qt, j5):
                if p == 0:
                    return qt + j5
                r = qt + j5
                return 4 + r if r < 4 else r - 4

            sa_p = [pA, pB]
            sb_p = [pE, pF]
            av_p = [pC, pD]
            attn_done = []
            heads = [(qt, hh) for qt in range(4) for hh in range(16)]
            NH = len(heads)
            htok = {}

            def em_scores(gi):
                qt, hh = heads[gi]
                pi = gi % 2
                sa, sb = kcols_for(qt)
                qa, qb = sa_p[pi], sb_p[pi]
                wq_ = qa.begin_write()
                tsc = None
                for ii, (j0, w, kc) in enumerate(sa):
                    tsc = S.emit("tensor", lambda t, q_=qa, j_=j0, w_=w, kc_=kc,
                                 h_=hh, qt_=qt: t.matmul(
                                     q_.ap[:, j_:j_ + w_],
                                     q_tile(h_, qt_ * 128, 128),
                                     kfm[:, h_ * 1024 + kc_: h_ * 1024 + kc_ + w_],
                                     start=True, stop=True),
                                 waits=(wq_ + [q_rope[hh]] + k_rope[hh])
                                 if ii == 0 else [],
                                 inc="pe" if ii == len(sa) - 1 else None)
                qa.set_w(tsc)
                wq_ = qb.begin_write()
                (j0, w, kc) = sb
                tscb = S.emit("tensor", lambda t, q_=qb, kc_=kc, h_=hh, qt_=qt:
                              t.matmul(q_.ap[:, 0:128],
                                       q_tile(h_, qt_ * 128, 128),
                                       kfm[:, h_ * 1024 + kc_: h_ * 1024 + kc_ + 128],
                                       start=True, stop=True),
                              waits=wq_ + [q_rope[hh]] + k_rope[hh], inc="pe")
                qb.set_w(tscb)
                htok[(gi, "sa")] = tsc
                htok[(gi, "sb")] = tscb

            def em_softmax(gi):
                qt, hh = heads[gi]
                pi = gi % 2
                qa, qb = sa_p[pi], sb_p[pi]
                sm_b = smax_par[pi]
                e_b = eb[pi]
                if SCORES_RAW:
                    t1_ = S.emit("vector", lambda v, q_=qa, pp=pi: v.tensor_copy(
                        ebuf(pp)[:, 0:512], q_.ap[:, 0:512]),
                        waits=[htok[(gi, "sa")]] + e_b.begin_write(), inc="dve")
                    qa.add_r(t1_)
                    t2_ = S.emit("vector", lambda v, q_=qb, pp=pi: v.tensor_copy(
                        ebuf(pp)[:, 512:640], q_.ap[:, 0:128]),
                        waits=[htok[(gi, "sb")]], inc="dve")
                    qb.add_r(t2_)
                    e_b.set_w(t2_)
                    htok[(gi, "sm")] = t2_
                    return
                wsm = sm_b.begin_write()
                tm1 = S.emit("vector", lambda v, q_=qa, pp=pi: v.reduce_max(
                    smax_s[:, pp * 4: pp * 4 + 1], q_.ap[:, 0:512], axis=AX.X,
                    negate=True), waits=[htok[(gi, "sa")]] + wsm, inc="dve",
                    force=("dve",))
                tm2 = S.emit("vector", lambda v, q_=qb, pp=pi: v.reduce_max(
                    smax_s[:, pp * 4 + 1: pp * 4 + 2], q_.ap[:, 0:128], axis=AX.X,
                    negate=True), waits=[htok[(gi, "sb")]], inc="dve",
                    force=("dve",))
                tm3 = S.emit("vector", lambda v, pp=pi: v.tensor_tensor(
                    smax_s[:, pp * 4: pp * 4 + 1],
                    smax_s[:, pp * 4: pp * 4 + 1],
                    smax_s[:, pp * 4 + 1: pp * 4 + 2], op=ALU.min),
                    waits=[tm1, tm2], inc="dve", force=("dve",))
                we_ = e_b.begin_write()
                te1 = S.emit("scalar", lambda s, q_=qa, pp=pi: s.activation(
                    ebuf(pp)[:, 0:512], q_.ap[:, 0:512], AF.Exp,
                    bias=smax_s[:, pp * 4: pp * 4 + 1]),
                    waits=[tm3] + we_, inc="act")
                qa.add_r(te1)
                te2 = S.emit("scalar", lambda s, q_=qb, pp=pi: s.activation(
                    ebuf(pp)[:, 512:640], q_.ap[:, 0:128], AF.Exp,
                    bias=smax_s[:, pp * 4: pp * 4 + 1]), inc="act")
                qb.add_r(te2)
                sm_b.add_r(te2)
                if SCORES_RAW == 2:
                    e_b.set_w(te2)
                    htok[(gi, "sm")] = te2
                    return
                mask_a_c0 = qt * 512 if p == 0 else 2176
                tma = S.emit("vector", lambda v, pp=pi, mc=mask_a_c0: v.tensor_mul(
                    ebuf(pp)[:, 0:512], ebuf(pp)[:, 0:512],
                    masks_s[:, mc:mc + 512]),
                    waits=[te1, t_masks], inc="dve")
                tmb = S.emit("vector", lambda v, pp=pi: v.tensor_mul(
                    ebuf(pp)[:, 512:640], ebuf(pp)[:, 512:640],
                    masks_s[:, 2048:2176]),
                    waits=[te2], inc="dve", force=("dve",))
                trs = S.emit("vector", lambda v, pp=pi: v.reduce_sum(
                    smax_s[:, pp * 4 + 2: pp * 4 + 3], ebuf(pp)[:, 0:640],
                    axis=AX.X),
                    waits=[tma, tmb], inc="dve", force=("dve",))
                tri = S.emit("vector", lambda v, pp=pi: v.reciprocal(
                    smax_s[:, pp * 4 + 3: pp * 4 + 4],
                    smax_s[:, pp * 4 + 2: pp * 4 + 3]),
                    waits=[trs], inc="dve", force=("dve",))
                tnm = S.emit("vector", lambda v, pp=pi: v.tensor_scalar(
                    ebuf(pp)[:, 0:640], ebuf(pp)[:, 0:640],
                    smax_s[:, pp * 4 + 3: pp * 4 + 4], None, op0=ALU.mult),
                    waits=[tri], inc="dve", force=("dve",))
                e_b.set_w(tnm)
                sm_b.add_r(tnm)
                htok[(gi, "sm")] = tnm

            def em_transpose(gi):
                pi = gi % 2
                e_b, at_b = eb[pi], atb[pi]
                wat = at_b.begin_write()
                tat = None
                lbl = "ctr" if pi == 0 else "cts"
                for k in range(5):
                    tat = S.emit("sync", lambda sy, k_=k, pp=pi:
                                 sy.dma_start_transpose(
                                     atbuf(pp)[:, k_ * 128:(k_ + 1) * 128],
                                     ebuf(pp)[:, k_ * 128:(k_ + 1) * 128]),
                                 waits=([htok[(gi, "sm")]] + wat) if k == 0 else [],
                                 inc=lbl)
                e_b.add_r(tat)
                at_b.set_w(tat)
                htok[(gi, "at")] = tat

            def em_av(gi):
                qt, hh = heads[gi]
                pi = gi % 2
                qv = av_p[pi]
                at_b = atb[pi]
                wq_ = qv.begin_write()
                tpe = None
                for j5 in range(5):
                    c = vchunk_for(qt, j5)
                    tpe = S.emit("tensor", lambda t, q_=qv, j_=j5, c_=c, h_=hh,
                                 pp=pi: t.matmul(
                                     q_.ap[:, 0:128],
                                     vtm[:, c_ * 2048 + h_ * 128:
                                         c_ * 2048 + (h_ + 1) * 128],
                                     atbuf(pp)[:, j_ * 128:(j_ + 1) * 128],
                                     start=(j_ == 0), stop=(j_ == 4)),
                                 waits=(wq_ + [htok[(gi, "at")]] +
                                        v_evac.get(c, [])) if j5 == 0 else [],
                                 inc="pe" if j5 == 4 else None)
                qv.set_w(tpe)
                at_b.add_r(tpe)
                tev = S.emit("vector", lambda v, q_=qv, h_=hh, qt_=qt: v.tensor_copy(
                    ao_tile(h_, qt_ * 128, 128), q_.ap[:, 0:128]),
                    waits=[tpe, t_x1st, t_ropes], inc="dve")
                qv.add_r(tev)
                attn_done.append(tev)

            for gi in range(NH + 2):
                if HEADS_ONLY is not None and gi < NH and gi not in HEADS_ONLY:
                    continue
                if gi < NH:
                    em_scores(gi)
                    em_softmax(gi)
                if ATTN_SUB >= 2 and 1 <= gi <= NH:
                    em_transpose(gi - 1)
                if ATTN_SUB >= 3 and gi >= 2:
                    em_av(gi - 2)
                if ATTN_BARRIER:
                    toks = [T("pe"), T("dve"), T("act")]
                    for e_ in ("tensor", "vector", "scalar"):
                        S.emit(e_, None, waits=toks, force=("pe", "dve", "act"))

            phase_end["attn_pe"] = T("pe")
            if PHASES < 4:
                break

            t_oB = S.emit("gpsimd", lambda g: g.dma_start(
                out=oBt, in_=loraB_d[:, 6144:8192]), waits=[qkv_pe_end], inc="cl")
            q = pC
            wq_ = q.begin_write()
            tpe = None
            for d in range(NT):
                tpe = S.emit("tensor", lambda t, d_=d, q_=q: t.matmul(
                    q_.ap[0:16, 0:512], loraA_s[:, 768 + d_ * 16: 768 + d_ * 16 + 16],
                    ao_tile(d_), start=(d_ == 0), stop=(d_ == 15)),
                    waits=(wq_ + attn_done) if d == 0 else [],
                    inc="pe" if d == 15 else None)
            q.set_w(tpe)
            t_s1o = S.emit("vector", lambda v, q_=q: v.tensor_scalar(
                s1o[:, 0:512], q_.ap[0:16, 0:512], 1.0 / 16.0, None, op0=ALU.mult),
                waits=[tpe], inc="dve")
            q.add_r(t_s1o)

            for g in range(8):
                sl, tw = wload(woT_d[g * 128:(g + 1) * 128, :])
                for sub in range(2):
                    c = g * 2 + sub
                    q = [pA, pB][c % 2]
                    wq_ = q.begin_write()
                    tpe = None
                    for d in range(NT):
                        tpe = S.emit("tensor", lambda t, d_=d, q_=q, s_=sl,
                                     sub_=sub: t.matmul(
                                         q_.ap,
                                         wchunk(s_, d_, 256, sub_ * 128)[:, 0:128],
                                         ao_tile(d_), start=(d_ == 0), stop=False),
                                     waits=(wq_ + [tw] + attn_done)
                                     if d == 0 else [], inc=None)
                    tpe = S.emit("tensor", lambda t, q_=q, c_=c: t.matmul(
                        q_.ap, oBt[:, c_ * 128:(c_ + 1) * 128], s1o[:, 0:512],
                        start=False, stop=True), waits=[t_oB, t_s1o], inc="pe")
                    q.set_w(tpe)
                    wslot[sl].add_r(tpe)
                    osb = ost_b2[c % 2]
                    wo_ = osb.begin_write()
                    tev = S.emit("scalar", lambda s, q_=q, pp=c % 2: s.copy(
                        ostage_p(pp), q_.ap), waits=[tpe] + wo_, inc="act")
                    q.add_r(tev)
                    osb.set_w(tev)
                    last_add = None
                    for tt in range(4):
                        qtr = [pE, pF][tt % 2]
                        wq2 = qtr.begin_write()
                        tpt = S.emit("tensor", lambda t, q_=qtr, t_=tt, pp=c % 2:
                                     t.transpose(
                                         q_.ap, ostage_p(pp)[:, t_ * 128:(t_ + 1) * 128],
                                         identf_s[:]),
                                     waits=[tev, t_idf] + wq2, inc="pe")
                        qtr.set_w(tpt)
                        tadd = S.emit("vector", lambda v, q_=qtr, t_=tt, c_=c:
                                      v.tensor_add(
                                          acc[:, t_ * 2048 + c_ * 128:
                                              t_ * 2048 + (c_ + 1) * 128],
                                          acc[:, t_ * 2048 + c_ * 128:
                                              t_ * 2048 + (c_ + 1) * 128],
                                          q_.ap),
                                      waits=[tpt, acc_last[p][tt]], inc="dve")
                        qtr.add_r(tadd)
                        acc_last[p][tt] = tadd
                        last_add = tadd
                    osb.add_r(last_add)

            phase_end["oproj_pe"] = T("pe")
            phase_end["oproj_dve"] = T("dve")
            phase_end["oproj_act"] = T("act")
            if PHASES < 5:
                break

            t_x1l = S.emit("sync", lambda sy: sy.dma_start(out=x1, in_=x1scr_d),
                           waits=[t_x1st, phase_end["oproj_pe"],
                                  phase_end["oproj_dve"], phase_end["oproj_act"]],
                           inc="x1l")
            wring[0] = 4
            wload_i[0] = wload_i[0] - (wload_i[0] % 2)
            vtm_gate = [phase_end["attn_pe"]]
            t_x1l_prev[0] = t_x1l
            z_toks = [None] * 64
            zh_tok = [None] * 64
            slot_ap = [z_tile(i) for i in range(64)]
            bspare = {"vector": spare_bf_ap, "gpsimd": spare_bf_ap2}
            bchain = {"vector": None, "gpsimd": None}
            bcnt = [0]
            for mg in range(32):
                sl_u, tw_u = wload(wupT_d[mg * 128:(mg + 1) * 128, :],
                                   extra=(vtm_gate if mg < 2 else ()))
                sl_g, tw_g = wload(wgT_d[mg * 128:(mg + 1) * 128, :],
                                   extra=(vtm_gate if mg < 2 else ()))
                for sub in range(2):
                    mt = mg * 2 + sub
                    pu = [pA, pC][mt % 2]
                    pg = [pB, pD][mt % 2]
                    wq_ = pu.begin_write()
                    tpe = None
                    for d in range(NT):
                        tpe = S.emit("tensor", lambda t, d_=d, q_=pu, s_=sl_u,
                                     sub_=sub: t.matmul(
                                         q_.ap,
                                         wchunk(s_, d_, 256, sub_ * 128)[:, 0:128],
                                         x1_tile(d_), start=(d_ == 0),
                                         stop=(d_ == 15)),
                                     waits=(wq_ + [tw_u, t_x1l])
                                     if d == 0 else [],
                                     inc="pe" if d == 15 else None)
                    pu.set_w(tpe)
                    wslot[sl_u].add_r(tpe)
                    wu_ = ub[mt % 2].begin_write()
                    tu = S.emit("vector", lambda v, q_=pu, m_=mt, pp=mt % 2:
                                v.tensor_scalar(ubuf(pp), q_.ap,
                                                wscup_su[:, m_:m_ + 1], None,
                                                op0=ALU.mult),
                                waits=[tpe, t_dv] + wu_, inc="dve")
                    pu.add_r(tu)
                    ub[mt % 2].set_w(tu)
                    wq_ = pg.begin_write()
                    tpe = None
                    for d in range(NT):
                        tpe = S.emit("tensor", lambda t, d_=d, q_=pg, s_=sl_g,
                                     sub_=sub: t.matmul(
                                         q_.ap,
                                         wchunk(s_, d_, 256, sub_ * 128)[:, 0:128],
                                         x1_tile(d_), start=(d_ == 0),
                                         stop=(d_ == 15)),
                                     waits=(wq_ + [tw_g, t_x1l])
                                     if d == 0 else [],
                                     inc="pe" if d == 15 else None)
                    pg.set_w(tpe)
                    wslot[sl_g].add_r(tpe)
                    wg_ = gb[mt % 2].begin_write()
                    tg = S.emit("scalar", lambda s, q_=pg, m_=mt, pp=mt % 2:
                                s.activation(gbuf(pp), q_.ap, AF.Silu,
                                             scale=wscg_v[:, m_:m_ + 1]),
                                waits=[tpe] + wg_, inc="act")
                    pg.add_r(tg)
                    gb[mt % 2].set_w(tg)
                    zgate = ([phase_end["oproj_pe"]] if mt >= 32
                             else [qkv_pe_end])
                    tz = S.emit("vector", lambda v, m_=mt, pp=mt % 2: v.tensor_mul(
                        z_tile(m_), gbuf(pp), ubuf(pp)),
                        waits=[tu, tg] + zgate, inc="dve", force=("dve",))
                    ub[mt % 2].add_r(tz)
                    gb[mt % 2].add_r(tz)
                    z_toks[mt] = tz
                    qh = [pE, pF][mt % 2]
                    wqh = qh.begin_write()
                    tph = S.emit("tensor", lambda t, m_=mt, q_=qh: t.matmul(
                        q_.full, h128_s[:], z_tile(m_), start=True, stop=True),
                        waits=wqh + [tz], inc="pe")
                    qh.set_w(tph)
                    tze = S.emit("scalar", lambda s, m_=mt, q_=qh: s.copy(
                        z_tile(m_), q_.full), waits=[tph], inc="act")
                    qh.add_r(tze)
                    zh_tok[mt] = tze
                    for s_i in range(6):
                        blk = 1 << (s_i + 1)
                        if (mt + 1) % blk != 0:
                            break
                        base = mt + 1 - blk
                        half = 1 << s_i
                        for k in range(half):
                            ia, ib = base + k, base + k + half
                            eng = "vector" if bcnt[0] % 2 == 0 else "gpsimd"
                            lbl = "dve" if eng == "vector" else "gp"
                            bcnt[0] += 1
                            a_ap = slot_ap[ia]
                            b_ap = slot_ap[ib]
                            sp_ap = bspare[eng]
                            tsub = S.emit(eng,
                                          lambda v, a=a_ap, b=b_ap, sp=sp_ap:
                                          v.tensor_sub(sp, a, b),
                                          waits=[zh_tok[ia], zh_tok[ib], bchain[eng]],
                                          inc=lbl, force=("dve", "gp"))
                            tadd = S.emit(eng,
                                          lambda v, a=a_ap, b=b_ap:
                                          v.tensor_add(a, a, b),
                                          waits=[tsub], inc=lbl, force=("dve", "gp"))
                            slot_ap[ib] = sp_ap
                            bspare[eng] = b_ap
                            zh_tok[ia] = tadd
                            zh_tok[ib] = tsub
                            bchain[eng] = tadd
            phase_end["ug_pe"] = T("pe")
            for sp in (spare_bf_ap, spare_bf_ap2):
                for jx in range(64):
                    if slot_ap[jx] is sp:
                        free_slot = (bspare["vector"]
                                     if bspare["vector"] is not sp and
                                     bspare["vector"] is not spare_bf_ap and
                                     bspare["vector"] is not spare_bf_ap2
                                     else bspare["gpsimd"])
                        assert free_slot is not spare_bf_ap and free_slot is not spare_bf_ap2
                        tmv = S.emit("vector", lambda v, d_=free_slot, s_=sp:
                                     v.tensor_copy(d_, s_),
                                     waits=[zh_tok[jx], bchain["vector"],
                                            bchain["gpsimd"]],
                                     inc="dve", force=("dve",))
                        if bspare["vector"] is free_slot:
                            bspare["vector"] = sp
                        else:
                            bspare["gpsimd"] = sp
                        slot_ap[jx] = free_slot
                        zh_tok[jx] = tmv
                        bchain["vector"] = tmv
                        break
            zh_fin = zh_tok

            for cs in range(8):
                pair = [pA, pB] if cs % 2 == 0 else [pC, pD]

                def dload(li, cs_=cs):
                    return wload(wdT_d[(cs_ * 4 + li) * 128:
                                       (cs_ * 4 + li + 1) * 128, :], 4096)
                loads = [dload(0), dload(1), dload(2)]
                wq_ = pair[0].begin_write() + pair[1].begin_write()
                tpe = None
                for j in range(64):
                    sl, tw = loads[j // 16]
                    jj = j % 16
                    for ct in range(2):
                        q = pair[ct]
                        rhs_ap = slot_ap[j]
                        tpe = S.emit("tensor", lambda t, q_=q, s_=sl, jj_=jj,
                                     ct_=ct, j_=j, r_=rhs_ap: t.matmul(
                                         q_.ap,
                                         wslot_ap(s_, jj_ * 256 + ct_ * 128,
                                                  jj_ * 256 + ct_ * 128 + 128),
                                         r_,
                                         start=(j_ == 0), stop=(j_ == 63)),
                                     waits=((wq_ + [tw, zh_fin[j]])
                                            if (j == 0 and ct == 0) else
                                            ([tw, zh_fin[j]] if ct == 0 else [])),
                                     inc="pe" if (jj == 15 and ct == 1) else None)
                        if ct == 1 and jj == 15:
                            wslot[sl].add_r(tpe)
                    if jj == 15 and j // 16 + 3 < 4:
                        loads.append(dload(j // 16 + 3))
                for ct in range(2):
                    pair[ct].set_w(tpe)
                for ct in range(2):
                    c = cs * 2 + ct
                    q = pair[ct]
                    osb = ost_b2[c % 2]
                    wd_ = osb.begin_write()
                    tev = S.emit("scalar", lambda s, q_=q, c_=c, pp=c % 2: s.mul(
                        ostage_p(pp), q_.ap, wscdp_v[:, c_:c_ + 1]),
                        waits=[tpe] + wd_, inc="act")
                    q.add_r(tev)
                    osb.set_w(tev)
                    last_add = None
                    for tt in range(4):
                        qtr = [pE, pF][tt % 2]
                        wq2 = qtr.begin_write()
                        tpt = S.emit("tensor", lambda t, q_=qtr, t_=tt, pp=c % 2:
                                     t.transpose(
                                         q_.ap, ostage_p(pp)[:, t_ * 128:(t_ + 1) * 128],
                                         identf_s[:]),
                                     waits=[tev, t_idf] + wq2, inc="pe")
                        qtr.set_w(tpt)
                        tadd = S.emit("vector", lambda v, q_=qtr, t_=tt, c_=c:
                                      v.tensor_add(
                                          acc[:, t_ * 2048 + c_ * 128:
                                              t_ * 2048 + (c_ + 1) * 128],
                                          acc[:, t_ * 2048 + c_ * 128:
                                              t_ * 2048 + (c_ + 1) * 128],
                                          q_.ap),
                                      waits=[tpt, acc_last[p][tt]], inc="dve")
                        qtr.add_r(tadd)
                        acc_last[p][tt] = tadd
                        last_add = tadd
                    osb.add_r(last_add)

            phase_end["ffn_pe"] = T("pe")
            phase_end["ffn_dve"] = T("dve")
            phase_end["ffn_act"] = T("act")
            wring[0] = 2
            wload_i[0] = wload_i[0] - (wload_i[0] % 2)

            if PHASES < 6 and p == 0:
                for tt in range(4):
                    store_toks[p][tt] = S.emit("sync", lambda sy, t_=tt, p_=p:
                                               sy.dma_start(
                                                   out=out_d[p_ * 512 + t_ * 128:
                                                             p_ * 512 + (t_ + 1) * 128, :],
                                                   in_=acc[:, t_ * 2048:(t_ + 1) * 2048]),
                                               waits=[acc_last[p][tt]], inc="cst")
                break
            for tt in range(4):
                store_toks[p][tt] = S.emit("sync", lambda sy, t_=tt, p_=p:
                                           sy.dma_start(
                                               out=out_d[p_ * 512 + t_ * 128:
                                                         p_ * 512 + (t_ + 1) * 128, :],
                                               in_=acc[:, t_ * 2048:(t_ + 1) * 2048]),
                                           waits=[acc_last[p][tt]], inc="cst")

        tap_map = {
            "h": bigA[:, 0:16384], "qreg": bigA[:, 16384:24576],
            "ao": bigA[:, 24576:32768], "kfm": kfm, "vtm": vtm,
            "x1": x1, "z": bigA, "acc": acc, "x1scr": x1scr_d, "par": par,
        }
        for name, shape, dt in DEBUG_SPEC:
            src = tap_map[name]
            S.emit("sync", lambda sy, s_=src, d_=dbg_d[name]: sy.dma_start(
                out=d_, in_=s_),
                waits=[T("pe"), T("dve"), T("act"), T("gp")], inc="cst")

        S.emit("sync", None, waits=[T("cst")])

        def replay(eng_name):
            def f(eng):
                for (waits, fn, inc, amt) in S.ops[eng_name]:
                    for (lbl, val) in waits:
                        eng.wait_ge(sems[lbl], val)
                    if fn is None:
                        continue
                    inst = fn(eng)
                    if inc is not None:
                        inst.then_inc(sems[inc], amt)
            return f

        block.sync(replay("sync"))
        block.gpsimd(replay("gpsimd"))
        block.tensor(replay("tensor"))
        block.vector(replay("vector"))
        block.scalar(replay("scalar"))

    return nc



def _tile_proj(wT, ng, gw):
    d_in, n_out = wT.shape
    nc_ = d_in // 128
    return np.ascontiguousarray(
        wT.reshape(nc_, 128, ng, gw).transpose(2, 1, 0, 3).reshape(ng * 128, nc_ * gw))


def _tile_down(wT):
    arr = wT.reshape(4, 16, 128, 8, 256)
    return np.ascontiguousarray(
        arr.transpose(3, 0, 2, 1, 4).reshape(4 * 8 * 128, 16 * 256))


def make_in_maps(inputs):
    x = np.ascontiguousarray(np.asarray(inputs["L0_attn_x"], np.float32))
    wqT = _tile_proj(np.asarray(inputs["L0_attn_wq"], np.float32).T, 8, 256)
    wkT = _tile_proj(np.asarray(inputs["L0_attn_wk"], np.float32).T, 8, 256)
    wvT = _tile_proj(np.asarray(inputs["L0_attn_wv"], np.float32).T, 8, 256)
    woT = _tile_proj(np.asarray(inputs["L0_attn_wo"], np.float32).T, 8, 256)
    wupT = _tile_proj(np.asarray(inputs["L0_ffn_W_up"], np.float32).T, 32, 256)
    wgT = _tile_proj(np.asarray(inputs["L0_ffn_W_gate"], np.float32).T, 32, 256)
    wdT = _tile_down(np.asarray(inputs["L0_ffn_W_down"], np.float32).T)
    loraA = _pack_loraA(inputs["L0_attn_qA"], inputs["L0_attn_kA"],
                        inputs["L0_attn_vA"], inputs["L0_attn_oA"])
    loraB = _pack_loraB(inputs["L0_attn_qB"], inputs["L0_attn_kB"],
                        inputs["L0_attn_vB"], inputs["L0_attn_oB"])
    wsc = np.asarray(inputs["L0_ffn_Wscale_ug"], np.float32)
    vecs = np.concatenate([
        _pack_cols(np.asarray(inputs["L0_attn_norm_w"], np.float32), 16),
        _pack_cols(np.asarray(inputs["L0_ffn_SU_ug"], np.float32), 16),
        _pack_cols(wsc[:8192], 64),
        _pack_cols(wsc[8192:], 64),
        _pack_cols(np.asarray(inputs["L0_ffn_SU_dp"], np.float32), 64),
        _pack_cols(np.asarray(inputs["L0_ffn_Wscale_dp"], np.float32), 16),
    ], axis=1).astype(np.float32)
    h128 = _h128_bf16()
    identb = np.eye(128, dtype=BF_NP)
    identf = np.eye(128, dtype=np.float32)

    in_maps = []
    for c in range(NCORES):
        beta, gamma = c // 4, c % 4
        s = gamma * 1024
        xs = np.zeros((1536, D), np.float32)
        lo = s - 512
        src_lo = max(lo, 0)
        xs[src_lo - lo:, :] = x[beta, src_lo: s + 1024, :]
        in_maps.append({
            "xs": xs, "wqT": wqT, "wkT": wkT, "wvT": wvT, "woT": woT,
            "wupT": wupT, "wgT": wgT, "wdT": wdT,
            "loraA": loraA, "loraB": loraB,
            "masks": _masks_bf16(gamma), "ropes": _ropes_f32(gamma),
            "vecs": vecs, "h128": h128, "identb": identb, "identf": identf,
        })
    return in_maps


def kernel(**inputs):
    in_maps = make_in_maps(inputs)
    nc = build_nc()
    res = run_bass_kernel_spmd(nc, in_maps, core_ids=list(range(NCORES)))
    outs = [np.asarray(res.results[c]["out"], np.float32) for c in range(NCORES)]
    return np.stack([np.concatenate(outs[:4], axis=0),
                     np.concatenate(outs[4:], axis=0)])


# version 06
# speedup vs baseline: 1.1965x; 1.1965x over previous
import numpy as np
import ml_dtypes
import concourse.bass as bass
import concourse.mybir as mybir
from concourse.bass_utils import run_bass_kernel_spmd

F32 = mybir.dt.float32
BF16 = mybir.dt.bfloat16
AF = mybir.ActivationFunctionType
ALU = mybir.AluOpType
AX = mybir.AxisListType
BF_NP = ml_dtypes.bfloat16

D = 2048
NT = 16
TP = 512
NCORES = 8

OWNER = {
    "cx": "sync", "cst": "sync", "x1l": "sync", "cc": "sync", "cr": "sync",
    "cm": "sync", "ctr": "sync", "cts": "sync",
    "cl": "gpsimd", "wa": "gpsimd", "wb": "gpsimd", "wc": "gpsimd",
    "wd": "gpsimd",
    "pe": "tensor", "dve": "vector", "act": "scalar", "gp": "gpsimd",
}
DMA_LBL = {"cx", "cst", "x1l", "cl", "wa", "wb", "wc", "wd", "cc", "cr", "cm", "ctr", "cts"}

DEBUG_SPEC = []
PHASES = 99
ATTN_SUB = 3
TR_LIMIT = 9999
TR_EVAC = True
ATTN_BARRIER = False
HEADS_ONLY = None
SCORES_RAW = False


class Sch:
    ENGS = ("sync", "gpsimd", "tensor", "vector", "scalar")

    def __init__(self):
        self.ops = {e: [] for e in self.ENGS}
        self.cnt = {k: 0 for k in OWNER}

    def tok(self, lbl):
        return (lbl, self.cnt[lbl])

    def emit(self, eng, fn, waits=(), inc=None, force=()):
        wmax = {}
        for t in waits:
            if t is None:
                continue
            lbl, val = t[0], t[1]
            if val <= 0:
                continue
            if lbl in DMA_LBL:
                val = self.cnt[lbl]
                if val <= 0:
                    continue
            elif OWNER[lbl] == eng and lbl not in force:
                continue
            if wmax.get(lbl, 0) < val:
                wmax[lbl] = val
        amt = 0
        if inc is not None:
            amt = 16 if inc in DMA_LBL else 1
            self.cnt[inc] += amt
        self.ops[eng].append((sorted(wmax.items()), fn, inc, amt))
        return (inc, self.cnt[inc]) if inc is not None else None


class PBuf:
    def __init__(self):
        self.w = None
        self.r = []
        self.ap = None

    def begin_write(self):
        w = list(self.r)
        self.r = []
        return w

    def set_w(self, tok):
        self.w = tok

    def add_r(self, tok):
        if tok is not None:
            self.r.append(tok)



def _popcount_mat():
    i = np.arange(128)
    a = i[:, None] & i[None, :]
    pc = np.zeros_like(a)
    v = a.copy()
    while v.any():
        pc += v & 1
        v >>= 1
    return pc


def _h128_bf16():
    return np.where(_popcount_mat() % 2 == 0, 1.0, -1.0).astype(BF_NP)


def _masks_bf16(gamma):
    a = np.arange(128)[:, None]
    j = np.arange(512)[None, :]
    std = (j >= a + 1)
    out = np.zeros((128, 2688), np.float32)
    for qt in range(4):
        m = std.copy()
        if gamma == 0:
            m &= (j >= 512 - qt * 128)
        out[:, qt * 512:(qt + 1) * 512] = m
    b = np.arange(128)[None, :]
    out[:, 2048:2176] = (b <= a)
    out[:, 2176:2688] = std
    return out.astype(BF_NP)


def _ropes_f32(gamma):
    half = 64
    theta = 1.0 / 10000.0 ** (np.arange(half, dtype=np.float64) / half)
    th = np.concatenate([theta, theta])[:, None]
    scale = 128.0 ** -0.25
    out = np.zeros((128, 3072), np.float32)
    g0 = gamma * 1024
    t0 = np.arange(g0 - 512, g0 + 512, dtype=np.float64)[None, :]
    c0 = np.cos(th * t0) * scale
    s0 = np.sin(th * t0) * scale
    neg = (t0[0] < 0)
    c0[:, neg] = 0.0
    s0[:, neg] = 0.0
    out[:, 0:1024] = c0
    out[:, 1024:2048] = s0
    t1 = np.arange(g0 + 512, g0 + 1024, dtype=np.float64)[None, :]
    out[:, 2048:2560] = np.cos(th * t1) * scale
    out[:, 2560:3072] = np.sin(th * t1) * scale
    return out


def _pack_cols(v, n):
    return np.ascontiguousarray(v.reshape(n, 128).T)


def _pack_loraA(qA, kA, vA, oA):
    outs = []
    for A in (qA, kA, vA, oA):
        AT = np.asarray(A, np.float32).T.reshape(16, 128, 16).transpose(1, 0, 2).reshape(128, 256)
        outs.append(AT)
    return np.ascontiguousarray(np.concatenate(outs, axis=1))


def _pack_loraB(qB, kB, vB, oB):
    return np.ascontiguousarray(np.concatenate(
        [np.asarray(m, np.float32).T for m in (qB, kB, vB, oB)], axis=1))



def build_nc():
    nc = bass.Bass("TRN2", target_bir_lowering=False)

    xs_d = nc.dram_tensor("xs", [1536, D], F32, kind="ExternalInput").ap()
    wqT_d = nc.dram_tensor("wqT", [1024, 4096], F32, kind="ExternalInput").ap()
    wkT_d = nc.dram_tensor("wkT", [1024, 4096], F32, kind="ExternalInput").ap()
    wvT_d = nc.dram_tensor("wvT", [1024, 4096], F32, kind="ExternalInput").ap()
    woT_d = nc.dram_tensor("woT", [1024, 4096], F32, kind="ExternalInput").ap()
    wupT_d = nc.dram_tensor("wupT", [4096, 4096], F32, kind="ExternalInput").ap()
    wgT_d = nc.dram_tensor("wgT", [4096, 4096], F32, kind="ExternalInput").ap()
    wdT_d = nc.dram_tensor("wdT", [4096, 4096], F32, kind="ExternalInput").ap()
    loraA_d = nc.dram_tensor("loraA", [128, 1024], F32, kind="ExternalInput").ap()
    loraB_d = nc.dram_tensor("loraB", [16, 8192], F32, kind="ExternalInput").ap()
    masks_d = nc.dram_tensor("masks", [128, 2688], BF16, kind="ExternalInput").ap()
    ropes_d = nc.dram_tensor("ropes", [128, 3072], F32, kind="ExternalInput").ap()
    vecs_d = nc.dram_tensor("vecs", [128, 240], F32, kind="ExternalInput").ap()
    h128_d = nc.dram_tensor("h128", [128, 128], BF16, kind="ExternalInput").ap()
    identb_d = nc.dram_tensor("identb", [128, 128], BF16, kind="ExternalInput").ap()
    identf_d = nc.dram_tensor("identf", [128, 128], F32, kind="ExternalInput").ap()
    out_d = nc.dram_tensor("out", [1024, D], F32, kind="ExternalOutput").ap()
    x1scr_d = nc.dram_tensor("x1scr", [128, 8192], BF16, kind="Internal").ap()
    dbg_d = {}
    for name, shape, dt in DEBUG_SPEC:
        dbg_d[name] = nc.dram_tensor("dbg_" + name, list(shape), dt,
                                     kind="ExternalOutput").ap()

    S = Sch()
    T = S.tok

    from contextlib import ExitStack
    es = ExitStack()
    with es:
        acc_t = es.enter_context(nc.sbuf_tensor("acc", [128, 8192], F32))
        kfm_t = es.enter_context(nc.sbuf_tensor("kfm", [128, 16384], BF16))
        vtm_t = es.enter_context(nc.sbuf_tensor("vtm", [128, 16384], BF16))
        bigA_t = es.enter_context(nc.sbuf_tensor("bigA", [128, 32768], BF16))
        par_t = es.enter_context(nc.sbuf_tensor("parena", [128, 12288], BF16))
        war_t = es.enter_context(nc.sbuf_tensor("warena", [128, 8192], BF16))
        loraA_s = es.enter_context(nc.sbuf_tensor("loraA_s", [128, 1024], BF16))
        h128_s = es.enter_context(nc.sbuf_tensor("h128_s", [128, 128], BF16))
        identb_s = es.enter_context(nc.sbuf_tensor("identb_s", [128, 128], BF16))
        identf_s = es.enter_context(nc.sbuf_tensor("identf_s", [128, 128], F32))
        vecs_s = es.enter_context(nc.sbuf_tensor("vecs_s", [128, 240], F32))
        dvecs_s = es.enter_context(nc.sbuf_tensor("dvecs", [128, 144], F32))
        stats_s = es.enter_context(nc.sbuf_tensor("stats", [128, 32], F32))
        smax_s = es.enter_context(nc.sbuf_tensor("smax", [128, 8], F32))
        pA_t = es.enter_context(nc.psum_tensor("pA", [128, 512], F32))
        pB_t = es.enter_context(nc.psum_tensor("pB", [128, 512], F32))
        pC_t = es.enter_context(nc.psum_tensor("pC", [128, 512], F32))
        pD_t = es.enter_context(nc.psum_tensor("pD", [128, 512], F32))
        pT_t = es.enter_context(nc.psum_tensor("pT", [128, 2048], BF16))
        pE_t = es.enter_context(nc.psum_tensor("pE", [128, 512], F32))
        pF_t = es.enter_context(nc.psum_tensor("pF", [128, 512], F32))
        s_cx = es.enter_context(nc.semaphore("s_cx"))
        s_cc = es.enter_context(nc.semaphore("s_cc"))
        s_cr = es.enter_context(nc.semaphore("s_cr"))
        s_cm = es.enter_context(nc.semaphore("s_cm"))
        s_ctr = es.enter_context(nc.semaphore("s_ctr"))
        s_cts = es.enter_context(nc.semaphore("s_cts"))
        s_cst = es.enter_context(nc.semaphore("s_cst"))
        s_x1l = es.enter_context(nc.semaphore("s_x1l"))
        s_cl = es.enter_context(nc.semaphore("s_cl"))
        s_wa = es.enter_context(nc.semaphore("s_wa"))
        s_wb = es.enter_context(nc.semaphore("s_wb"))
        s_wc = es.enter_context(nc.semaphore("s_wc"))
        s_wd = es.enter_context(nc.semaphore("s_wd"))
        s_pe = es.enter_context(nc.semaphore("s_pe"))
        s_dve = es.enter_context(nc.semaphore("s_dve"))
        s_act = es.enter_context(nc.semaphore("s_act"))
        s_gp = es.enter_context(nc.semaphore("s_gp"))
        block = es.enter_context(nc.Block())

        sems = {"cx": s_cx, "cst": s_cst, "x1l": s_x1l, "cl": s_cl, "wa": s_wa,
                "wb": s_wb, "wc": s_wc, "wd": s_wd, "pe": s_pe, "dve": s_dve, "act": s_act, "gp": s_gp,
                "cc": s_cc, "cr": s_cr, "cm": s_cm, "ctr": s_ctr, "cts": s_cts}

        acc = acc_t[:]
        kfm = kfm_t[:]
        vtm = vtm_t[:]
        bigA = bigA_t[:]
        par = par_t[:]
        war = war_t[:]

        def h_tile(f, c0, w):
            return bigA[:, f * 1024 + c0: f * 1024 + c0 + w]

        def q_tile(o, c0=0, w=TP):
            return bigA[:, 16384 + o * 512 + c0: 16384 + o * 512 + c0 + w]

        def ao_tile(o, c0=0, w=TP):
            return bigA[:, 24576 + o * 512 + c0: 24576 + o * 512 + c0 + w]

        def z_tile(m, c0=0, w=TP):
            return bigA[:, m * 512 + c0: m * 512 + c0 + w]

        ropes_f = bigA[:, 24576:32768].bitcast(F32)
        x0_tile = q_tile
        x1b_tile = ao_tile

        xstage = par[:, 0:4096].bitcast(F32)

        def xn_buf(p):
            return par[:, 4096 + p * 2048: 6144 + p * 2048]

        def hst_buf(p):
            return par[:, 8192 + p * 2048: 10240 + p * 2048]

        lbq = par[0:16, 0:2048]
        lbk = par[0:16, 2048:4096]
        lbv = par[0:16, 4096:6144]
        s1q = par[0:16, 6144:6656]
        s1k = par[0:16, 6656:7680]
        s1v = par[0:16, 7680:8704]
        rotb = par[:, 8704:9216]
        masks_s = par[:, 0:2688]

        def ebuf(p):
            return par[:, 2688 + p * 640: 3328 + p * 640]

        def atbuf(p):
            return par[:, 3968 + p * 640: 4608 + p * 640]

        s1o = par[0:16, 5248:5760]
        oBt = par[0:16, 5760:7808]
        def ostage_p(pp):
            return par[:, 10240 + pp * 1024: 11264 + pp * 1024].bitcast(F32)
        x1 = par[:, 0:8192]

        def x1_tile(f):
            return par[:, f * 512:(f + 1) * 512]

        def ubuf(p):
            return par[:, 8192 + p * 512: 8704 + p * 512]

        def gbuf(p):
            return par[:, 9216 + p * 512: 9728 + p * 512]

        spare_bf_ap = par[:, 10240:10752]
        spare_bf_ap2 = par[:, 10752:11264]

        normw_v = vecs_s[:, 0:16]
        suug_v = vecs_s[:, 16:32]
        wscup_v = vecs_s[:, 32:96]
        wscg_v = vecs_s[:, 96:160]
        sudp_v = vecs_s[:, 160:224]
        wscdp_v = vecs_s[:, 224:240]
        su_a = dvecs_s[:, 0:16]
        su_b = dvecs_s[:, 16:80]
        wscup_su = dvecs_s[:, 80:144]

        ST = {n: i for i, n in enumerate(
            ["ssum", "ssq", "m", "q2", "mm", "var", "varp", "rp", "qq", "qq2",
             "rp2", "rstd", "rstd2", "nmr", "onep"])}

        def st(n, p):
            c = ST[n] * 2 + p
            return stats_s[:, c:c + 1]

        pA, pB, pC, pD = PBuf(), PBuf(), PBuf(), PBuf()
        pE, pF = PBuf(), PBuf()
        pT0, pT1 = PBuf(), PBuf()
        pA.ap, pB.ap, pC.ap, pD.ap = pA_t[:], pB_t[:], pC_t[:], pD_t[:]
        pE.ap, pF.ap = pE_t[:, 0:128], pF_t[:, 0:128]
        pE.full, pF.full = pE_t[:], pF_t[:]
        pT0.ap, pT1.ap = pT_t[:, 0:1024], pT_t[:, 1024:2048]

        xstage_b = PBuf()
        xnb = [PBuf(), PBuf()]
        hstb = [PBuf(), PBuf()]
        wslot = [PBuf(), PBuf(), PBuf(), PBuf()]
        wslot_lbl = ["wa", "wb", "wc", "wd"]
        rot_b = PBuf()
        eb = [PBuf(), PBuf()]
        atb = [PBuf(), PBuf()]
        ub = [PBuf(), PBuf()]
        gb = [PBuf(), PBuf()]
        stat_par = [PBuf(), PBuf()]
        smax_par = [PBuf(), PBuf()]
        ost_b2 = [PBuf(), PBuf()]

        wload_i = [0]
        wring = [2]

        def wslot_ap(s, a, b2):
            if s < 2:
                return war[:, s * 4096 + a: s * 4096 + b2]
            return vtm[:, (s - 2) * 4096 + a: (s - 2) * 4096 + b2]

        def wload(in_ap, nelem=4096, extra=()):
            s = wload_i[0] % wring[0]
            wload_i[0] += 1
            b = wslot[s]
            out_ap = wslot_ap(s, 0, nelem)
            waits = b.begin_write() + list(extra)
            t = S.emit("gpsimd",
                       lambda g, o=out_ap, i=in_ap: g.dma_start(out=o, in_=i),
                       waits=waits, inc=wslot_lbl[s])
            b.set_w(t)
            return s, t

        def wchunk(s, d, w, off=0):
            return wslot_ap(s, d * w + off, d * w + w)

        taps = {}

        t_vecs = S.emit("sync", lambda sy: sy.dma_start(out=vecs_s[:], in_=vecs_d), inc="cc")
        t_h128 = S.emit("sync", lambda sy: sy.dma_start(out=h128_s[:], in_=h128_d), inc="cc")
        t_idb = S.emit("sync", lambda sy: sy.dma_start(out=identb_s[:], in_=identb_d), inc="cc")
        t_idf = S.emit("sync", lambda sy: sy.dma_start(out=identf_s[:], in_=identf_d), inc="cc")
        t_loraA = S.emit("gpsimd", lambda g: g.dma_start(out=loraA_s[:], in_=loraA_d), inc="cl")

        t1 = S.emit("vector", lambda v: v.tensor_scalar(
            su_a, suug_v, float(1.0 / np.sqrt(2048.0)), None, op0=ALU.mult),
            waits=[t_vecs], inc="dve")
        t2 = S.emit("vector", lambda v: v.tensor_scalar(
            su_b, sudp_v, float(1.0 / np.sqrt(8192.0)), None, op0=ALU.mult),
            inc="dve")
        t_dv = S.emit("vector", lambda v: v.tensor_mul(wscup_su, wscup_v, su_b),
                      waits=[t2], force=("dve",), inc="dve")


        acc_last = [[None] * 4, [None] * 4]
        store_toks = [[None] * 4, [None] * 4]
        phase_end = {}
        t_x1l_prev = [None]

        for p in range(2):
            n_tiles = 8 if p == 0 else 4
            tile_rows0 = 0 if p == 0 else 1024
            h_ready = []
            x0_ready = []
            gate_bigA = []
            gate_par = []
            gate_x = []
            if p == 1:
                gate_bigA = [phase_end["ffn_pe"], phase_end["ffn_dve"],
                             phase_end["ffn_act"]]
                gate_par = gate_bigA
                gate_x = [phase_end["ug_pe"]]

            ptq_rot = [pT0, pT1]
            ptq_i = [0]

            for i in range(n_tiles):
                par2 = i % 2
                row0 = tile_rows0 + i * 128
                own = (p == 1) or (i >= 4)
                hc = row0 - tile_rows0
                waits = xstage_b.begin_write() + (gate_x if i == 0 else [])
                tx = S.emit("sync", lambda sy, r=row0: sy.dma_start(
                    out=xstage, in_=xs_d[r:r + 128, :]), waits=waits, inc="cx")
                xstage_b.set_w(tx)

                sb = stat_par[par2]
                wstat = sb.begin_write()
                t_sum = S.emit("vector", lambda v, pp=par2: v.reduce_sum(
                    st("ssum", pp), xstage, axis=AX.X),
                    waits=[tx] + wstat, inc="dve", force=("dve",))
                t_sq = S.emit("scalar", lambda s, pp=par2: s.activation(
                    hst_buf(pp), xstage, AF.Square, accum_out=st("ssq", pp)),
                    waits=[tx] + wstat + hstb[par2].begin_write() +
                          (gate_par if i < 2 else []),
                    inc="act", force=("act",))

                def dvstep(fn, waits=()):
                    return S.emit("vector", fn, waits=list(waits), inc="dve",
                                  force=("dve",))

                td = dvstep(lambda v, pp=par2: v.tensor_scalar(
                    st("m", pp), st("ssum", pp), 1.0 / 2048.0, None, op0=ALU.mult),
                    [t_sum])
                td = dvstep(lambda v, pp=par2: v.tensor_scalar(
                    st("q2", pp), st("ssq", pp), 1.0 / 2048.0, None, op0=ALU.mult),
                    [td, t_sq])
                td = dvstep(lambda v, pp=par2: v.tensor_mul(
                    st("mm", pp), st("m", pp), st("m", pp)), [td])
                td = dvstep(lambda v, pp=par2: v.tensor_sub(
                    st("var", pp), st("q2", pp), st("mm", pp)), [td])
                td = dvstep(lambda v, pp=par2: v.tensor_scalar(
                    st("varp", pp), st("var", pp), 1e-5, None, op0=ALU.add), [td])
                td = dvstep(lambda v, pp=par2: v.reciprocal(
                    st("rp", pp), st("varp", pp)), [td])
                td = dvstep(lambda v, pp=par2: v.tensor_mul(
                    st("qq", pp), st("var", pp), st("rp", pp)), [td])
                td = dvstep(lambda v, pp=par2: v.tensor_scalar(
                    st("qq2", pp), st("qq", pp), 1e-6, None, op0=ALU.add), [td])
                td = dvstep(lambda v, pp=par2: v.reciprocal(
                    st("rp2", pp), st("qq2", pp)), [td])
                tsq1 = S.emit("scalar", lambda s, pp=par2: s.sqrt(
                    st("rstd", pp), st("rp", pp)), waits=[td], inc="act",
                    force=("act",))
                tsq2 = S.emit("scalar", lambda s, pp=par2: s.sqrt(
                    st("rstd2", pp), st("rp2", pp)), waits=[td], inc="act",
                    force=("act",))
                t_nmr = dvstep(lambda v, pp=par2: v.tensor_scalar(
                    st("nmr", pp), st("m", pp), st("rstd", pp), -1.0,
                    op0=ALU.mult, op1=ALU.mult), [tsq1])
                t_on = dvstep(lambda v, pp=par2: v.tensor_scalar(
                    st("onep", pp), st("rstd", pp), 1.0, None, op0=ALU.add), [tsq1])

                wxn = xnb[par2].begin_write()
                txn = S.emit("scalar", lambda s, pp=par2: s.activation(
                    xn_buf(pp), xstage, AF.Identity, bias=st("nmr", pp),
                    scale=st("rstd", pp)),
                    waits=[t_nmr, tsq1] + wxn + (gate_par if i < 2 else []),
                    inc="act", force=("act",))
                xnb[par2].set_w(txn)
                t_xlast = txn
                if own:
                    tt = (row0 - (512 + p * 512)) // 128
                    tacc = S.emit("scalar", lambda s, pp=par2, t_=tt: s.activation(
                        acc[:, t_ * 2048:(t_ + 1) * 2048], xstage, AF.Identity,
                        bias=st("nmr", pp), scale=st("onep", pp)),
                        waits=[t_on, t_nmr,
                               store_toks[0][tt] if p == 1 else None],
                        inc="act", force=("act",))
                    acc_last[p][tt] = tacc
                    t_xlast = tacc
                xstage_b.add_r(td)
                xstage_b.add_r(t_xlast)
                th = S.emit("scalar", lambda s, pp=par2: s.activation(
                    hst_buf(pp), xn_buf(pp), AF.Identity, scale=st("rstd2", pp)),
                    waits=[txn, tsq2], inc="act", force=("act",))
                hstb[par2].set_w(th)

                for half in range(2):
                    q = ptq_rot[ptq_i[0] % 2]
                    ptq_i[0] += 1
                    wq_ = q.begin_write()
                    tpe = None
                    for k in range(8):
                        f = half * 8 + k
                        tpe = S.emit("tensor", lambda t, f_=f, q_=q, k_=k, pp=par2:
                                     t.transpose(
                                         q_.ap[:, k_ * 128:(k_ + 1) * 128],
                                         hst_buf(pp)[:, f_ * 128:(f_ + 1) * 128],
                                         identb_s[:]),
                                     waits=([th, t_idb] + wq_ + gate_bigA)
                                     if k == 0 else [],
                                     inc="pe" if k == 7 else None)
                    q.set_w(tpe)
                    hstb[par2].add_r(tpe)
                    tev = None
                    for k in range(8):
                        f = half * 8 + k
                        tev = S.emit("scalar", lambda s, f_=f, q_=q, k_=k, c_=hc:
                                     s.mul(h_tile(f_, c_, 128),
                                           q_.ap[:, k_ * 128:(k_ + 1) * 128],
                                           normw_v[:, f_:f_ + 1]),
                                     waits=[tpe] if k == 0 else [],
                                     inc="act" if k == 7 else None)
                    q.add_r(tev)
                    h_ready.append(tev)
                if own:
                    oc = hc - (512 if p == 0 else 0)
                    for half in range(2):
                        q = ptq_rot[ptq_i[0] % 2]
                        ptq_i[0] += 1
                        wq_ = q.begin_write()
                        tpe = None
                        for k in range(8):
                            f = half * 8 + k
                            tpe = S.emit("tensor", lambda t, f_=f, q_=q, k_=k, pp=par2:
                                         t.transpose(
                                             q_.ap[:, k_ * 128:(k_ + 1) * 128],
                                             xn_buf(pp)[:, f_ * 128:(f_ + 1) * 128],
                                             identb_s[:]),
                                         waits=([txn, t_idb] + wq_ + gate_bigA)
                                         if k == 0 else [],
                                         inc="pe" if k == 7 else None)
                        q.set_w(tpe)
                        xnb[par2].add_r(tpe)
                        tev = None
                        for k in range(8):
                            f = half * 8 + k
                            tev = S.emit("vector", lambda v, f_=f, q_=q, k_=k, o_=oc:
                                         v.tensor_scalar(
                                             x0_tile(f_, o_, 128),
                                             q_.ap[:, k_ * 128:(k_ + 1) * 128],
                                             su_a[:, f_:f_ + 1], None, op0=ALU.mult),
                                         waits=[tpe, t1] if k == 0 else [],
                                         inc="dve" if k == 7 else None)
                        q.add_r(tev)
                        x0_ready.append(tev)

            x1_toks = []
            for f in range(NT):
                q = [pA, pB][f % 2]
                wq_ = q.begin_write()
                tpe = S.emit("tensor", lambda t, f_=f, q_=q: t.matmul(
                    q_.ap, h128_s[:], x0_tile(f_), start=True, stop=True),
                    waits=wq_ + x0_ready + [t_h128], inc="pe")
                q.set_w(tpe)
                tev = S.emit("vector", lambda v, f_=f, q_=q: v.tensor_copy(
                    x1b_tile(f_), q_.ap), waits=[tpe] + (gate_bigA if f < 2 else []),
                    inc="dve")
                q.add_r(tev)
                x1_toks.append(tev)
            cur_in, cur_out = x1b_tile, x0_tile
            stg_tok = x1_toks
            for s_i in range(4):
                newtoks = []
                for ho in range(2):
                    def bf_fn(v, s_=s_i, h_=ho, ci=cur_in, co=cur_out):
                        restv = (1 << s_) * 512
                        iv = ci(0, 0, 8192).rearrange("p (g t r) -> p g t r",
                                                      t=2, r=restv)
                        ov = co(0, 0, 8192).rearrange("p (g t r) -> p g t r",
                                                      t=2, r=restv)
                        if h_ == 0:
                            return v.tensor_add(ov[:, :, 0, :], iv[:, :, 0, :],
                                                iv[:, :, 1, :])
                        return v.tensor_sub(ov[:, :, 1, :], iv[:, :, 0, :],
                                            iv[:, :, 1, :])
                    tb = S.emit("vector", bf_fn,
                                waits=(stg_tok if ho == 0 else []),
                                inc="dve", force=("dve",))
                    newtoks.append(tb)
                stg_tok = newtoks
                cur_in, cur_out = cur_out, cur_in
            t_x1st = S.emit("sync", lambda sy: sy.dma_start(
                out=x1scr_d, in_=bigA[:, 24576:32768]),
                waits=stg_tok + [t_x1l_prev[0]], inc="cst")

            if p == 0:
                t_ropes = S.emit("sync", lambda sy: sy.dma_start(
                    out=ropes_f[:, 0:2048], in_=ropes_d[:, 0:2048]),
                    waits=[t_x1st], inc="cr")

                def cosv(c0, w):
                    return ropes_f[:, c0:c0 + w]

                def sinv(c0, w):
                    return ropes_f[:, 1024 + c0:1024 + c0 + w]
            else:
                t_ropes = S.emit("sync", lambda sy: sy.dma_start(
                    out=ropes_f[:, 0:1024], in_=ropes_d[:, 2048:3072]),
                    waits=[t_x1st], inc="cr")

                def cosv(c0, w):
                    return ropes_f[:, c0:c0 + w]

                def sinv(c0, w):
                    return ropes_f[:, 512 + c0:512 + c0 + w]

            if PHASES < 2:
                break
            own_c0 = 512 if p == 0 else 0
            ln_end = [T("act"), T("dve"), T("pe")]
            tlb = []
            for mi, (dst, c0) in enumerate([(lbq, 0), (lbk, 2048), (lbv, 4096)]):
                tt_ = S.emit("gpsimd", lambda g, d_=dst, c_=c0: g.dma_start(
                    out=d_, in_=loraB_d[:, c_:c_ + 2048]),
                    waits=ln_end if mi == 0 else [], inc="cl")
                tlb.append(tt_)
            t_lb = tlb[-1]

            s1_rot = [pA, pB, pC, pD]
            s1_jobs = [("q", s1q, 0, [(own_c0, 512, 0)]),
                       ("k", s1k, 256,
                        [(0, 512, 0), (512, 512, 512)] if p == 0 else [(0, 512, 0)]),
                       ("v", s1v, 512,
                        [(0, 512, 0), (512, 512, 512)] if p == 0 else [(0, 512, 0)])]
            s1_toks = {}
            ji = 0
            for name, dst, acol, groups in s1_jobs:
                for (hc0, w, dcol) in groups:
                    q = s1_rot[ji % 4]
                    ji += 1
                    wq_ = q.begin_write()
                    tpe = None
                    for d in range(NT):
                        tpe = S.emit("tensor", lambda t, d_=d, q_=q, a_=acol, h0=hc0,
                                     w_=w: t.matmul(
                                         q_.ap[0:16, 0:w_],
                                         loraA_s[:, a_ + d_ * 16: a_ + d_ * 16 + 16],
                                         h_tile(d_, h0, w_),
                                         start=(d_ == 0), stop=(d_ == 15)),
                                     waits=(wq_ + h_ready + [t_loraA])
                                     if d == 0 else [],
                                     inc="pe" if d == 15 else None)
                    q.set_w(tpe)
                    tev = S.emit("vector", lambda v, q_=q, d_=dst, c_=dcol, w_=w:
                                 v.tensor_scalar(d_[:, c_:c_ + w_],
                                                 q_.ap[0:16, 0:w_],
                                                 1.0 / 16.0, None, op0=ALU.mult),
                                 waits=[tpe] + (gate_par if ji <= 4 else []),
                                 inc="dve")
                    q.add_r(tev)
                    s1_toks.setdefault(name, []).append(tev)

            def rope_ops(buf_ap, cos_ap, sin_ap, t_ev):
                tr1 = S.emit("scalar", lambda s: s.mul(
                    rotb[0:64, :], buf_ap[64:128, :], -1.0),
                    waits=[t_ev, t_ropes] + rot_b.begin_write(),
                    inc="act", force=("act",))
                tr2 = S.emit("scalar", lambda s: s.copy(
                    rotb[64:128, :], buf_ap[0:64, :]), inc="act")
                rot_b.set_w(tr2)
                tm1 = S.emit("vector", lambda v: v.tensor_mul(buf_ap, buf_ap, cos_ap),
                             waits=[tr2], inc="dve")
                tm2 = S.emit("vector", lambda v: v.tensor_mul(rotb, rotb, sin_ap),
                             inc="dve", force=("dve",))
                tm3 = S.emit("vector", lambda v: v.tensor_add(buf_ap, buf_ap, rotb),
                             waits=[tm1, tm2], inc="dve", force=("dve",))
                rot_b.add_r(tm3)
                return tm3

            q_rope = [None] * 16
            k_rope = [[] for _ in range(16)]

            for mat_d, isQ in [(wqT_d, True), (wkT_d, False)]:
                for g in range(8):
                    sl, tw = wload(mat_d[g * 128:(g + 1) * 128, :])
                    for sub in range(2):
                        o = g * 2 + sub
                        if isQ:
                            segs = [(own_c0, 512, 0)]
                        else:
                            segs = ([(0, 512, 0), (512, 512, 512)]
                                    if p == 0 else [(0, 512, 0)])
                        for si, (hc0, w, kc0) in enumerate(segs):
                            q = [pA, pB][(o + si) % 2]
                            wq_ = q.begin_write()
                            tpe = None
                            for d in range(NT):
                                tpe = S.emit("tensor", lambda t, d_=d, q_=q, s_=sl,
                                             h0=hc0, w_=w, sub_=sub: t.matmul(
                                                 q_.ap[:, 0:w_],
                                                 wchunk(s_, d_, 256, sub_ * 128)[:, 0:128],
                                                 h_tile(d_, h0, w_),
                                                 start=(d_ == 0), stop=False),
                                             waits=(wq_ + [tw] + h_ready)
                                             if d == 0 else [], inc=None)
                            lb = lbq if isQ else lbk
                            s1buf = s1q if isQ else s1k
                            tpe = S.emit("tensor", lambda t, q_=q, o_=o, lb_=lb,
                                         s1_=s1buf, w_=w, kc=kc0: t.matmul(
                                             q_.ap[:, 0:w_],
                                             lb_[:, o_ * 128:(o_ + 1) * 128],
                                             s1_[:, kc:kc + w_],
                                             start=False, stop=True),
                                         waits=[t_lb] + s1_toks["q" if isQ else "k"],
                                         inc="pe")
                            q.set_w(tpe)
                            wslot[sl].add_r(tpe)
                            if isQ:
                                dst = q_tile(o)
                                tev = S.emit("vector", lambda v, d_=dst, q_=q:
                                             v.tensor_copy(d_, q_.ap),
                                             waits=[tpe, t_x1st], inc="dve")
                                q.add_r(tev)
                                q_rope[o] = rope_ops(dst, cosv(own_c0, 512),
                                                     sinv(own_c0, 512), tev)
                            else:
                                dst = kfm[:, o * 1024 + kc0: o * 1024 + kc0 + w]
                                war_gate = ([phase_end["attn_pe"]]
                                            if p == 1 else [])
                                tev = S.emit("vector", lambda v, d_=dst, q_=q:
                                             v.tensor_copy(d_, q_.ap),
                                             waits=[tpe] + war_gate, inc="dve")
                                q.add_r(tev)
                                k_rope[o].append(rope_ops(dst, cosv(kc0, w),
                                                          sinv(kc0, w), tev))

            nchunks = 8 if p == 0 else 4
            vp = [pA, pB, pC, pD]
            v_evac = {}
            vj = 0
            for g in range(8):
                sl, tw = wload(wvT_d[g * 128:(g + 1) * 128, :])
                for c in range(nchunks):
                    q = vp[vj % 4]
                    vj += 1
                    wq_ = q.begin_write()
                    col = 0
                    tpe = None
                    for d in range(NT):
                        tpe = S.emit("tensor", lambda t, d_=d, q_=q, s_=sl, c_=c,
                                     col_=col: t.matmul(
                                         q_.ap[:, col_:col_ + 256],
                                         h_tile(d_, c_ * 128, 128),
                                         wchunk(s_, d_, 256),
                                         start=(d_ == 0), stop=False),
                                     waits=(wq_ + [tw] + h_ready)
                                     if d == 0 else [], inc=None)
                    tpe = S.emit("tensor", lambda t, q_=q, g_=g, c_=c, col_=col:
                                 t.matmul(q_.ap[:, col_:col_ + 256],
                                          s1v[:, c_ * 128:(c_ + 1) * 128],
                                          lbv[:, g_ * 256:(g_ + 1) * 256],
                                          start=False, stop=True),
                                 waits=[t_lb] + s1_toks["v"], inc="pe")
                    q.set_w(tpe)
                    wslot[sl].add_r(tpe)
                    war_gate = [phase_end["attn_pe"]] if p == 1 else []
                    tev = S.emit("scalar", lambda s, q_=q, c_=c, g_=g, col_=col:
                                 s.copy(vtm[:, c_ * 2048 + g_ * 256:
                                            c_ * 2048 + (g_ + 1) * 256],
                                        q_.ap[:, col_:col_ + 256]),
                                 waits=[tpe] + war_gate, inc="act")
                    q.add_r(tev)
                    v_evac.setdefault(c, []).append(tev)

            qkv_pe_end = T("pe")
            if PHASES < 3:
                break

            t_masks = S.emit("sync", lambda sy: sy.dma_start(out=masks_s, in_=masks_d),
                             waits=[qkv_pe_end, T("dve"), T("act")], inc="cm")

            def kcols_for(qt):
                if p == 0:
                    sa = [(0, 512, qt * 128)]
                    sb = (512, 128, qt * 128 + 512)
                else:
                    jb = 512 - qt * 128
                    sa = [(0, jb, 512 + qt * 128)]
                    if qt > 0:
                        sa.append((jb, qt * 128, 0))
                    sb = (512, 128, qt * 128)
                return sa, sb

            def vchunk_for(qt, j5):
                if p == 0:
                    return qt + j5
                r = qt + j5
                return 4 + r if r < 4 else r - 4

            sa_p = [pA, pB]
            sb_p = [pE, pF]
            av_p = [pC, pD]
            attn_done = []
            heads = [(qt, hh) for qt in range(4) for hh in range(16)]
            NH = len(heads)
            htok = {}

            def em_scores(gi):
                qt, hh = heads[gi]
                pi = gi % 2
                sa, sb = kcols_for(qt)
                qa, qb = sa_p[pi], sb_p[pi]
                wq_ = qa.begin_write()
                tsc = None
                for ii, (j0, w, kc) in enumerate(sa):
                    tsc = S.emit("tensor", lambda t, q_=qa, j_=j0, w_=w, kc_=kc,
                                 h_=hh, qt_=qt: t.matmul(
                                     q_.ap[:, j_:j_ + w_],
                                     q_tile(h_, qt_ * 128, 128),
                                     kfm[:, h_ * 1024 + kc_: h_ * 1024 + kc_ + w_],
                                     start=True, stop=True),
                                 waits=(wq_ + [q_rope[hh]] + k_rope[hh])
                                 if ii == 0 else [],
                                 inc="pe" if ii == len(sa) - 1 else None)
                qa.set_w(tsc)
                wq_ = qb.begin_write()
                (j0, w, kc) = sb
                tscb = S.emit("tensor", lambda t, q_=qb, kc_=kc, h_=hh, qt_=qt:
                              t.matmul(q_.ap[:, 0:128],
                                       q_tile(h_, qt_ * 128, 128),
                                       kfm[:, h_ * 1024 + kc_: h_ * 1024 + kc_ + 128],
                                       start=True, stop=True),
                              waits=wq_ + [q_rope[hh]] + k_rope[hh], inc="pe")
                qb.set_w(tscb)
                htok[(gi, "sa")] = tsc
                htok[(gi, "sb")] = tscb

            def em_softmax(gi):
                qt, hh = heads[gi]
                pi = gi % 2
                qa, qb = sa_p[pi], sb_p[pi]
                sm_b = smax_par[pi]
                e_b = eb[pi]
                if SCORES_RAW:
                    t1_ = S.emit("vector", lambda v, q_=qa, pp=pi: v.tensor_copy(
                        ebuf(pp)[:, 0:512], q_.ap[:, 0:512]),
                        waits=[htok[(gi, "sa")]] + e_b.begin_write(), inc="dve")
                    qa.add_r(t1_)
                    t2_ = S.emit("vector", lambda v, q_=qb, pp=pi: v.tensor_copy(
                        ebuf(pp)[:, 512:640], q_.ap[:, 0:128]),
                        waits=[htok[(gi, "sb")]], inc="dve")
                    qb.add_r(t2_)
                    e_b.set_w(t2_)
                    htok[(gi, "sm")] = t2_
                    return
                wsm = sm_b.begin_write()
                tm1 = S.emit("vector", lambda v, q_=qa, pp=pi: v.reduce_max(
                    smax_s[:, pp * 4: pp * 4 + 1], q_.ap[:, 0:512], axis=AX.X,
                    negate=True), waits=[htok[(gi, "sa")]] + wsm, inc="dve",
                    force=("dve",))
                tm2 = S.emit("vector", lambda v, q_=qb, pp=pi: v.reduce_max(
                    smax_s[:, pp * 4 + 1: pp * 4 + 2], q_.ap[:, 0:128], axis=AX.X,
                    negate=True), waits=[htok[(gi, "sb")]], inc="dve",
                    force=("dve",))
                tm3 = S.emit("vector", lambda v, pp=pi: v.tensor_tensor(
                    smax_s[:, pp * 4: pp * 4 + 1],
                    smax_s[:, pp * 4: pp * 4 + 1],
                    smax_s[:, pp * 4 + 1: pp * 4 + 2], op=ALU.min),
                    waits=[tm1, tm2], inc="dve", force=("dve",))
                we_ = e_b.begin_write()
                te1 = S.emit("scalar", lambda s, q_=qa, pp=pi: s.activation(
                    ebuf(pp)[:, 0:512], q_.ap[:, 0:512], AF.Exp,
                    bias=smax_s[:, pp * 4: pp * 4 + 1]),
                    waits=[tm3] + we_, inc="act")
                qa.add_r(te1)
                te2 = S.emit("scalar", lambda s, q_=qb, pp=pi: s.activation(
                    ebuf(pp)[:, 512:640], q_.ap[:, 0:128], AF.Exp,
                    bias=smax_s[:, pp * 4: pp * 4 + 1]), inc="act")
                qb.add_r(te2)
                sm_b.add_r(te2)
                if SCORES_RAW == 2:
                    e_b.set_w(te2)
                    htok[(gi, "sm")] = te2
                    return
                mask_a_c0 = qt * 512 if p == 0 else 2176
                tma = S.emit("vector", lambda v, pp=pi, mc=mask_a_c0: v.tensor_mul(
                    ebuf(pp)[:, 0:512], ebuf(pp)[:, 0:512],
                    masks_s[:, mc:mc + 512]),
                    waits=[te1, t_masks], inc="dve")
                tmb = S.emit("vector", lambda v, pp=pi: v.tensor_mul(
                    ebuf(pp)[:, 512:640], ebuf(pp)[:, 512:640],
                    masks_s[:, 2048:2176]),
                    waits=[te2], inc="dve", force=("dve",))
                trs = S.emit("vector", lambda v, pp=pi: v.reduce_sum(
                    smax_s[:, pp * 4 + 2: pp * 4 + 3], ebuf(pp)[:, 0:640],
                    axis=AX.X),
                    waits=[tma, tmb], inc="dve", force=("dve",))
                tri = S.emit("vector", lambda v, pp=pi: v.reciprocal(
                    smax_s[:, pp * 4 + 3: pp * 4 + 4],
                    smax_s[:, pp * 4 + 2: pp * 4 + 3]),
                    waits=[trs], inc="dve", force=("dve",))
                tnm = S.emit("vector", lambda v, pp=pi: v.tensor_scalar(
                    ebuf(pp)[:, 0:640], ebuf(pp)[:, 0:640],
                    smax_s[:, pp * 4 + 3: pp * 4 + 4], None, op0=ALU.mult),
                    waits=[tri], inc="dve", force=("dve",))
                e_b.set_w(tnm)
                sm_b.add_r(tnm)
                htok[(gi, "sm")] = tnm

            def em_transpose(gi):
                pi = gi % 2
                e_b, at_b = eb[pi], atb[pi]
                qx = [pT0, pT1][pi]
                wq_ = qx.begin_write()
                tpe = None
                for k in range(5):
                    tpe = S.emit("tensor", lambda t, q_=qx, k_=k, pp=pi: t.transpose(
                        q_.ap[:, k_ * 128:(k_ + 1) * 128],
                        ebuf(pp)[:, k_ * 128:(k_ + 1) * 128], identb_s[:]),
                        waits=([htok[(gi, "sm")]] + wq_) if k == 0 else [],
                        inc="pe" if k == 4 else None)
                qx.set_w(tpe)
                e_b.add_r(tpe)
                wat = at_b.begin_write()
                tat = S.emit("scalar", lambda s, q_=qx, pp=pi: s.copy(
                    atbuf(pp)[:, 0:640], q_.ap[:, 0:640]),
                    waits=[tpe] + wat, inc="act")
                qx.add_r(tat)
                at_b.set_w(tat)
                htok[(gi, "at")] = tat

            def em_av(gi):
                qt, hh = heads[gi]
                pi = gi % 2
                qv = av_p[pi]
                at_b = atb[pi]
                wq_ = qv.begin_write()
                tpe = None
                for j5 in range(5):
                    c = vchunk_for(qt, j5)
                    tpe = S.emit("tensor", lambda t, q_=qv, j_=j5, c_=c, h_=hh,
                                 pp=pi: t.matmul(
                                     q_.ap[:, 0:128],
                                     vtm[:, c_ * 2048 + h_ * 128:
                                         c_ * 2048 + (h_ + 1) * 128],
                                     atbuf(pp)[:, j_ * 128:(j_ + 1) * 128],
                                     start=(j_ == 0), stop=(j_ == 4)),
                                 waits=(wq_ + [htok[(gi, "at")]] +
                                        v_evac.get(c, [])) if j5 == 0 else [],
                                 inc="pe" if j5 == 4 else None)
                qv.set_w(tpe)
                at_b.add_r(tpe)
                tev = S.emit("vector", lambda v, q_=qv, h_=hh, qt_=qt: v.tensor_copy(
                    ao_tile(h_, qt_ * 128, 128), q_.ap[:, 0:128]),
                    waits=[tpe, t_x1st, t_ropes], inc="dve")
                qv.add_r(tev)
                attn_done.append(tev)

            for gi in range(NH + 2):
                if HEADS_ONLY is not None and gi < NH and gi not in HEADS_ONLY:
                    continue
                if gi < NH:
                    em_scores(gi)
                    em_softmax(gi)
                if ATTN_SUB >= 2 and 1 <= gi <= NH:
                    em_transpose(gi - 1)
                if ATTN_SUB >= 3 and gi >= 2:
                    em_av(gi - 2)
                if ATTN_BARRIER:
                    toks = [T("pe"), T("dve"), T("act")]
                    for e_ in ("tensor", "vector", "scalar"):
                        S.emit(e_, None, waits=toks, force=("pe", "dve", "act"))

            phase_end["attn_pe"] = T("pe")
            if PHASES < 4:
                break

            t_oB = S.emit("gpsimd", lambda g: g.dma_start(
                out=oBt, in_=loraB_d[:, 6144:8192]), waits=[qkv_pe_end], inc="cl")
            q = pC
            wq_ = q.begin_write()
            tpe = None
            for d in range(NT):
                tpe = S.emit("tensor", lambda t, d_=d, q_=q: t.matmul(
                    q_.ap[0:16, 0:512], loraA_s[:, 768 + d_ * 16: 768 + d_ * 16 + 16],
                    ao_tile(d_), start=(d_ == 0), stop=(d_ == 15)),
                    waits=(wq_ + attn_done) if d == 0 else [],
                    inc="pe" if d == 15 else None)
            q.set_w(tpe)
            t_s1o = S.emit("vector", lambda v, q_=q: v.tensor_scalar(
                s1o[:, 0:512], q_.ap[0:16, 0:512], 1.0 / 16.0, None, op0=ALU.mult),
                waits=[tpe], inc="dve")
            q.add_r(t_s1o)

            for g in range(8):
                sl, tw = wload(woT_d[g * 128:(g + 1) * 128, :])
                for sub in range(2):
                    c = g * 2 + sub
                    q = [pA, pB][c % 2]
                    wq_ = q.begin_write()
                    tpe = None
                    for d in range(NT):
                        tpe = S.emit("tensor", lambda t, d_=d, q_=q, s_=sl,
                                     sub_=sub: t.matmul(
                                         q_.ap,
                                         wchunk(s_, d_, 256, sub_ * 128)[:, 0:128],
                                         ao_tile(d_), start=(d_ == 0), stop=False),
                                     waits=(wq_ + [tw] + attn_done)
                                     if d == 0 else [], inc=None)
                    tpe = S.emit("tensor", lambda t, q_=q, c_=c: t.matmul(
                        q_.ap, oBt[:, c_ * 128:(c_ + 1) * 128], s1o[:, 0:512],
                        start=False, stop=True), waits=[t_oB, t_s1o], inc="pe")
                    q.set_w(tpe)
                    wslot[sl].add_r(tpe)
                    osb = ost_b2[c % 2]
                    wo_ = osb.begin_write()
                    tev = S.emit("scalar", lambda s, q_=q, pp=c % 2: s.copy(
                        ostage_p(pp), q_.ap), waits=[tpe] + wo_, inc="act")
                    q.add_r(tev)
                    osb.set_w(tev)
                    last_add = None
                    for tt in range(4):
                        qtr = [pE, pF][tt % 2]
                        wq2 = qtr.begin_write()
                        tpt = S.emit("tensor", lambda t, q_=qtr, t_=tt, pp=c % 2:
                                     t.transpose(
                                         q_.ap, ostage_p(pp)[:, t_ * 128:(t_ + 1) * 128],
                                         identf_s[:]),
                                     waits=[tev, t_idf] + wq2, inc="pe")
                        qtr.set_w(tpt)
                        tadd = S.emit("vector", lambda v, q_=qtr, t_=tt, c_=c:
                                      v.tensor_add(
                                          acc[:, t_ * 2048 + c_ * 128:
                                              t_ * 2048 + (c_ + 1) * 128],
                                          acc[:, t_ * 2048 + c_ * 128:
                                              t_ * 2048 + (c_ + 1) * 128],
                                          q_.ap),
                                      waits=[tpt, acc_last[p][tt]], inc="dve")
                        qtr.add_r(tadd)
                        acc_last[p][tt] = tadd
                        last_add = tadd
                    osb.add_r(last_add)

            phase_end["oproj_pe"] = T("pe")
            phase_end["oproj_dve"] = T("dve")
            phase_end["oproj_act"] = T("act")
            if PHASES < 5:
                break

            t_x1l = S.emit("sync", lambda sy: sy.dma_start(out=x1, in_=x1scr_d),
                           waits=[t_x1st, phase_end["oproj_pe"],
                                  phase_end["oproj_dve"], phase_end["oproj_act"]],
                           inc="x1l")
            wring[0] = 4
            wload_i[0] = wload_i[0] - (wload_i[0] % 2)
            vtm_gate = [phase_end["attn_pe"]]
            t_x1l_prev[0] = t_x1l
            z_toks = [None] * 64
            zh_tok = [None] * 64
            slot_ap = [z_tile(i) for i in range(64)]
            bspare = {"vector": spare_bf_ap, "gpsimd": spare_bf_ap2}
            bchain = {"vector": None, "gpsimd": None}
            bcnt = [0]
            for mg in range(32):
                sl_u, tw_u = wload(wupT_d[mg * 128:(mg + 1) * 128, :],
                                   extra=(vtm_gate if mg < 2 else ()))
                sl_g, tw_g = wload(wgT_d[mg * 128:(mg + 1) * 128, :],
                                   extra=(vtm_gate if mg < 2 else ()))
                for sub in range(2):
                    mt = mg * 2 + sub
                    pu = [pA, pC][mt % 2]
                    pg = [pB, pD][mt % 2]
                    wq_ = pu.begin_write()
                    tpe = None
                    for d in range(NT):
                        tpe = S.emit("tensor", lambda t, d_=d, q_=pu, s_=sl_u,
                                     sub_=sub: t.matmul(
                                         q_.ap,
                                         wchunk(s_, d_, 256, sub_ * 128)[:, 0:128],
                                         x1_tile(d_), start=(d_ == 0),
                                         stop=(d_ == 15)),
                                     waits=(wq_ + [tw_u, t_x1l])
                                     if d == 0 else [],
                                     inc="pe" if d == 15 else None)
                    pu.set_w(tpe)
                    wslot[sl_u].add_r(tpe)
                    wu_ = ub[mt % 2].begin_write()
                    tu = S.emit("vector", lambda v, q_=pu, m_=mt, pp=mt % 2:
                                v.tensor_scalar(ubuf(pp), q_.ap,
                                                wscup_su[:, m_:m_ + 1], None,
                                                op0=ALU.mult),
                                waits=[tpe, t_dv] + wu_, inc="dve")
                    pu.add_r(tu)
                    ub[mt % 2].set_w(tu)
                    wq_ = pg.begin_write()
                    tpe = None
                    for d in range(NT):
                        tpe = S.emit("tensor", lambda t, d_=d, q_=pg, s_=sl_g,
                                     sub_=sub: t.matmul(
                                         q_.ap,
                                         wchunk(s_, d_, 256, sub_ * 128)[:, 0:128],
                                         x1_tile(d_), start=(d_ == 0),
                                         stop=(d_ == 15)),
                                     waits=(wq_ + [tw_g, t_x1l])
                                     if d == 0 else [],
                                     inc="pe" if d == 15 else None)
                    pg.set_w(tpe)
                    wslot[sl_g].add_r(tpe)
                    wg_ = gb[mt % 2].begin_write()
                    tg = S.emit("scalar", lambda s, q_=pg, m_=mt, pp=mt % 2:
                                s.activation(gbuf(pp), q_.ap, AF.Silu,
                                             scale=wscg_v[:, m_:m_ + 1]),
                                waits=[tpe] + wg_, inc="act")
                    pg.add_r(tg)
                    gb[mt % 2].set_w(tg)
                    zgate = ([phase_end["oproj_pe"]] if mt >= 32
                             else [qkv_pe_end])
                    tz = S.emit("vector", lambda v, m_=mt, pp=mt % 2: v.tensor_mul(
                        z_tile(m_), gbuf(pp), ubuf(pp)),
                        waits=[tu, tg] + zgate, inc="dve", force=("dve",))
                    ub[mt % 2].add_r(tz)
                    gb[mt % 2].add_r(tz)
                    z_toks[mt] = tz
                    qh = [pE, pF][mt % 2]
                    wqh = qh.begin_write()
                    tph = S.emit("tensor", lambda t, m_=mt, q_=qh: t.matmul(
                        q_.full, h128_s[:], z_tile(m_), start=True, stop=True),
                        waits=wqh + [tz], inc="pe")
                    qh.set_w(tph)
                    tze = S.emit("scalar", lambda s, m_=mt, q_=qh: s.copy(
                        z_tile(m_), q_.full), waits=[tph], inc="act")
                    qh.add_r(tze)
                    zh_tok[mt] = tze
                    for s_i in range(6):
                        blk = 1 << (s_i + 1)
                        if (mt + 1) % blk != 0:
                            break
                        base = mt + 1 - blk
                        half = 1 << s_i
                        for k in range(half):
                            ia, ib = base + k, base + k + half
                            eng = "vector" if bcnt[0] % 2 == 0 else "gpsimd"
                            lbl = "dve" if eng == "vector" else "gp"
                            bcnt[0] += 1
                            a_ap = slot_ap[ia]
                            b_ap = slot_ap[ib]
                            sp_ap = bspare[eng]
                            tsub = S.emit(eng,
                                          lambda v, a=a_ap, b=b_ap, sp=sp_ap:
                                          v.tensor_sub(sp, a, b),
                                          waits=[zh_tok[ia], zh_tok[ib], bchain[eng]],
                                          inc=lbl, force=("dve", "gp"))
                            tadd = S.emit(eng,
                                          lambda v, a=a_ap, b=b_ap:
                                          v.tensor_add(a, a, b),
                                          waits=[tsub], inc=lbl, force=("dve", "gp"))
                            slot_ap[ib] = sp_ap
                            bspare[eng] = b_ap
                            zh_tok[ia] = tadd
                            zh_tok[ib] = tsub
                            bchain[eng] = tadd
            phase_end["ug_pe"] = T("pe")
            for sp in (spare_bf_ap, spare_bf_ap2):
                for jx in range(64):
                    if slot_ap[jx] is sp:
                        free_slot = (bspare["vector"]
                                     if bspare["vector"] is not sp and
                                     bspare["vector"] is not spare_bf_ap and
                                     bspare["vector"] is not spare_bf_ap2
                                     else bspare["gpsimd"])
                        assert free_slot is not spare_bf_ap and free_slot is not spare_bf_ap2
                        tmv = S.emit("vector", lambda v, d_=free_slot, s_=sp:
                                     v.tensor_copy(d_, s_),
                                     waits=[zh_tok[jx], bchain["vector"],
                                            bchain["gpsimd"]],
                                     inc="dve", force=("dve",))
                        if bspare["vector"] is free_slot:
                            bspare["vector"] = sp
                        else:
                            bspare["gpsimd"] = sp
                        slot_ap[jx] = free_slot
                        zh_tok[jx] = tmv
                        bchain["vector"] = tmv
                        break
            zh_fin = zh_tok

            for cs in range(8):
                pair = [pA, pB] if cs % 2 == 0 else [pC, pD]

                def dload(li, cs_=cs):
                    return wload(wdT_d[(cs_ * 4 + li) * 128:
                                       (cs_ * 4 + li + 1) * 128, :], 4096)
                loads = [dload(0), dload(1), dload(2)]
                wq_ = pair[0].begin_write() + pair[1].begin_write()
                tpe = None
                for j in range(64):
                    sl, tw = loads[j // 16]
                    jj = j % 16
                    for ct in range(2):
                        q = pair[ct]
                        rhs_ap = slot_ap[j]
                        tpe = S.emit("tensor", lambda t, q_=q, s_=sl, jj_=jj,
                                     ct_=ct, j_=j, r_=rhs_ap: t.matmul(
                                         q_.ap,
                                         wslot_ap(s_, jj_ * 256 + ct_ * 128,
                                                  jj_ * 256 + ct_ * 128 + 128),
                                         r_,
                                         start=(j_ == 0), stop=(j_ == 63)),
                                     waits=((wq_ + [tw, zh_fin[j]])
                                            if (j == 0 and ct == 0) else
                                            ([tw, zh_fin[j]] if ct == 0 else [])),
                                     inc="pe" if (jj == 15 and ct == 1) else None)
                        if ct == 1 and jj == 15:
                            wslot[sl].add_r(tpe)
                    if jj == 15 and j // 16 + 3 < 4:
                        loads.append(dload(j // 16 + 3))
                for ct in range(2):
                    pair[ct].set_w(tpe)
                for ct in range(2):
                    c = cs * 2 + ct
                    q = pair[ct]
                    osb = ost_b2[c % 2]
                    wd_ = osb.begin_write()
                    tev = S.emit("scalar", lambda s, q_=q, c_=c, pp=c % 2: s.mul(
                        ostage_p(pp), q_.ap, wscdp_v[:, c_:c_ + 1]),
                        waits=[tpe] + wd_, inc="act")
                    q.add_r(tev)
                    osb.set_w(tev)
                    last_add = None
                    for tt in range(4):
                        qtr = [pE, pF][tt % 2]
                        wq2 = qtr.begin_write()
                        tpt = S.emit("tensor", lambda t, q_=qtr, t_=tt, pp=c % 2:
                                     t.transpose(
                                         q_.ap, ostage_p(pp)[:, t_ * 128:(t_ + 1) * 128],
                                         identf_s[:]),
                                     waits=[tev, t_idf] + wq2, inc="pe")
                        qtr.set_w(tpt)
                        tadd = S.emit("vector", lambda v, q_=qtr, t_=tt, c_=c:
                                      v.tensor_add(
                                          acc[:, t_ * 2048 + c_ * 128:
                                              t_ * 2048 + (c_ + 1) * 128],
                                          acc[:, t_ * 2048 + c_ * 128:
                                              t_ * 2048 + (c_ + 1) * 128],
                                          q_.ap),
                                      waits=[tpt, acc_last[p][tt]], inc="dve")
                        qtr.add_r(tadd)
                        acc_last[p][tt] = tadd
                        last_add = tadd
                    osb.add_r(last_add)

            phase_end["ffn_pe"] = T("pe")
            phase_end["ffn_dve"] = T("dve")
            phase_end["ffn_act"] = T("act")
            wring[0] = 2
            wload_i[0] = wload_i[0] - (wload_i[0] % 2)

            if PHASES < 6 and p == 0:
                for tt in range(4):
                    store_toks[p][tt] = S.emit("sync", lambda sy, t_=tt, p_=p:
                                               sy.dma_start(
                                                   out=out_d[p_ * 512 + t_ * 128:
                                                             p_ * 512 + (t_ + 1) * 128, :],
                                                   in_=acc[:, t_ * 2048:(t_ + 1) * 2048]),
                                               waits=[acc_last[p][tt]], inc="cst")
                break
            for tt in range(4):
                store_toks[p][tt] = S.emit("sync", lambda sy, t_=tt, p_=p:
                                           sy.dma_start(
                                               out=out_d[p_ * 512 + t_ * 128:
                                                         p_ * 512 + (t_ + 1) * 128, :],
                                               in_=acc[:, t_ * 2048:(t_ + 1) * 2048]),
                                           waits=[acc_last[p][tt]], inc="cst")

        tap_map = {
            "h": bigA[:, 0:16384], "qreg": bigA[:, 16384:24576],
            "ao": bigA[:, 24576:32768], "kfm": kfm, "vtm": vtm,
            "x1": x1, "z": bigA, "acc": acc, "x1scr": x1scr_d, "par": par,
        }
        for name, shape, dt in DEBUG_SPEC:
            src = tap_map[name]
            S.emit("sync", lambda sy, s_=src, d_=dbg_d[name]: sy.dma_start(
                out=d_, in_=s_),
                waits=[T("pe"), T("dve"), T("act"), T("gp")], inc="cst")

        S.emit("sync", None, waits=[T("cst")])

        def replay(eng_name):
            def f(eng):
                for (waits, fn, inc, amt) in S.ops[eng_name]:
                    for (lbl, val) in waits:
                        eng.wait_ge(sems[lbl], val)
                    if fn is None:
                        continue
                    inst = fn(eng)
                    if inc is not None:
                        inst.then_inc(sems[inc], amt)
            return f

        block.sync(replay("sync"))
        block.gpsimd(replay("gpsimd"))
        block.tensor(replay("tensor"))
        block.vector(replay("vector"))
        block.scalar(replay("scalar"))

    return nc



def _tile_proj(wT, ng, gw):
    d_in, n_out = wT.shape
    nc_ = d_in // 128
    return np.ascontiguousarray(
        wT.reshape(nc_, 128, ng, gw).transpose(2, 1, 0, 3).reshape(ng * 128, nc_ * gw))


def _tile_down(wT):
    arr = wT.reshape(4, 16, 128, 8, 256)
    return np.ascontiguousarray(
        arr.transpose(3, 0, 2, 1, 4).reshape(4 * 8 * 128, 16 * 256))


def make_in_maps(inputs):
    x = np.ascontiguousarray(np.asarray(inputs["L0_attn_x"], np.float32))
    wqT = _tile_proj(np.asarray(inputs["L0_attn_wq"], np.float32).T, 8, 256)
    wkT = _tile_proj(np.asarray(inputs["L0_attn_wk"], np.float32).T, 8, 256)
    wvT = _tile_proj(np.asarray(inputs["L0_attn_wv"], np.float32).T, 8, 256)
    woT = _tile_proj(np.asarray(inputs["L0_attn_wo"], np.float32).T, 8, 256)
    wupT = _tile_proj(np.asarray(inputs["L0_ffn_W_up"], np.float32).T, 32, 256)
    wgT = _tile_proj(np.asarray(inputs["L0_ffn_W_gate"], np.float32).T, 32, 256)
    wdT = _tile_down(np.asarray(inputs["L0_ffn_W_down"], np.float32).T)
    loraA = _pack_loraA(inputs["L0_attn_qA"], inputs["L0_attn_kA"],
                        inputs["L0_attn_vA"], inputs["L0_attn_oA"])
    loraB = _pack_loraB(inputs["L0_attn_qB"], inputs["L0_attn_kB"],
                        inputs["L0_attn_vB"], inputs["L0_attn_oB"])
    wsc = np.asarray(inputs["L0_ffn_Wscale_ug"], np.float32)
    vecs = np.concatenate([
        _pack_cols(np.asarray(inputs["L0_attn_norm_w"], np.float32), 16),
        _pack_cols(np.asarray(inputs["L0_ffn_SU_ug"], np.float32), 16),
        _pack_cols(wsc[:8192], 64),
        _pack_cols(wsc[8192:], 64),
        _pack_cols(np.asarray(inputs["L0_ffn_SU_dp"], np.float32), 64),
        _pack_cols(np.asarray(inputs["L0_ffn_Wscale_dp"], np.float32), 16),
    ], axis=1).astype(np.float32)
    h128 = _h128_bf16()
    identb = np.eye(128, dtype=BF_NP)
    identf = np.eye(128, dtype=np.float32)

    in_maps = []
    for c in range(NCORES):
        beta, gamma = c // 4, c % 4
        s = gamma * 1024
        xs = np.zeros((1536, D), np.float32)
        lo = s - 512
        src_lo = max(lo, 0)
        xs[src_lo - lo:, :] = x[beta, src_lo: s + 1024, :]
        in_maps.append({
            "xs": xs, "wqT": wqT, "wkT": wkT, "wvT": wvT, "woT": woT,
            "wupT": wupT, "wgT": wgT, "wdT": wdT,
            "loraA": loraA, "loraB": loraB,
            "masks": _masks_bf16(gamma), "ropes": _ropes_f32(gamma),
            "vecs": vecs, "h128": h128, "identb": identb, "identf": identf,
        })
    return in_maps


def kernel(**inputs):
    in_maps = make_in_maps(inputs)
    nc = build_nc()
    res = run_bass_kernel_spmd(nc, in_maps, core_ids=list(range(NCORES)))
    outs = [np.asarray(res.results[c]["out"], np.float32) for c in range(NCORES)]
    return np.stack([np.concatenate(outs[:4], axis=0),
                     np.concatenate(outs[4:], axis=0)])


# version 07
# speedup vs baseline: 1.3868x; 1.1590x over previous
import numpy as np
import ml_dtypes
import concourse.bass as bass
import concourse.mybir as mybir
from concourse.bass_utils import run_bass_kernel_spmd

F32 = mybir.dt.float32
BF16 = mybir.dt.bfloat16
AF = mybir.ActivationFunctionType
ALU = mybir.AluOpType
AX = mybir.AxisListType
BF_NP = ml_dtypes.bfloat16

D = 2048
NT = 16
TP = 512
NCORES = 8

OWNER = {
    "cx": "sync", "cst": "sync", "x1l": "sync", "cc": "sync", "cr": "sync",
    "cm": "sync", "ctr": "sync", "cts": "sync", "cx2": "sync",
    "cl": "gpsimd", "wa": "gpsimd", "wb": "gpsimd", "wc": "gpsimd",
    "wd": "gpsimd",
    "pe": "tensor", "dve": "vector", "act": "scalar", "gp": "gpsimd",
}
DMA_LBL = {"cx", "cx2", "cst", "x1l", "cl", "wa", "wb", "wc", "wd", "cc", "cr", "cm", "ctr", "cts"}

DEBUG_SPEC = []
PHASES = 99
ATTN_SUB = 3
TR_LIMIT = 9999
TR_EVAC = True
ATTN_BARRIER = False
HEADS_ONLY = None
SCORES_RAW = False


class Sch:
    ENGS = ("sync", "gpsimd", "tensor", "vector", "scalar")

    def __init__(self):
        self.ops = {e: [] for e in self.ENGS}
        self.cnt = {k: 0 for k in OWNER}

    def tok(self, lbl):
        return (lbl, self.cnt[lbl])

    def emit(self, eng, fn, waits=(), inc=None, force=()):
        wmax = {}
        for t in waits:
            if t is None:
                continue
            lbl, val = t[0], t[1]
            if val <= 0:
                continue
            if lbl in DMA_LBL:
                val = self.cnt[lbl]
                if val <= 0:
                    continue
            elif OWNER[lbl] == eng and lbl not in force:
                continue
            if wmax.get(lbl, 0) < val:
                wmax[lbl] = val
        amt = 0
        if inc is not None:
            amt = 16 if inc in DMA_LBL else 1
            self.cnt[inc] += amt
        self.ops[eng].append((sorted(wmax.items()), fn, inc, amt))
        return (inc, self.cnt[inc]) if inc is not None else None


class PBuf:
    def __init__(self):
        self.w = None
        self.r = []
        self.ap = None

    def begin_write(self):
        w = list(self.r)
        self.r = []
        return w

    def set_w(self, tok):
        self.w = tok

    def add_r(self, tok):
        if tok is not None:
            self.r.append(tok)



def _popcount_mat():
    i = np.arange(128)
    a = i[:, None] & i[None, :]
    pc = np.zeros_like(a)
    v = a.copy()
    while v.any():
        pc += v & 1
        v >>= 1
    return pc


def _h128_bf16():
    return np.where(_popcount_mat() % 2 == 0, 1.0, -1.0).astype(BF_NP)


def _masks_bf16(gamma):
    NEG = np.float32(-1e30)
    a = np.arange(128)[:, None]
    j = np.arange(512)[None, :]
    std = (j >= a + 1)
    b = np.arange(128)[None, :]
    caus = (b <= a)
    out = np.full((128, 3200), NEG, np.float32)
    for qt in range(4):
        m = std.copy()
        if gamma == 0:
            m &= (j >= 512 - qt * 128)
        out[:, qt * 640: qt * 640 + 512] = np.where(m, 0.0, NEG)
        out[:, qt * 640 + 512: qt * 640 + 640] = np.where(caus, 0.0, NEG)
    out[:, 2560:3072] = np.where(std, 0.0, NEG)
    out[:, 3072:3200] = np.where(caus, 0.0, NEG)
    return out.astype(BF_NP)


def _ropes_f32(gamma):
    half = 64
    theta = 1.0 / 10000.0 ** (np.arange(half, dtype=np.float64) / half)
    th = np.concatenate([theta, theta])[:, None]
    scale = 128.0 ** -0.25
    out = np.zeros((128, 3072), np.float32)
    g0 = gamma * 1024
    t0 = np.arange(g0 - 512, g0 + 512, dtype=np.float64)[None, :]
    c0 = np.cos(th * t0) * scale
    s0 = np.sin(th * t0) * scale
    neg = (t0[0] < 0)
    c0[:, neg] = 0.0
    s0[:, neg] = 0.0
    out[:, 0:1024] = c0
    out[:, 1024:2048] = s0
    t1 = np.arange(g0 + 512, g0 + 1024, dtype=np.float64)[None, :]
    out[:, 2048:2560] = np.cos(th * t1) * scale
    out[:, 2560:3072] = np.sin(th * t1) * scale
    return out


def _pack_cols(v, n):
    return np.ascontiguousarray(v.reshape(n, 128).T)


def _pack_loraA(qA, kA, vA, oA):
    outs = []
    for A in (qA, kA, vA, oA):
        AT = np.asarray(A, np.float32).T.reshape(16, 128, 16).transpose(1, 0, 2).reshape(128, 256)
        outs.append(AT)
    return np.ascontiguousarray(np.concatenate(outs, axis=1))


def _pack_loraB(qB, kB, vB, oB):
    return np.ascontiguousarray(np.concatenate(
        [np.asarray(m, np.float32).T for m in (qB, kB, vB, oB)], axis=1))



def build_nc():
    nc = bass.Bass("TRN2", target_bir_lowering=False)

    xs_d = nc.dram_tensor("xs", [1536, D], F32, kind="ExternalInput").ap()
    wqT_d = nc.dram_tensor("wqT", [1024, 4096], F32, kind="ExternalInput").ap()
    wkT_d = nc.dram_tensor("wkT", [1024, 4096], F32, kind="ExternalInput").ap()
    wvT_d = nc.dram_tensor("wvT", [1024, 4096], F32, kind="ExternalInput").ap()
    woT_d = nc.dram_tensor("woT", [1024, 4096], F32, kind="ExternalInput").ap()
    wupT_d = nc.dram_tensor("wupT", [4096, 4096], F32, kind="ExternalInput").ap()
    wgT_d = nc.dram_tensor("wgT", [4096, 4096], F32, kind="ExternalInput").ap()
    wdT_d = nc.dram_tensor("wdT", [4096, 4096], F32, kind="ExternalInput").ap()
    loraA_d = nc.dram_tensor("loraA", [128, 1024], F32, kind="ExternalInput").ap()
    loraB_d = nc.dram_tensor("loraB", [16, 8192], F32, kind="ExternalInput").ap()
    masks_d = nc.dram_tensor("masks", [128, 3200], BF16, kind="ExternalInput").ap()
    ropes_d = nc.dram_tensor("ropes", [128, 3072], F32, kind="ExternalInput").ap()
    vecs_d = nc.dram_tensor("vecs", [128, 240], F32, kind="ExternalInput").ap()
    h128_d = nc.dram_tensor("h128", [128, 128], BF16, kind="ExternalInput").ap()
    identb_d = nc.dram_tensor("identb", [128, 128], BF16, kind="ExternalInput").ap()
    identf_d = nc.dram_tensor("identf", [128, 128], F32, kind="ExternalInput").ap()
    out_d = nc.dram_tensor("out", [1024, D], F32, kind="ExternalOutput").ap()
    x1scr_d = nc.dram_tensor("x1scr", [128, 8192], BF16, kind="Internal").ap()
    dbg_d = {}
    for name, shape, dt in DEBUG_SPEC:
        dbg_d[name] = nc.dram_tensor("dbg_" + name, list(shape), dt,
                                     kind="ExternalOutput").ap()

    S = Sch()
    T = S.tok

    from contextlib import ExitStack
    es = ExitStack()
    with es:
        acc_t = es.enter_context(nc.sbuf_tensor("acc", [128, 8192], F32))
        kfm_t = es.enter_context(nc.sbuf_tensor("kfm", [128, 16384], BF16))
        vtm_t = es.enter_context(nc.sbuf_tensor("vtm", [128, 16384], BF16))
        bigA_t = es.enter_context(nc.sbuf_tensor("bigA", [128, 32768], BF16))
        par_t = es.enter_context(nc.sbuf_tensor("parena", [128, 12288], BF16))
        war_t = es.enter_context(nc.sbuf_tensor("warena", [128, 8192], BF16))
        loraA_s = es.enter_context(nc.sbuf_tensor("loraA_s", [128, 1024], BF16))
        h128_s = es.enter_context(nc.sbuf_tensor("h128_s", [128, 128], BF16))
        identb_s = es.enter_context(nc.sbuf_tensor("identb_s", [128, 128], BF16))
        identf_s = es.enter_context(nc.sbuf_tensor("identf_s", [128, 128], F32))
        vecs_s = es.enter_context(nc.sbuf_tensor("vecs_s", [128, 240], F32))
        dvecs_s = es.enter_context(nc.sbuf_tensor("dvecs", [128, 144], F32))
        stats_s = es.enter_context(nc.sbuf_tensor("stats", [128, 32], F32))
        smax_s = es.enter_context(nc.sbuf_tensor("smax", [128, 8], F32))
        pAB_t = es.enter_context(nc.psum_tensor("pAB", [128, 1024], F32))
        pCD_t = es.enter_context(nc.psum_tensor("pCD", [128, 1024], F32))
        pT_t = es.enter_context(nc.psum_tensor("pT", [128, 2048], BF16))
        pE_t = es.enter_context(nc.psum_tensor("pE", [128, 512], F32))
        pF_t = es.enter_context(nc.psum_tensor("pF", [128, 512], F32))
        s_cx = es.enter_context(nc.semaphore("s_cx"))
        s_cc = es.enter_context(nc.semaphore("s_cc"))
        s_cr = es.enter_context(nc.semaphore("s_cr"))
        s_cm = es.enter_context(nc.semaphore("s_cm"))
        s_ctr = es.enter_context(nc.semaphore("s_ctr"))
        s_cts = es.enter_context(nc.semaphore("s_cts"))
        s_cx2 = es.enter_context(nc.semaphore("s_cx2"))
        s_cst = es.enter_context(nc.semaphore("s_cst"))
        s_x1l = es.enter_context(nc.semaphore("s_x1l"))
        s_cl = es.enter_context(nc.semaphore("s_cl"))
        s_wa = es.enter_context(nc.semaphore("s_wa"))
        s_wb = es.enter_context(nc.semaphore("s_wb"))
        s_wc = es.enter_context(nc.semaphore("s_wc"))
        s_wd = es.enter_context(nc.semaphore("s_wd"))
        s_pe = es.enter_context(nc.semaphore("s_pe"))
        s_dve = es.enter_context(nc.semaphore("s_dve"))
        s_act = es.enter_context(nc.semaphore("s_act"))
        s_gp = es.enter_context(nc.semaphore("s_gp"))
        block = es.enter_context(nc.Block())

        sems = {"cx": s_cx, "cst": s_cst, "x1l": s_x1l, "cl": s_cl, "wa": s_wa,
                "wb": s_wb, "wc": s_wc, "wd": s_wd, "pe": s_pe, "dve": s_dve, "act": s_act, "gp": s_gp,
                "cc": s_cc, "cr": s_cr, "cm": s_cm, "ctr": s_ctr, "cts": s_cts, "cx2": s_cx2}

        acc = acc_t[:]
        kfm = kfm_t[:]
        vtm = vtm_t[:]
        bigA = bigA_t[:]
        par = par_t[:]
        war = war_t[:]

        def h_tile(f, c0, w):
            return bigA[:, f * 1024 + c0: f * 1024 + c0 + w]

        def q_tile(o, c0=0, w=TP):
            return bigA[:, 16384 + o * 512 + c0: 16384 + o * 512 + c0 + w]

        def ao_tile(o, c0=0, w=TP):
            return bigA[:, 24576 + o * 512 + c0: 24576 + o * 512 + c0 + w]

        def z_tile(m, c0=0, w=TP):
            return bigA[:, m * 512 + c0: m * 512 + c0 + w]

        ropes_f = bigA[:, 24576:32768].bitcast(F32)
        x0_tile = q_tile
        x1b_tile = ao_tile

        xstage0 = par[:, 0:4096].bitcast(F32)
        xstage1 = vtm[:, 0:4096].bitcast(F32)

        def xstage_p(pp):
            return xstage0 if pp == 0 else xstage1

        def xn_buf(p):
            return par[:, 4096 + p * 2048: 6144 + p * 2048]

        def hst_buf(p):
            return par[:, 8192 + p * 2048: 10240 + p * 2048]

        lbq = par[0:16, 0:2048]
        lbk = par[0:16, 2048:4096]
        lbv = par[0:16, 4096:6144]
        s1q = par[0:16, 6144:6656]
        s1k = par[0:16, 6656:7680]
        s1v = par[0:16, 7680:8704]
        rotb = par[:, 8704:9216]
        masks_s = par[:, 0:3200]

        def ebuf(p):
            return par[:, 3200 + p * 640: 3840 + p * 640]

        def atbuf(p):
            return par[:, 4480 + p * 640: 5120 + p * 640]

        s1o = par[0:16, 5760:6272]
        oBt = par[0:16, 6272:8320]
        def ostage_p(pp):
            return par[:, 10240 + pp * 1024: 11264 + pp * 1024].bitcast(F32)
        x1 = par[:, 0:8192]

        def x1_tile(f):
            return par[:, f * 512:(f + 1) * 512]

        def ubuf(p):
            return par[:, 8192 + p * 512: 8704 + p * 512]

        def gbuf(p):
            return par[:, 9216 + p * 512: 9728 + p * 512]

        spare_bf_ap = par[:, 10240:10752]
        spare_bf_ap2 = par[:, 10752:11264]

        normw_v = vecs_s[:, 0:16]
        suug_v = vecs_s[:, 16:32]
        wscup_v = vecs_s[:, 32:96]
        wscg_v = vecs_s[:, 96:160]
        sudp_v = vecs_s[:, 160:224]
        wscdp_v = vecs_s[:, 224:240]
        su_a = dvecs_s[:, 0:16]
        su_b = dvecs_s[:, 16:80]
        wscup_su = dvecs_s[:, 80:144]

        ST = {n: i for i, n in enumerate(
            ["ssum", "ssq", "m", "q2", "mm", "var", "varp", "rp", "qq", "qq2",
             "rp2", "rstd", "rstd2", "nmr", "onep"])}

        def st(n, p):
            c = ST[n] * 2 + p
            return stats_s[:, c:c + 1]

        pA, pB, pC, pD = PBuf(), PBuf(), PBuf(), PBuf()
        pE, pF = PBuf(), PBuf()
        pT0, pT1 = PBuf(), PBuf()
        pA.ap, pB.ap = pAB_t[:, 0:512], pAB_t[:, 512:1024]
        pC.ap, pD.ap = pCD_t[:, 0:512], pCD_t[:, 512:1024]
        pA.full, pC.full = pAB_t[:], pCD_t[:]
        pE.ap, pF.ap = pE_t[:, 0:128], pF_t[:, 0:128]
        pE.full, pF.full = pE_t[:], pF_t[:]
        pT0.ap, pT1.ap = pT_t[:, 0:1024], pT_t[:, 1024:2048]

        xstage_b2 = [PBuf(), PBuf()]
        xnb = [PBuf(), PBuf()]
        hstb = [PBuf(), PBuf()]
        wslot = [PBuf(), PBuf(), PBuf(), PBuf()]
        wslot_lbl = ["wa", "wb", "wc", "wd"]
        rot_b = PBuf()
        eb = [PBuf(), PBuf()]
        atb = [PBuf(), PBuf()]
        ub = [PBuf(), PBuf()]
        gb = [PBuf(), PBuf()]
        stat_par = [PBuf(), PBuf()]
        smax_par = [PBuf(), PBuf()]
        ost_b2 = [PBuf(), PBuf()]

        wload_i = [0]
        wring = [2]

        def wslot_ap(s, a, b2):
            if s < 2:
                return war[:, s * 4096 + a: s * 4096 + b2]
            return vtm[:, (s - 2) * 4096 + a: (s - 2) * 4096 + b2]

        def wload(in_ap, nelem=4096, extra=()):
            s = wload_i[0] % wring[0]
            wload_i[0] += 1
            b = wslot[s]
            out_ap = wslot_ap(s, 0, nelem)
            waits = b.begin_write() + list(extra)
            t = S.emit("gpsimd",
                       lambda g, o=out_ap, i=in_ap: g.dma_start(out=o, in_=i),
                       waits=waits, inc=wslot_lbl[s])
            b.set_w(t)
            return s, t

        def wchunk(s, d, w, off=0):
            return wslot_ap(s, d * w + off, d * w + w)

        taps = {}

        t_vecs = S.emit("sync", lambda sy: sy.dma_start(out=vecs_s[:], in_=vecs_d), inc="cc")
        t_h128 = S.emit("sync", lambda sy: sy.dma_start(out=h128_s[:], in_=h128_d), inc="cc")
        t_idb = S.emit("sync", lambda sy: sy.dma_start(out=identb_s[:], in_=identb_d), inc="cc")
        t_idf = S.emit("sync", lambda sy: sy.dma_start(out=identf_s[:], in_=identf_d), inc="cc")
        t_loraA = S.emit("gpsimd", lambda g: g.dma_start(out=loraA_s[:], in_=loraA_d), inc="cl")

        t1 = S.emit("vector", lambda v: v.tensor_scalar(
            su_a, suug_v, float(1.0 / np.sqrt(2048.0)), None, op0=ALU.mult),
            waits=[t_vecs], inc="dve")
        t2 = S.emit("vector", lambda v: v.tensor_scalar(
            su_b, sudp_v, float(1.0 / np.sqrt(8192.0)), None, op0=ALU.mult),
            inc="dve")
        t_dv = S.emit("vector", lambda v: v.tensor_mul(wscup_su, wscup_v, su_b),
                      waits=[t2], force=("dve",), inc="dve")


        acc_last = [[None] * 4, [None] * 4]
        store_toks = [[None] * 4, [None] * 4]
        phase_end = {}
        t_x1l_prev = [None]

        for p in range(2):
            n_tiles = 8 if p == 0 else 4
            tile_rows0 = 0 if p == 0 else 1024
            h_ready = []
            x0_ready = []
            gate_bigA = []
            gate_par = []
            if p == 1:
                gate_bigA = [phase_end["ffn_pe"], phase_end["ffn_dve"],
                             phase_end["ffn_act"]]
                gate_par = gate_bigA

            ptq_rot = [pT0, pT1]
            ptq_i = [0]

            for i in range(n_tiles):
                par2 = i % 2
                row0 = tile_rows0 + i * 128
                own = (p == 1) or (i >= 4)
                hc = row0 - tile_rows0
                xsb = xstage_b2[par2]
                gx = []
                if p == 1 and i < 2:
                    gx = ([phase_end["ug_pe"]] if par2 == 0
                          else [phase_end["ffn_pe"]])
                waits = xsb.begin_write() + gx
                tx = S.emit("sync", lambda sy, r=row0, pp=par2: sy.dma_start(
                    out=xstage_p(pp), in_=xs_d[r:r + 128, :]), waits=waits,
                    inc="cx" if par2 == 0 else "cx2")
                xsb.set_w(tx)

                sb = stat_par[par2]
                wstat = sb.begin_write()
                t_sum = S.emit("vector", lambda v, pp=par2: v.reduce_sum(
                    st("ssum", pp), xstage_p(pp), axis=AX.X),
                    waits=[tx] + wstat, inc="dve", force=("dve",))
                t_sq = S.emit("scalar", lambda s, pp=par2: s.activation(
                    hst_buf(pp), xstage_p(pp), AF.Square, accum_out=st("ssq", pp)),
                    waits=[tx] + wstat + hstb[par2].begin_write() +
                          (gate_par if i < 2 else []),
                    inc="act", force=("act",))

                def dvstep(fn, waits=()):
                    return S.emit("vector", fn, waits=list(waits), inc="dve",
                                  force=("dve",))

                td = dvstep(lambda v, pp=par2: v.tensor_scalar(
                    st("m", pp), st("ssum", pp), 1.0 / 2048.0, None, op0=ALU.mult),
                    [t_sum])
                td = dvstep(lambda v, pp=par2: v.tensor_scalar(
                    st("q2", pp), st("ssq", pp), 1.0 / 2048.0, None, op0=ALU.mult),
                    [td, t_sq])
                td = dvstep(lambda v, pp=par2: v.tensor_mul(
                    st("mm", pp), st("m", pp), st("m", pp)), [td])
                td = dvstep(lambda v, pp=par2: v.tensor_sub(
                    st("var", pp), st("q2", pp), st("mm", pp)), [td])
                td = dvstep(lambda v, pp=par2: v.tensor_scalar(
                    st("varp", pp), st("var", pp), 1e-5, None, op0=ALU.add), [td])
                td = dvstep(lambda v, pp=par2: v.reciprocal(
                    st("rp", pp), st("varp", pp)), [td])
                td = dvstep(lambda v, pp=par2: v.tensor_mul(
                    st("qq", pp), st("var", pp), st("rp", pp)), [td])
                td = dvstep(lambda v, pp=par2: v.tensor_scalar(
                    st("qq2", pp), st("qq", pp), 1e-6, None, op0=ALU.add), [td])
                td = dvstep(lambda v, pp=par2: v.reciprocal(
                    st("rp2", pp), st("qq2", pp)), [td])
                tsq1 = S.emit("scalar", lambda s, pp=par2: s.sqrt(
                    st("rstd", pp), st("rp", pp)), waits=[td], inc="act",
                    force=("act",))
                tsq2 = S.emit("scalar", lambda s, pp=par2: s.sqrt(
                    st("rstd2", pp), st("rp2", pp)), waits=[td], inc="act",
                    force=("act",))
                t_nmr = dvstep(lambda v, pp=par2: v.tensor_scalar(
                    st("nmr", pp), st("m", pp), st("rstd", pp), -1.0,
                    op0=ALU.mult, op1=ALU.mult), [tsq1])
                t_on = dvstep(lambda v, pp=par2: v.tensor_scalar(
                    st("onep", pp), st("rstd", pp), 1.0, None, op0=ALU.add), [tsq1])

                wxn = xnb[par2].begin_write()
                txn = S.emit("scalar", lambda s, pp=par2: s.activation(
                    xn_buf(pp), xstage_p(pp), AF.Identity, bias=st("nmr", pp),
                    scale=st("rstd", pp)),
                    waits=[t_nmr, tsq1] + wxn + (gate_par if i < 2 else []),
                    inc="act", force=("act",))
                xnb[par2].set_w(txn)
                t_xlast = txn
                if own:
                    tt = (row0 - (512 + p * 512)) // 128
                    tacc = S.emit("scalar", lambda s, pp=par2, t_=tt: s.activation(
                        acc[:, t_ * 2048:(t_ + 1) * 2048], xstage_p(pp), AF.Identity,
                        bias=st("nmr", pp), scale=st("onep", pp)),
                        waits=[t_on, t_nmr,
                               store_toks[0][tt] if p == 1 else None],
                        inc="act", force=("act",))
                    acc_last[p][tt] = tacc
                    t_xlast = tacc
                xsb.add_r(td)
                xsb.add_r(t_xlast)
                th = S.emit("scalar", lambda s, pp=par2: s.activation(
                    hst_buf(pp), xn_buf(pp), AF.Identity, scale=st("rstd2", pp)),
                    waits=[txn, tsq2], inc="act", force=("act",))
                hstb[par2].set_w(th)

                for half in range(2):
                    q = ptq_rot[ptq_i[0] % 2]
                    ptq_i[0] += 1
                    wq_ = q.begin_write()
                    tpe = None
                    for k in range(8):
                        f = half * 8 + k
                        tpe = S.emit("tensor", lambda t, f_=f, q_=q, k_=k, pp=par2:
                                     t.transpose(
                                         q_.ap[:, k_ * 128:(k_ + 1) * 128],
                                         hst_buf(pp)[:, f_ * 128:(f_ + 1) * 128],
                                         identb_s[:]),
                                     waits=([th, t_idb] + wq_ + gate_bigA)
                                     if k == 0 else [],
                                     inc="pe" if k == 7 else None)
                    q.set_w(tpe)
                    hstb[par2].add_r(tpe)
                    tev = None
                    for k in range(8):
                        f = half * 8 + k
                        tev = S.emit("scalar", lambda s, f_=f, q_=q, k_=k, c_=hc:
                                     s.mul(h_tile(f_, c_, 128),
                                           q_.ap[:, k_ * 128:(k_ + 1) * 128],
                                           normw_v[:, f_:f_ + 1]),
                                     waits=[tpe] if k == 0 else [],
                                     inc="act" if k == 7 else None)
                    q.add_r(tev)
                    h_ready.append(tev)
                if own:
                    oc = hc - (512 if p == 0 else 0)
                    for half in range(2):
                        q = ptq_rot[ptq_i[0] % 2]
                        ptq_i[0] += 1
                        wq_ = q.begin_write()
                        tpe = None
                        for k in range(8):
                            f = half * 8 + k
                            tpe = S.emit("tensor", lambda t, f_=f, q_=q, k_=k, pp=par2:
                                         t.transpose(
                                             q_.ap[:, k_ * 128:(k_ + 1) * 128],
                                             xn_buf(pp)[:, f_ * 128:(f_ + 1) * 128],
                                             identb_s[:]),
                                         waits=([txn, t_idb] + wq_ + gate_bigA)
                                         if k == 0 else [],
                                         inc="pe" if k == 7 else None)
                        q.set_w(tpe)
                        xnb[par2].add_r(tpe)
                        tev = None
                        for k in range(8):
                            f = half * 8 + k
                            tev = S.emit("vector", lambda v, f_=f, q_=q, k_=k, o_=oc:
                                         v.tensor_scalar(
                                             x0_tile(f_, o_, 128),
                                             q_.ap[:, k_ * 128:(k_ + 1) * 128],
                                             su_a[:, f_:f_ + 1], None, op0=ALU.mult),
                                         waits=[tpe, t1] if k == 0 else [],
                                         inc="dve" if k == 7 else None)
                        q.add_r(tev)
                        x0_ready.append(tev)

            x1_toks = []
            for f in range(NT):
                q = [pA, pB][f % 2]
                wq_ = q.begin_write()
                tpe = S.emit("tensor", lambda t, f_=f, q_=q: t.matmul(
                    q_.ap, h128_s[:], x0_tile(f_), start=True, stop=True),
                    waits=wq_ + x0_ready + [t_h128], inc="pe")
                q.set_w(tpe)
                tev = S.emit("vector", lambda v, f_=f, q_=q: v.tensor_copy(
                    x1b_tile(f_), q_.ap), waits=[tpe] + (gate_bigA if f < 2 else []),
                    inc="dve")
                q.add_r(tev)
                x1_toks.append(tev)
            cur_in, cur_out = x1b_tile, x0_tile
            stg_tok = x1_toks
            for s_i in range(4):
                newtoks = []
                for ho in range(2):
                    def bf_fn(v, s_=s_i, h_=ho, ci=cur_in, co=cur_out):
                        restv = (1 << s_) * 512
                        iv = ci(0, 0, 8192).rearrange("p (g t r) -> p g t r",
                                                      t=2, r=restv)
                        ov = co(0, 0, 8192).rearrange("p (g t r) -> p g t r",
                                                      t=2, r=restv)
                        if h_ == 0:
                            return v.tensor_add(ov[:, :, 0, :], iv[:, :, 0, :],
                                                iv[:, :, 1, :])
                        return v.tensor_sub(ov[:, :, 1, :], iv[:, :, 0, :],
                                            iv[:, :, 1, :])
                    tb = S.emit("vector", bf_fn,
                                waits=(stg_tok if ho == 0 else []),
                                inc="dve", force=("dve",))
                    newtoks.append(tb)
                stg_tok = newtoks
                cur_in, cur_out = cur_out, cur_in
            t_x1st = S.emit("sync", lambda sy: sy.dma_start(
                out=x1scr_d, in_=bigA[:, 24576:32768]),
                waits=stg_tok + [t_x1l_prev[0]], inc="cst")

            if p == 0:
                t_ropes = S.emit("sync", lambda sy: sy.dma_start(
                    out=ropes_f[:, 0:2048], in_=ropes_d[:, 0:2048]),
                    waits=[t_x1st], inc="cr")

                def cosv(c0, w):
                    return ropes_f[:, c0:c0 + w]

                def sinv(c0, w):
                    return ropes_f[:, 1024 + c0:1024 + c0 + w]
            else:
                t_ropes = S.emit("sync", lambda sy: sy.dma_start(
                    out=ropes_f[:, 0:1024], in_=ropes_d[:, 2048:3072]),
                    waits=[t_x1st], inc="cr")

                def cosv(c0, w):
                    return ropes_f[:, c0:c0 + w]

                def sinv(c0, w):
                    return ropes_f[:, 512 + c0:512 + c0 + w]

            if PHASES < 2:
                break
            own_c0 = 512 if p == 0 else 0
            ln_end = [T("act"), T("dve"), T("pe")]
            tlb = []
            for mi, (dst, c0) in enumerate([(lbq, 0), (lbk, 2048), (lbv, 4096)]):
                tt_ = S.emit("gpsimd", lambda g, d_=dst, c_=c0: g.dma_start(
                    out=d_, in_=loraB_d[:, c_:c_ + 2048]),
                    waits=ln_end if mi == 0 else [], inc="cl")
                tlb.append(tt_)
            t_lb = tlb[-1]

            s1_rot = [pA, pB, pC, pD]
            s1_jobs = [("q", s1q, 0, [(own_c0, 512, 0)]),
                       ("k", s1k, 256,
                        [(0, 512, 0), (512, 512, 512)] if p == 0 else [(0, 512, 0)]),
                       ("v", s1v, 512,
                        [(0, 512, 0), (512, 512, 512)] if p == 0 else [(0, 512, 0)])]
            s1_toks = {}
            ji = 0
            for name, dst, acol, groups in s1_jobs:
                for (hc0, w, dcol) in groups:
                    q = s1_rot[ji % 4]
                    ji += 1
                    wq_ = q.begin_write()
                    tpe = None
                    for d in range(NT):
                        tpe = S.emit("tensor", lambda t, d_=d, q_=q, a_=acol, h0=hc0,
                                     w_=w: t.matmul(
                                         q_.ap[0:16, 0:w_],
                                         loraA_s[:, a_ + d_ * 16: a_ + d_ * 16 + 16],
                                         h_tile(d_, h0, w_),
                                         start=(d_ == 0), stop=(d_ == 15)),
                                     waits=(wq_ + h_ready + [t_loraA])
                                     if d == 0 else [],
                                     inc="pe" if d == 15 else None)
                    q.set_w(tpe)
                    tev = S.emit("vector", lambda v, q_=q, d_=dst, c_=dcol, w_=w:
                                 v.tensor_scalar(d_[:, c_:c_ + w_],
                                                 q_.ap[0:16, 0:w_],
                                                 1.0 / 16.0, None, op0=ALU.mult),
                                 waits=[tpe] + (gate_par if ji <= 4 else []),
                                 inc="dve")
                    q.add_r(tev)
                    s1_toks.setdefault(name, []).append(tev)

            def rope_ops(buf_ap, cos_ap, sin_ap, t_ev):
                tr1 = S.emit("scalar", lambda s: s.mul(
                    rotb[0:64, :], buf_ap[64:128, :], -1.0),
                    waits=[t_ev, t_ropes] + rot_b.begin_write(),
                    inc="act", force=("act",))
                tr2 = S.emit("scalar", lambda s: s.copy(
                    rotb[64:128, :], buf_ap[0:64, :]), inc="act")
                rot_b.set_w(tr2)
                tm1 = S.emit("vector", lambda v: v.tensor_mul(buf_ap, buf_ap, cos_ap),
                             waits=[tr2], inc="dve")
                tm2 = S.emit("vector", lambda v: v.tensor_mul(rotb, rotb, sin_ap),
                             inc="dve", force=("dve",))
                tm3 = S.emit("vector", lambda v: v.tensor_add(buf_ap, buf_ap, rotb),
                             waits=[tm1, tm2], inc="dve", force=("dve",))
                rot_b.add_r(tm3)
                return tm3

            q_rope = [None] * 16
            k_rope = [[] for _ in range(16)]

            for mat_d, isQ in [(wqT_d, True), (wkT_d, False)]:
                for g in range(8):
                    sl, tw = wload(mat_d[g * 128:(g + 1) * 128, :])
                    for sub in range(2):
                        o = g * 2 + sub
                        if isQ:
                            segs = [(own_c0, 512, 0)]
                        else:
                            segs = ([(0, 512, 0), (512, 512, 512)]
                                    if p == 0 else [(0, 512, 0)])
                        for si, (hc0, w, kc0) in enumerate(segs):
                            q = [pA, pB][(o + si) % 2]
                            wq_ = q.begin_write()
                            tpe = None
                            for d in range(NT):
                                tpe = S.emit("tensor", lambda t, d_=d, q_=q, s_=sl,
                                             h0=hc0, w_=w, sub_=sub: t.matmul(
                                                 q_.ap[:, 0:w_],
                                                 wchunk(s_, d_, 256, sub_ * 128)[:, 0:128],
                                                 h_tile(d_, h0, w_),
                                                 start=(d_ == 0), stop=False),
                                             waits=(wq_ + [tw] + h_ready)
                                             if d == 0 else [], inc=None)
                            lb = lbq if isQ else lbk
                            s1buf = s1q if isQ else s1k
                            tpe = S.emit("tensor", lambda t, q_=q, o_=o, lb_=lb,
                                         s1_=s1buf, w_=w, kc=kc0: t.matmul(
                                             q_.ap[:, 0:w_],
                                             lb_[:, o_ * 128:(o_ + 1) * 128],
                                             s1_[:, kc:kc + w_],
                                             start=False, stop=True),
                                         waits=[t_lb] + s1_toks["q" if isQ else "k"],
                                         inc="pe")
                            q.set_w(tpe)
                            wslot[sl].add_r(tpe)
                            if isQ:
                                dst = q_tile(o)
                                tev = S.emit("vector", lambda v, d_=dst, q_=q:
                                             v.tensor_copy(d_, q_.ap),
                                             waits=[tpe, t_x1st], inc="dve")
                                q.add_r(tev)
                                q_rope[o] = rope_ops(dst, cosv(own_c0, 512),
                                                     sinv(own_c0, 512), tev)
                            else:
                                dst = kfm[:, o * 1024 + kc0: o * 1024 + kc0 + w]
                                war_gate = ([phase_end["attn_pe"]]
                                            if p == 1 else [])
                                tev = S.emit("vector", lambda v, d_=dst, q_=q:
                                             v.tensor_copy(d_, q_.ap),
                                             waits=[tpe] + war_gate, inc="dve")
                                q.add_r(tev)
                                k_rope[o].append(rope_ops(dst, cosv(kc0, w),
                                                          sinv(kc0, w), tev))

            nchunks = 8 if p == 0 else 4
            vp = [pA, pB, pC, pD]
            v_evac = {}
            vj = 0
            for g in range(8):
                sl, tw = wload(wvT_d[g * 128:(g + 1) * 128, :])
                for c in range(nchunks):
                    q = vp[vj % 4]
                    vj += 1
                    wq_ = q.begin_write()
                    col = 0
                    tpe = None
                    for d in range(NT):
                        tpe = S.emit("tensor", lambda t, d_=d, q_=q, s_=sl, c_=c,
                                     col_=col: t.matmul(
                                         q_.ap[:, col_:col_ + 256],
                                         h_tile(d_, c_ * 128, 128),
                                         wchunk(s_, d_, 256),
                                         start=(d_ == 0), stop=False),
                                     waits=(wq_ + [tw] + h_ready)
                                     if d == 0 else [], inc=None)
                    tpe = S.emit("tensor", lambda t, q_=q, g_=g, c_=c, col_=col:
                                 t.matmul(q_.ap[:, col_:col_ + 256],
                                          s1v[:, c_ * 128:(c_ + 1) * 128],
                                          lbv[:, g_ * 256:(g_ + 1) * 256],
                                          start=False, stop=True),
                                 waits=[t_lb] + s1_toks["v"], inc="pe")
                    q.set_w(tpe)
                    wslot[sl].add_r(tpe)
                    war_gate = [phase_end["attn_pe"]] if p == 1 else []
                    tev = S.emit("scalar", lambda s, q_=q, c_=c, g_=g, col_=col:
                                 s.copy(vtm[:, c_ * 2048 + g_ * 256:
                                            c_ * 2048 + (g_ + 1) * 256],
                                        q_.ap[:, col_:col_ + 256]),
                                 waits=[tpe] + war_gate, inc="act")
                    q.add_r(tev)
                    v_evac.setdefault(c, []).append(tev)

            qkv_pe_end = T("pe")
            if PHASES < 3:
                break

            t_masks = S.emit("sync", lambda sy: sy.dma_start(out=masks_s, in_=masks_d),
                             waits=[qkv_pe_end, T("dve"), T("act")], inc="cm")

            def kcols_for(qt):
                if p == 0:
                    sa = [(0, 512, qt * 128)]
                    sb = (512, 128, qt * 128 + 512)
                else:
                    jb = 512 - qt * 128
                    sa = [(0, jb, 512 + qt * 128)]
                    if qt > 0:
                        sa.append((jb, qt * 128, 0))
                    sb = (512, 128, qt * 128)
                return sa, sb

            def vchunk_for(qt, j5):
                if p == 0:
                    return qt + j5
                r = qt + j5
                return 4 + r if r < 4 else r - 4

            pA.r += pB.r
            pB.r = []
            pC.r += pD.r
            pD.r = []
            s_p = [pA, pC]
            av_p = [pE, pF]
            attn_done = []
            heads = [(qt, hh) for qt in range(4) for hh in range(16)]
            NH = len(heads)
            htok = {}

            def em_scores(gi):
                qt, hh = heads[gi]
                pi = gi % 2
                sa, sb = kcols_for(qt)
                qa = s_p[pi]
                wq_ = qa.begin_write()
                segs = sa + [sb]
                tsc = None
                for ii, (j0, w, kc) in enumerate(segs):
                    tsc = S.emit("tensor", lambda t, q_=qa, j_=j0, w_=w, kc_=kc,
                                 h_=hh, qt_=qt: t.matmul(
                                     q_.full[:, j_:j_ + w_],
                                     q_tile(h_, qt_ * 128, 128),
                                     kfm[:, h_ * 1024 + kc_: h_ * 1024 + kc_ + w_],
                                     start=True, stop=True),
                                 waits=(wq_ + [q_rope[hh]] + k_rope[hh])
                                 if ii == 0 else [],
                                 inc="pe" if ii == len(segs) - 1 else None)
                qa.set_w(tsc)
                htok[(gi, "s")] = tsc

            def em_softmax(gi):
                qt, hh = heads[gi]
                pi = gi % 2
                qa = s_p[pi]
                sm_b = smax_par[pi]
                e_b = eb[pi]
                wsm = sm_b.begin_write()
                mc = qt * 640 if p == 0 else 2560
                tmadd = S.emit("vector", lambda v, q_=qa, mc_=mc: v.tensor_add(
                    q_.full[:, 0:640], q_.full[:, 0:640], masks_s[:, mc_:mc_ + 640]),
                    waits=[htok[(gi, "s")], t_masks], inc="dve", force=("dve",))
                tm1 = S.emit("vector", lambda v, q_=qa, pp=pi: v.reduce_max(
                    smax_s[:, pp * 4: pp * 4 + 1], q_.full[:, 0:640], axis=AX.X,
                    negate=True), waits=[tmadd] + wsm, inc="dve", force=("dve",))
                we_ = e_b.begin_write()
                te1 = S.emit("scalar", lambda s, q_=qa, pp=pi: s.activation(
                    ebuf(pp)[:, 0:640], q_.full[:, 0:640], AF.Exp,
                    bias=smax_s[:, pp * 4: pp * 4 + 1],
                    accum_out=smax_s[:, pp * 4 + 2: pp * 4 + 3]),
                    waits=[tm1] + we_, inc="act")
                qa.add_r(te1)
                sm_b.add_r(te1)
                tri = S.emit("vector", lambda v, pp=pi: v.reciprocal(
                    smax_s[:, pp * 4 + 3: pp * 4 + 4],
                    smax_s[:, pp * 4 + 2: pp * 4 + 3]),
                    waits=[te1], inc="dve", force=("dve",))
                tnm = S.emit("vector", lambda v, pp=pi: v.tensor_scalar(
                    ebuf(pp)[:, 0:640], ebuf(pp)[:, 0:640],
                    smax_s[:, pp * 4 + 3: pp * 4 + 4], None, op0=ALU.mult),
                    waits=[tri], inc="dve", force=("dve",))
                e_b.set_w(tnm)
                sm_b.add_r(tnm)
                htok[(gi, "sm")] = tnm

            def em_transpose(gi):
                pi = gi % 2
                e_b, at_b = eb[pi], atb[pi]
                qx = [pT0, pT1][pi]
                wq_ = qx.begin_write()
                tpe = None
                for k in range(5):
                    tpe = S.emit("tensor", lambda t, q_=qx, k_=k, pp=pi: t.transpose(
                        q_.ap[:, k_ * 128:(k_ + 1) * 128],
                        ebuf(pp)[:, k_ * 128:(k_ + 1) * 128], identb_s[:]),
                        waits=([htok[(gi, "sm")]] + wq_) if k == 0 else [],
                        inc="pe" if k == 4 else None)
                qx.set_w(tpe)
                e_b.add_r(tpe)
                wat = at_b.begin_write()
                tat = S.emit("scalar", lambda s, q_=qx, pp=pi: s.copy(
                    atbuf(pp)[:, 0:640], q_.ap[:, 0:640]),
                    waits=[tpe] + wat, inc="act")
                qx.add_r(tat)
                at_b.set_w(tat)
                htok[(gi, "at")] = tat

            def em_av(gi):
                qt, hh = heads[gi]
                pi = gi % 2
                qv = av_p[pi]
                at_b = atb[pi]
                wq_ = qv.begin_write()
                tpe = None
                for j5 in range(5):
                    c = vchunk_for(qt, j5)
                    tpe = S.emit("tensor", lambda t, q_=qv, j_=j5, c_=c, h_=hh,
                                 pp=pi: t.matmul(
                                     q_.ap[:, 0:128],
                                     vtm[:, c_ * 2048 + h_ * 128:
                                         c_ * 2048 + (h_ + 1) * 128],
                                     atbuf(pp)[:, j_ * 128:(j_ + 1) * 128],
                                     start=(j_ == 0), stop=(j_ == 4)),
                                 waits=(wq_ + [htok[(gi, "at")]] +
                                        v_evac.get(c, [])) if j5 == 0 else [],
                                 inc="pe" if j5 == 4 else None)
                qv.set_w(tpe)
                at_b.add_r(tpe)
                tev = S.emit("scalar", lambda s, q_=qv, h_=hh, qt_=qt: s.copy(
                    ao_tile(h_, qt_ * 128, 128), q_.ap[:, 0:128]),
                    waits=[tpe, t_x1st, t_ropes], inc="act")
                qv.add_r(tev)
                attn_done.append(tev)

            for gi in range(NH + 2):
                if HEADS_ONLY is not None and gi < NH and gi not in HEADS_ONLY:
                    continue
                if gi < NH:
                    em_scores(gi)
                    em_softmax(gi)
                if ATTN_SUB >= 2 and 1 <= gi <= NH:
                    em_transpose(gi - 1)
                if ATTN_SUB >= 3 and gi >= 2:
                    em_av(gi - 2)
                if ATTN_BARRIER:
                    toks = [T("pe"), T("dve"), T("act")]
                    for e_ in ("tensor", "vector", "scalar"):
                        S.emit(e_, None, waits=toks, force=("pe", "dve", "act"))

            phase_end["attn_pe"] = T("pe")
            if PHASES < 4:
                break

            t_oB = S.emit("gpsimd", lambda g: g.dma_start(
                out=oBt, in_=loraB_d[:, 6144:8192]), waits=[qkv_pe_end], inc="cl")
            q = pC
            wq_ = q.begin_write()
            tpe = None
            for d in range(NT):
                tpe = S.emit("tensor", lambda t, d_=d, q_=q: t.matmul(
                    q_.ap[0:16, 0:512], loraA_s[:, 768 + d_ * 16: 768 + d_ * 16 + 16],
                    ao_tile(d_), start=(d_ == 0), stop=(d_ == 15)),
                    waits=(wq_ + attn_done) if d == 0 else [],
                    inc="pe" if d == 15 else None)
            q.set_w(tpe)
            t_s1o = S.emit("vector", lambda v, q_=q: v.tensor_scalar(
                s1o[:, 0:512], q_.ap[0:16, 0:512], 1.0 / 16.0, None, op0=ALU.mult),
                waits=[tpe], inc="dve")
            q.add_r(t_s1o)

            for g in range(8):
                sl, tw = wload(woT_d[g * 128:(g + 1) * 128, :])
                for sub in range(2):
                    c = g * 2 + sub
                    q = [pA, pB][c % 2]
                    wq_ = q.begin_write()
                    tpe = None
                    for d in range(NT):
                        tpe = S.emit("tensor", lambda t, d_=d, q_=q, s_=sl,
                                     sub_=sub: t.matmul(
                                         q_.ap,
                                         wchunk(s_, d_, 256, sub_ * 128)[:, 0:128],
                                         ao_tile(d_), start=(d_ == 0), stop=False),
                                     waits=(wq_ + [tw] + attn_done)
                                     if d == 0 else [], inc=None)
                    tpe = S.emit("tensor", lambda t, q_=q, c_=c: t.matmul(
                        q_.ap, oBt[:, c_ * 128:(c_ + 1) * 128], s1o[:, 0:512],
                        start=False, stop=True), waits=[t_oB, t_s1o], inc="pe")
                    q.set_w(tpe)
                    wslot[sl].add_r(tpe)
                    osb = ost_b2[c % 2]
                    wo_ = osb.begin_write()
                    tev = S.emit("scalar", lambda s, q_=q, pp=c % 2: s.copy(
                        ostage_p(pp), q_.ap), waits=[tpe] + wo_, inc="act")
                    q.add_r(tev)
                    osb.set_w(tev)
                    last_add = None
                    for tt in range(4):
                        qtr = [pE, pF][tt % 2]
                        wq2 = qtr.begin_write()
                        tpt = S.emit("tensor", lambda t, q_=qtr, t_=tt, pp=c % 2:
                                     t.transpose(
                                         q_.ap, ostage_p(pp)[:, t_ * 128:(t_ + 1) * 128],
                                         identf_s[:]),
                                     waits=[tev, t_idf] + wq2, inc="pe")
                        qtr.set_w(tpt)
                        tadd = S.emit("vector", lambda v, q_=qtr, t_=tt, c_=c:
                                      v.tensor_add(
                                          acc[:, t_ * 2048 + c_ * 128:
                                              t_ * 2048 + (c_ + 1) * 128],
                                          acc[:, t_ * 2048 + c_ * 128:
                                              t_ * 2048 + (c_ + 1) * 128],
                                          q_.ap),
                                      waits=[tpt, acc_last[p][tt]], inc="dve")
                        qtr.add_r(tadd)
                        acc_last[p][tt] = tadd
                        last_add = tadd
                    osb.add_r(last_add)

            phase_end["oproj_pe"] = T("pe")
            phase_end["oproj_dve"] = T("dve")
            phase_end["oproj_act"] = T("act")
            if PHASES < 5:
                break

            t_x1l = S.emit("sync", lambda sy: sy.dma_start(out=x1, in_=x1scr_d),
                           waits=[t_x1st, phase_end["oproj_pe"],
                                  phase_end["oproj_dve"], phase_end["oproj_act"]],
                           inc="x1l")
            wring[0] = 4
            wload_i[0] = wload_i[0] - (wload_i[0] % 2)
            vtm_gate = [phase_end["attn_pe"]]
            t_x1l_prev[0] = t_x1l
            z_toks = [None] * 64
            zh_tok = [None] * 64
            slot_ap = [z_tile(i) for i in range(64)]
            bspare = {"vector": spare_bf_ap, "gpsimd": spare_bf_ap2}
            bchain = {"vector": None, "gpsimd": None}
            bcnt = [0]
            pend_bf = []

            def emit_h128(mh):
                qh = [pE, pF][mh % 2]
                wqh = qh.begin_write()
                tph = S.emit("tensor", lambda t, m_=mh, q_=qh: t.matmul(
                    q_.full, h128_s[:], z_tile(m_), start=True, stop=True),
                    waits=wqh + [z_toks[mh]], inc="pe")
                qh.set_w(tph)
                tze = S.emit("scalar", lambda s, m_=mh, q_=qh: s.copy(
                    z_tile(m_), q_.full), waits=[tph], inc="act")
                qh.add_r(tze)
                zh_tok[mh] = tze
                pend_bf.append(mh)

            def emit_butterflies(mt):
                for s_i in range(6):
                    blk = 1 << (s_i + 1)
                    if (mt + 1) % blk != 0:
                        break
                    base = mt + 1 - blk
                    half = 1 << s_i
                    for k in range(half):
                        ia, ib = base + k, base + k + half
                        eng = "vector" if bcnt[0] % 2 == 0 else "gpsimd"
                        lbl = "dve" if eng == "vector" else "gp"
                        bcnt[0] += 1
                        a_ap = slot_ap[ia]
                        b_ap = slot_ap[ib]
                        sp_ap = bspare[eng]
                        tsub = S.emit(eng,
                                      lambda v, a=a_ap, b=b_ap, sp=sp_ap:
                                      v.tensor_sub(sp, a, b),
                                      waits=[zh_tok[ia], zh_tok[ib], bchain[eng]],
                                      inc=lbl, force=("dve", "gp"))
                        tadd = S.emit(eng,
                                      lambda v, a=a_ap, b=b_ap:
                                      v.tensor_add(a, a, b),
                                      waits=[tsub], inc=lbl, force=("dve", "gp"))
                        slot_ap[ib] = sp_ap
                        bspare[eng] = b_ap
                        zh_tok[ia] = tadd
                        zh_tok[ib] = tsub
                        bchain[eng] = tadd
            for mg in range(32):
                sl_u, tw_u = wload(wupT_d[mg * 128:(mg + 1) * 128, :],
                                   extra=(vtm_gate if mg < 2 else ()))
                sl_g, tw_g = wload(wgT_d[mg * 128:(mg + 1) * 128, :],
                                   extra=(vtm_gate if mg < 2 else ()))
                for sub in range(2):
                    mt = mg * 2 + sub
                    pu = [pA, pC][mt % 2]
                    pg = [pB, pD][mt % 2]
                    wq_ = pu.begin_write()
                    tpe = None
                    for d in range(NT):
                        tpe = S.emit("tensor", lambda t, d_=d, q_=pu, s_=sl_u,
                                     sub_=sub: t.matmul(
                                         q_.ap,
                                         wchunk(s_, d_, 256, sub_ * 128)[:, 0:128],
                                         x1_tile(d_), start=(d_ == 0),
                                         stop=(d_ == 15)),
                                     waits=(wq_ + [tw_u, t_x1l])
                                     if d == 0 else [],
                                     inc="pe" if d == 15 else None)
                    pu.set_w(tpe)
                    wslot[sl_u].add_r(tpe)
                    wu_ = ub[mt % 2].begin_write()
                    tu = S.emit("vector", lambda v, q_=pu, m_=mt, pp=mt % 2:
                                v.tensor_scalar(ubuf(pp), q_.ap,
                                                wscup_su[:, m_:m_ + 1], None,
                                                op0=ALU.mult),
                                waits=[tpe, t_dv] + wu_, inc="dve")
                    pu.add_r(tu)
                    ub[mt % 2].set_w(tu)
                    wq_ = pg.begin_write()
                    tpe = None
                    for d in range(NT):
                        tpe = S.emit("tensor", lambda t, d_=d, q_=pg, s_=sl_g,
                                     sub_=sub: t.matmul(
                                         q_.ap,
                                         wchunk(s_, d_, 256, sub_ * 128)[:, 0:128],
                                         x1_tile(d_), start=(d_ == 0),
                                         stop=(d_ == 15)),
                                     waits=(wq_ + [tw_g, t_x1l])
                                     if d == 0 else [],
                                     inc="pe" if d == 15 else None)
                    pg.set_w(tpe)
                    wslot[sl_g].add_r(tpe)
                    wg_ = gb[mt % 2].begin_write()
                    tg = S.emit("scalar", lambda s, q_=pg, m_=mt, pp=mt % 2:
                                s.activation(gbuf(pp), q_.ap, AF.Silu,
                                             scale=wscg_v[:, m_:m_ + 1]),
                                waits=[tpe] + wg_, inc="act")
                    pg.add_r(tg)
                    gb[mt % 2].set_w(tg)
                    zgate = ([phase_end["oproj_pe"]] if mt >= 32
                             else [qkv_pe_end])
                    tz = S.emit("vector", lambda v, m_=mt, pp=mt % 2: v.tensor_mul(
                        z_tile(m_), gbuf(pp), ubuf(pp)),
                        waits=[tu, tg] + zgate, inc="dve", force=("dve",))
                    ub[mt % 2].add_r(tz)
                    gb[mt % 2].add_r(tz)
                    z_toks[mt] = tz
                    if mt >= 1:
                        emit_h128(mt - 1)
                    if mt == 63:
                        emit_h128(63)
                    for mtq in list(pend_bf):
                        if zh_tok[mtq] is None:
                            break
                        pend_bf.remove(mtq)
                        emit_butterflies(mtq)
            phase_end["ug_pe"] = T("pe")
            for sp in (spare_bf_ap, spare_bf_ap2):
                for jx in range(64):
                    if slot_ap[jx] is sp:
                        free_slot = (bspare["vector"]
                                     if bspare["vector"] is not sp and
                                     bspare["vector"] is not spare_bf_ap and
                                     bspare["vector"] is not spare_bf_ap2
                                     else bspare["gpsimd"])
                        assert free_slot is not spare_bf_ap and free_slot is not spare_bf_ap2
                        tmv = S.emit("vector", lambda v, d_=free_slot, s_=sp:
                                     v.tensor_copy(d_, s_),
                                     waits=[zh_tok[jx], bchain["vector"],
                                            bchain["gpsimd"]],
                                     inc="dve", force=("dve",))
                        if bspare["vector"] is free_slot:
                            bspare["vector"] = sp
                        else:
                            bspare["gpsimd"] = sp
                        slot_ap[jx] = free_slot
                        zh_tok[jx] = tmv
                        bchain["vector"] = tmv
                        break
            zh_fin = zh_tok

            for cs in range(8):
                pair = [pA, pB] if cs % 2 == 0 else [pC, pD]

                def dload(li, cs_=cs):
                    return wload(wdT_d[(cs_ * 4 + li) * 128:
                                       (cs_ * 4 + li + 1) * 128, :], 4096)
                loads = [dload(0), dload(1), dload(2)]
                wq_ = pair[0].begin_write() + pair[1].begin_write()
                tpe = None
                for j in range(64):
                    sl, tw = loads[j // 16]
                    jj = j % 16
                    for ct in range(2):
                        q = pair[ct]
                        rhs_ap = slot_ap[j]
                        tpe = S.emit("tensor", lambda t, q_=q, s_=sl, jj_=jj,
                                     ct_=ct, j_=j, r_=rhs_ap: t.matmul(
                                         q_.ap,
                                         wslot_ap(s_, jj_ * 256 + ct_ * 128,
                                                  jj_ * 256 + ct_ * 128 + 128),
                                         r_,
                                         start=(j_ == 0), stop=(j_ == 63)),
                                     waits=((wq_ + [tw, zh_fin[j]])
                                            if (j == 0 and ct == 0) else
                                            ([tw, zh_fin[j]] if ct == 0 else [])),
                                     inc="pe" if (jj == 15 and ct == 1) else None)
                        if ct == 1 and jj == 15:
                            wslot[sl].add_r(tpe)
                    if jj == 15 and j // 16 + 3 < 4:
                        loads.append(dload(j // 16 + 3))
                for ct in range(2):
                    pair[ct].set_w(tpe)
                for ct in range(2):
                    c = cs * 2 + ct
                    q = pair[ct]
                    osb = ost_b2[c % 2]
                    wd_ = osb.begin_write()
                    tev = S.emit("scalar", lambda s, q_=q, c_=c, pp=c % 2: s.mul(
                        ostage_p(pp), q_.ap, wscdp_v[:, c_:c_ + 1]),
                        waits=[tpe] + wd_, inc="act")
                    q.add_r(tev)
                    osb.set_w(tev)
                    last_add = None
                    for tt in range(4):
                        qtr = [pE, pF][tt % 2]
                        wq2 = qtr.begin_write()
                        tpt = S.emit("tensor", lambda t, q_=qtr, t_=tt, pp=c % 2:
                                     t.transpose(
                                         q_.ap, ostage_p(pp)[:, t_ * 128:(t_ + 1) * 128],
                                         identf_s[:]),
                                     waits=[tev, t_idf] + wq2, inc="pe")
                        qtr.set_w(tpt)
                        tadd = S.emit("vector", lambda v, q_=qtr, t_=tt, c_=c:
                                      v.tensor_add(
                                          acc[:, t_ * 2048 + c_ * 128:
                                              t_ * 2048 + (c_ + 1) * 128],
                                          acc[:, t_ * 2048 + c_ * 128:
                                              t_ * 2048 + (c_ + 1) * 128],
                                          q_.ap),
                                      waits=[tpt, acc_last[p][tt]], inc="dve")
                        qtr.add_r(tadd)
                        acc_last[p][tt] = tadd
                        last_add = tadd
                    osb.add_r(last_add)

            phase_end["ffn_pe"] = T("pe")
            phase_end["ffn_dve"] = T("dve")
            phase_end["ffn_act"] = T("act")
            wring[0] = 2
            wload_i[0] = wload_i[0] - (wload_i[0] % 2)

            if PHASES < 6 and p == 0:
                for tt in range(4):
                    store_toks[p][tt] = S.emit("sync", lambda sy, t_=tt, p_=p:
                                               sy.dma_start(
                                                   out=out_d[p_ * 512 + t_ * 128:
                                                             p_ * 512 + (t_ + 1) * 128, :],
                                                   in_=acc[:, t_ * 2048:(t_ + 1) * 2048]),
                                               waits=[acc_last[p][tt]], inc="cst")
                break
            for tt in range(4):
                store_toks[p][tt] = S.emit("sync", lambda sy, t_=tt, p_=p:
                                           sy.dma_start(
                                               out=out_d[p_ * 512 + t_ * 128:
                                                         p_ * 512 + (t_ + 1) * 128, :],
                                               in_=acc[:, t_ * 2048:(t_ + 1) * 2048]),
                                           waits=[acc_last[p][tt]], inc="cst")

        tap_map = {
            "h": bigA[:, 0:16384], "qreg": bigA[:, 16384:24576],
            "ao": bigA[:, 24576:32768], "kfm": kfm, "vtm": vtm,
            "x1": x1, "z": bigA, "acc": acc, "x1scr": x1scr_d, "par": par,
        }
        for name, shape, dt in DEBUG_SPEC:
            src = tap_map[name]
            S.emit("sync", lambda sy, s_=src, d_=dbg_d[name]: sy.dma_start(
                out=d_, in_=s_),
                waits=[T("pe"), T("dve"), T("act"), T("gp")], inc="cst")

        S.emit("sync", None, waits=[T("cst")])

        def replay(eng_name):
            def f(eng):
                for (waits, fn, inc, amt) in S.ops[eng_name]:
                    for (lbl, val) in waits:
                        eng.wait_ge(sems[lbl], val)
                    if fn is None:
                        continue
                    inst = fn(eng)
                    if inc is not None:
                        inst.then_inc(sems[inc], amt)
            return f

        block.sync(replay("sync"))
        block.gpsimd(replay("gpsimd"))
        block.tensor(replay("tensor"))
        block.vector(replay("vector"))
        block.scalar(replay("scalar"))

    return nc



def _tile_proj(wT, ng, gw):
    d_in, n_out = wT.shape
    nc_ = d_in // 128
    return np.ascontiguousarray(
        wT.reshape(nc_, 128, ng, gw).transpose(2, 1, 0, 3).reshape(ng * 128, nc_ * gw))


def _tile_down(wT):
    arr = wT.reshape(4, 16, 128, 8, 256)
    return np.ascontiguousarray(
        arr.transpose(3, 0, 2, 1, 4).reshape(4 * 8 * 128, 16 * 256))


def make_in_maps(inputs):
    x = np.ascontiguousarray(np.asarray(inputs["L0_attn_x"], np.float32))
    wqT = _tile_proj(np.asarray(inputs["L0_attn_wq"], np.float32).T, 8, 256)
    wkT = _tile_proj(np.asarray(inputs["L0_attn_wk"], np.float32).T, 8, 256)
    wvT = _tile_proj(np.asarray(inputs["L0_attn_wv"], np.float32).T, 8, 256)
    woT = _tile_proj(np.asarray(inputs["L0_attn_wo"], np.float32).T, 8, 256)
    wupT = _tile_proj(np.asarray(inputs["L0_ffn_W_up"], np.float32).T, 32, 256)
    wgT = _tile_proj(np.asarray(inputs["L0_ffn_W_gate"], np.float32).T, 32, 256)
    wdT = _tile_down(np.asarray(inputs["L0_ffn_W_down"], np.float32).T)
    loraA = _pack_loraA(inputs["L0_attn_qA"], inputs["L0_attn_kA"],
                        inputs["L0_attn_vA"], inputs["L0_attn_oA"])
    loraB = _pack_loraB(inputs["L0_attn_qB"], inputs["L0_attn_kB"],
                        inputs["L0_attn_vB"], inputs["L0_attn_oB"])
    wsc = np.asarray(inputs["L0_ffn_Wscale_ug"], np.float32)
    vecs = np.concatenate([
        _pack_cols(np.asarray(inputs["L0_attn_norm_w"], np.float32), 16),
        _pack_cols(np.asarray(inputs["L0_ffn_SU_ug"], np.float32), 16),
        _pack_cols(wsc[:8192], 64),
        _pack_cols(wsc[8192:], 64),
        _pack_cols(np.asarray(inputs["L0_ffn_SU_dp"], np.float32), 64),
        _pack_cols(np.asarray(inputs["L0_ffn_Wscale_dp"], np.float32), 16),
    ], axis=1).astype(np.float32)
    h128 = _h128_bf16()
    identb = np.eye(128, dtype=BF_NP)
    identf = np.eye(128, dtype=np.float32)

    in_maps = []
    for c in range(NCORES):
        beta, gamma = c // 4, c % 4
        s = gamma * 1024
        xs = np.zeros((1536, D), np.float32)
        lo = s - 512
        src_lo = max(lo, 0)
        xs[src_lo - lo:, :] = x[beta, src_lo: s + 1024, :]
        in_maps.append({
            "xs": xs, "wqT": wqT, "wkT": wkT, "wvT": wvT, "woT": woT,
            "wupT": wupT, "wgT": wgT, "wdT": wdT,
            "loraA": loraA, "loraB": loraB,
            "masks": _masks_bf16(gamma), "ropes": _ropes_f32(gamma),
            "vecs": vecs, "h128": h128, "identb": identb, "identf": identf,
        })
    return in_maps


def kernel(**inputs):
    in_maps = make_in_maps(inputs)
    nc = build_nc()
    res = run_bass_kernel_spmd(nc, in_maps, core_ids=list(range(NCORES)))
    outs = [np.asarray(res.results[c]["out"], np.float32) for c in range(NCORES)]
    return np.stack([np.concatenate(outs[:4], axis=0),
                     np.concatenate(outs[4:], axis=0)])


# version 08
# speedup vs baseline: 1.4511x; 1.0464x over previous
import numpy as np
import ml_dtypes
import concourse.bass as bass
import concourse.mybir as mybir
from concourse.bass_utils import run_bass_kernel_spmd

F32 = mybir.dt.float32
BF16 = mybir.dt.bfloat16
AF = mybir.ActivationFunctionType
ALU = mybir.AluOpType
AX = mybir.AxisListType
BF_NP = ml_dtypes.bfloat16

D = 2048
NT = 16
TP = 512
NCORES = 8

OWNER = {
    "cx": "sync", "cst": "sync", "x1l": "sync", "cc": "sync", "cr": "sync",
    "cm": "sync", "ctr": "sync", "cts": "sync", "cx2": "sync",
    "cl": "gpsimd", "wa": "gpsimd", "wb": "gpsimd", "wc": "gpsimd",
    "wd": "gpsimd",
    "pe": "tensor", "dve": "vector", "act": "scalar", "gp": "gpsimd",
}
DMA_LBL = {"cx", "cx2", "cst", "x1l", "cl", "wa", "wb", "wc", "wd", "cc", "cr", "cm", "ctr", "cts"}

DEBUG_SPEC = []
PHASES = 99
ATTN_SUB = 3
TR_LIMIT = 9999
TR_EVAC = True
ATTN_BARRIER = False
HEADS_ONLY = None
SCORES_RAW = False


class Sch:
    ENGS = ("sync", "gpsimd", "tensor", "vector", "scalar")

    def __init__(self):
        self.ops = {e: [] for e in self.ENGS}
        self.cnt = {k: 0 for k in OWNER}

    def tok(self, lbl):
        return (lbl, self.cnt[lbl])

    def emit(self, eng, fn, waits=(), inc=None, force=()):
        wmax = {}
        for t in waits:
            if t is None:
                continue
            lbl, val = t[0], t[1]
            if val <= 0:
                continue
            if lbl in DMA_LBL:
                val = self.cnt[lbl]
                if val <= 0:
                    continue
            elif OWNER[lbl] == eng and lbl not in force:
                continue
            if wmax.get(lbl, 0) < val:
                wmax[lbl] = val
        amt = 0
        if inc is not None:
            amt = 16 if inc in DMA_LBL else 1
            self.cnt[inc] += amt
        self.ops[eng].append((sorted(wmax.items()), fn, inc, amt))
        return (inc, self.cnt[inc]) if inc is not None else None


class PBuf:
    def __init__(self):
        self.w = None
        self.r = []
        self.ap = None

    def begin_write(self):
        w = list(self.r)
        self.r = []
        return w

    def set_w(self, tok):
        self.w = tok

    def add_r(self, tok):
        if tok is not None:
            self.r.append(tok)



def _popcount_mat():
    i = np.arange(128)
    a = i[:, None] & i[None, :]
    pc = np.zeros_like(a)
    v = a.copy()
    while v.any():
        pc += v & 1
        v >>= 1
    return pc


def _h128_bf16():
    return np.where(_popcount_mat() % 2 == 0, 1.0, -1.0).astype(BF_NP)


def _masks_bf16(gamma):
    NEG = np.float32(-1e30)
    a = np.arange(128)[:, None]
    j = np.arange(512)[None, :]
    std = (j >= a + 1)
    b = np.arange(128)[None, :]
    caus = (b <= a)
    out = np.full((128, 3200), NEG, np.float32)
    for qt in range(4):
        m = std.copy()
        if gamma == 0:
            m &= (j >= 512 - qt * 128)
        out[:, qt * 640: qt * 640 + 512] = np.where(m, 0.0, NEG)
        out[:, qt * 640 + 512: qt * 640 + 640] = np.where(caus, 0.0, NEG)
    out[:, 2560:3072] = np.where(std, 0.0, NEG)
    out[:, 3072:3200] = np.where(caus, 0.0, NEG)
    return out.astype(BF_NP)


def _ropes_f32(gamma):
    half = 64
    theta = 1.0 / 10000.0 ** (np.arange(half, dtype=np.float64) / half)
    th = np.concatenate([theta, theta])[:, None]
    scale = 128.0 ** -0.25
    out = np.zeros((128, 3072), np.float32)
    g0 = gamma * 1024
    t0 = np.arange(g0 - 512, g0 + 512, dtype=np.float64)[None, :]
    c0 = np.cos(th * t0) * scale
    s0 = np.sin(th * t0) * scale
    neg = (t0[0] < 0)
    c0[:, neg] = 0.0
    s0[:, neg] = 0.0
    out[:, 0:1024] = c0
    out[:, 1024:2048] = s0
    t1 = np.arange(g0 + 512, g0 + 1024, dtype=np.float64)[None, :]
    out[:, 2048:2560] = np.cos(th * t1) * scale
    out[:, 2560:3072] = np.sin(th * t1) * scale
    return out


def _pack_cols(v, n):
    return np.ascontiguousarray(v.reshape(n, 128).T)


def _pack_loraA(qA, kA, vA, oA):
    outs = []
    for A in (qA, kA, vA, oA):
        AT = np.asarray(A, np.float32).T.reshape(16, 128, 16).transpose(1, 0, 2).reshape(128, 256)
        outs.append(AT)
    return np.ascontiguousarray(np.concatenate(outs, axis=1))


def _pack_loraB(qB, kB, vB, oB):
    return np.ascontiguousarray(np.concatenate(
        [np.asarray(m, np.float32).T for m in (qB, kB, vB, oB)], axis=1))



def build_nc():
    nc = bass.Bass("TRN2", target_bir_lowering=False)

    xs_d = nc.dram_tensor("xs", [1536, D], F32, kind="ExternalInput").ap()
    wqT_d = nc.dram_tensor("wqT", [1024, 4096], F32, kind="ExternalInput").ap()
    wkT_d = nc.dram_tensor("wkT", [1024, 4096], F32, kind="ExternalInput").ap()
    wvT_d = nc.dram_tensor("wvT", [1024, 4096], F32, kind="ExternalInput").ap()
    woT_d = nc.dram_tensor("woT", [1024, 4096], F32, kind="ExternalInput").ap()
    wupT_d = nc.dram_tensor("wupT", [4096, 4096], F32, kind="ExternalInput").ap()
    wgT_d = nc.dram_tensor("wgT", [4096, 4096], F32, kind="ExternalInput").ap()
    wdT_d = nc.dram_tensor("wdT", [4096, 4096], F32, kind="ExternalInput").ap()
    loraA_d = nc.dram_tensor("loraA", [128, 1024], F32, kind="ExternalInput").ap()
    loraB_d = nc.dram_tensor("loraB", [16, 8192], F32, kind="ExternalInput").ap()
    masks_d = nc.dram_tensor("masks", [128, 3200], BF16, kind="ExternalInput").ap()
    ropes_d = nc.dram_tensor("ropes", [128, 3072], F32, kind="ExternalInput").ap()
    vecs_d = nc.dram_tensor("vecs", [128, 240], F32, kind="ExternalInput").ap()
    h128_d = nc.dram_tensor("h128", [128, 128], BF16, kind="ExternalInput").ap()
    identb_d = nc.dram_tensor("identb", [128, 128], BF16, kind="ExternalInput").ap()
    identf_d = nc.dram_tensor("identf", [128, 128], F32, kind="ExternalInput").ap()
    out_d = nc.dram_tensor("out", [1024, D], F32, kind="ExternalOutput").ap()
    x1scr_d = nc.dram_tensor("x1scr", [128, 8192], BF16, kind="Internal").ap()
    dbg_d = {}
    for name, shape, dt in DEBUG_SPEC:
        dbg_d[name] = nc.dram_tensor("dbg_" + name, list(shape), dt,
                                     kind="ExternalOutput").ap()

    S = Sch()
    T = S.tok

    from contextlib import ExitStack
    es = ExitStack()
    with es:
        acc_t = es.enter_context(nc.sbuf_tensor("acc", [128, 8192], F32))
        kfm_t = es.enter_context(nc.sbuf_tensor("kfm", [128, 16384], BF16))
        vtm_t = es.enter_context(nc.sbuf_tensor("vtm", [128, 16384], BF16))
        bigA_t = es.enter_context(nc.sbuf_tensor("bigA", [128, 32768], BF16))
        par_t = es.enter_context(nc.sbuf_tensor("parena", [128, 12288], BF16))
        war_t = es.enter_context(nc.sbuf_tensor("warena", [128, 8192], BF16))
        loraA_s = es.enter_context(nc.sbuf_tensor("loraA_s", [128, 1024], BF16))
        h128_s = es.enter_context(nc.sbuf_tensor("h128_s", [128, 128], BF16))
        identb_s = es.enter_context(nc.sbuf_tensor("identb_s", [128, 128], BF16))
        identf_s = es.enter_context(nc.sbuf_tensor("identf_s", [128, 128], F32))
        vecs_s = es.enter_context(nc.sbuf_tensor("vecs_s", [128, 240], F32))
        dvecs_s = es.enter_context(nc.sbuf_tensor("dvecs", [128, 144], F32))
        stats_s = es.enter_context(nc.sbuf_tensor("stats", [128, 32], F32))
        smax_s = es.enter_context(nc.sbuf_tensor("smax", [128, 8], F32))
        pAB_t = es.enter_context(nc.psum_tensor("pAB", [128, 1024], F32))
        pCD_t = es.enter_context(nc.psum_tensor("pCD", [128, 1024], F32))
        pT_t = es.enter_context(nc.psum_tensor("pT", [128, 2048], BF16))
        pE_t = es.enter_context(nc.psum_tensor("pE", [128, 512], F32))
        pF_t = es.enter_context(nc.psum_tensor("pF", [128, 512], F32))
        s_cx = es.enter_context(nc.semaphore("s_cx"))
        s_cc = es.enter_context(nc.semaphore("s_cc"))
        s_cr = es.enter_context(nc.semaphore("s_cr"))
        s_cm = es.enter_context(nc.semaphore("s_cm"))
        s_ctr = es.enter_context(nc.semaphore("s_ctr"))
        s_cts = es.enter_context(nc.semaphore("s_cts"))
        s_cx2 = es.enter_context(nc.semaphore("s_cx2"))
        s_cst = es.enter_context(nc.semaphore("s_cst"))
        s_x1l = es.enter_context(nc.semaphore("s_x1l"))
        s_cl = es.enter_context(nc.semaphore("s_cl"))
        s_wa = es.enter_context(nc.semaphore("s_wa"))
        s_wb = es.enter_context(nc.semaphore("s_wb"))
        s_wc = es.enter_context(nc.semaphore("s_wc"))
        s_wd = es.enter_context(nc.semaphore("s_wd"))
        s_pe = es.enter_context(nc.semaphore("s_pe"))
        s_dve = es.enter_context(nc.semaphore("s_dve"))
        s_act = es.enter_context(nc.semaphore("s_act"))
        s_gp = es.enter_context(nc.semaphore("s_gp"))
        block = es.enter_context(nc.Block())

        sems = {"cx": s_cx, "cst": s_cst, "x1l": s_x1l, "cl": s_cl, "wa": s_wa,
                "wb": s_wb, "wc": s_wc, "wd": s_wd, "pe": s_pe, "dve": s_dve, "act": s_act, "gp": s_gp,
                "cc": s_cc, "cr": s_cr, "cm": s_cm, "ctr": s_ctr, "cts": s_cts, "cx2": s_cx2}

        acc = acc_t[:]
        kfm = kfm_t[:]
        vtm = vtm_t[:]
        bigA = bigA_t[:]
        par = par_t[:]
        war = war_t[:]

        def h_tile(f, c0, w):
            return bigA[:, f * 1024 + c0: f * 1024 + c0 + w]

        def q_tile(o, c0=0, w=TP):
            return bigA[:, 16384 + o * 512 + c0: 16384 + o * 512 + c0 + w]

        def ao_tile(o, c0=0, w=TP):
            return bigA[:, 24576 + o * 512 + c0: 24576 + o * 512 + c0 + w]

        def z_tile(m, c0=0, w=TP):
            return bigA[:, m * 512 + c0: m * 512 + c0 + w]

        ropes_f = bigA[:, 24576:32768].bitcast(F32)
        x0_tile = q_tile
        x1b_tile = ao_tile

        xstage0 = par[:, 0:4096].bitcast(F32)
        xstage1 = vtm[:, 0:4096].bitcast(F32)

        def xstage_p(pp):
            return xstage0 if pp == 0 else xstage1

        def xn_buf(p):
            return par[:, 4096 + p * 2048: 6144 + p * 2048]

        def hst_buf(p):
            return par[:, 8192 + p * 2048: 10240 + p * 2048]

        lbq = par[0:16, 0:2048]
        lbk = par[0:16, 2048:4096]
        lbv = par[0:16, 4096:6144]
        s1q = par[0:16, 6144:6656]
        s1k = par[0:16, 6656:7680]
        s1v = par[0:16, 7680:8704]
        rotb = par[:, 8704:9216]
        masks_s = par[:, 0:3200]

        def ebuf(p):
            return par[:, 3200 + p * 640: 3840 + p * 640]

        def atbuf(p):
            return par[:, 4480 + p * 640: 5120 + p * 640]

        s1o = par[0:16, 5760:6272]
        oBt = par[0:16, 6272:8320]
        def ostage_p(pp):
            return par[:, 10240 + pp * 1024: 11264 + pp * 1024].bitcast(F32)
        x1 = par[:, 0:8192]

        def x1_tile(f):
            return par[:, f * 512:(f + 1) * 512]

        def ubuf(p):
            return par[:, 8192 + p * 512: 8704 + p * 512]

        def gbuf(p):
            return par[:, 9216 + p * 512: 9728 + p * 512]

        spare_bf_ap = par[:, 10240:10752]
        spare_bf_ap2 = par[:, 10752:11264]

        normw_v = vecs_s[:, 0:16]
        suug_v = vecs_s[:, 16:32]
        wscup_v = vecs_s[:, 32:96]
        wscg_v = vecs_s[:, 96:160]
        sudp_v = vecs_s[:, 160:224]
        wscdp_v = vecs_s[:, 224:240]
        su_a = dvecs_s[:, 0:16]
        su_b = dvecs_s[:, 16:80]
        wscup_su = dvecs_s[:, 80:144]

        ST = {n: i for i, n in enumerate(
            ["ssum", "ssq", "m", "q2", "mm", "var", "varp", "rp", "qq", "qq2",
             "rp2", "rstd", "rstd2", "nmr", "onep"])}

        def st(n, p):
            c = ST[n] * 2 + p
            return stats_s[:, c:c + 1]

        pA, pB, pC, pD = PBuf(), PBuf(), PBuf(), PBuf()
        pE, pF = PBuf(), PBuf()
        pT0, pT1 = PBuf(), PBuf()
        pA.ap, pB.ap = pAB_t[:, 0:512], pAB_t[:, 512:1024]
        pC.ap, pD.ap = pCD_t[:, 0:512], pCD_t[:, 512:1024]
        pA.full, pC.full = pAB_t[:], pCD_t[:]
        pE.ap, pF.ap = pE_t[:, 0:128], pF_t[:, 0:128]
        pE.full, pF.full = pE_t[:], pF_t[:]
        pT0.ap, pT1.ap = pT_t[:, 0:1024], pT_t[:, 1024:2048]

        xstage_b2 = [PBuf(), PBuf()]
        xnb = [PBuf(), PBuf()]
        hstb = [PBuf(), PBuf()]
        wslot = [PBuf(), PBuf(), PBuf(), PBuf()]
        wslot_lbl = ["wa", "wb", "wc", "wd"]
        rot_b = PBuf()
        eb = [PBuf(), PBuf()]
        atb = [PBuf(), PBuf()]
        ub = [PBuf(), PBuf()]
        gb = [PBuf(), PBuf()]
        stat_par = [PBuf(), PBuf()]
        smax_par = [PBuf(), PBuf()]
        ost_b2 = [PBuf(), PBuf()]

        wload_i = [0]
        wring = [2]

        def wslot_ap(s, a, b2):
            if s < 2:
                return war[:, s * 4096 + a: s * 4096 + b2]
            return vtm[:, (s - 2) * 4096 + a: (s - 2) * 4096 + b2]

        def wload(in_ap, nelem=4096, extra=()):
            s = wload_i[0] % wring[0]
            wload_i[0] += 1
            b = wslot[s]
            out_ap = wslot_ap(s, 0, nelem)
            waits = b.begin_write() + list(extra)
            t = S.emit("gpsimd",
                       lambda g, o=out_ap, i=in_ap: g.dma_start(out=o, in_=i),
                       waits=waits, inc=wslot_lbl[s])
            b.set_w(t)
            return s, t

        def wchunk(s, d, w, off=0):
            return wslot_ap(s, d * w + off, d * w + w)

        taps = {}

        t_vecs = S.emit("sync", lambda sy: sy.dma_start(out=vecs_s[:], in_=vecs_d), inc="cc")
        t_h128 = S.emit("sync", lambda sy: sy.dma_start(out=h128_s[:], in_=h128_d), inc="cc")
        t_idb = S.emit("sync", lambda sy: sy.dma_start(out=identb_s[:], in_=identb_d), inc="cc")
        t_idf = S.emit("sync", lambda sy: sy.dma_start(out=identf_s[:], in_=identf_d), inc="cc")
        t_loraA = S.emit("gpsimd", lambda g: g.dma_start(out=loraA_s[:], in_=loraA_d), inc="cl")

        t1 = S.emit("vector", lambda v: v.tensor_scalar(
            su_a, suug_v, float(1.0 / np.sqrt(2048.0)), None, op0=ALU.mult),
            waits=[t_vecs], inc="dve")
        t2 = S.emit("vector", lambda v: v.tensor_scalar(
            su_b, sudp_v, float(1.0 / np.sqrt(8192.0)), None, op0=ALU.mult),
            inc="dve")
        t_dv = S.emit("vector", lambda v: v.tensor_mul(wscup_su, wscup_v, su_b),
                      waits=[t2], force=("dve",), inc="dve")


        acc_last = [[None] * 4, [None] * 4]
        store_toks = [[None] * 4, [None] * 4]
        phase_end = {}
        t_x1l_prev = [None]

        for p in range(2):
            n_tiles = 8 if p == 0 else 4
            tile_rows0 = 0 if p == 0 else 1024
            h_ready = []
            x0_ready = []
            gate_bigA = []
            gate_par = []
            if p == 1:
                gate_bigA = [phase_end["ffn_pe"], phase_end["ffn_dve"],
                             phase_end["ffn_act"]]
                gate_par = gate_bigA

            ptq_rot = [pT0, pT1]
            ptq_i = [0]

            for i in range(n_tiles):
                par2 = i % 2
                row0 = tile_rows0 + i * 128
                own = (p == 1) or (i >= 4)
                hc = row0 - tile_rows0
                xsb = xstage_b2[par2]
                gx = []
                if p == 1 and i < 2:
                    gx = ([phase_end["ug_pe"]] if par2 == 0
                          else [phase_end["ffn_pe"]])
                waits = xsb.begin_write() + gx
                tx = S.emit("sync", lambda sy, r=row0, pp=par2: sy.dma_start(
                    out=xstage_p(pp), in_=xs_d[r:r + 128, :]), waits=waits,
                    inc="cx" if par2 == 0 else "cx2")
                xsb.set_w(tx)

                sb = stat_par[par2]
                wstat = sb.begin_write()
                t_sum = S.emit("vector", lambda v, pp=par2: v.reduce_sum(
                    st("ssum", pp), xstage_p(pp), axis=AX.X),
                    waits=[tx] + wstat, inc="dve", force=("dve",))
                t_sq = S.emit("scalar", lambda s, pp=par2: s.activation(
                    hst_buf(pp), xstage_p(pp), AF.Square, accum_out=st("ssq", pp)),
                    waits=[tx] + wstat + hstb[par2].begin_write() +
                          (gate_par if i < 2 else []),
                    inc="act", force=("act",))

                def dvstep(fn, waits=()):
                    return S.emit("vector", fn, waits=list(waits), inc="dve",
                                  force=("dve",))

                td = dvstep(lambda v, pp=par2: v.tensor_scalar(
                    st("m", pp), st("ssum", pp), 1.0 / 2048.0, None, op0=ALU.mult),
                    [t_sum])
                td = dvstep(lambda v, pp=par2: v.tensor_scalar(
                    st("q2", pp), st("ssq", pp), 1.0 / 2048.0, None, op0=ALU.mult),
                    [td, t_sq])
                td = dvstep(lambda v, pp=par2: v.tensor_mul(
                    st("mm", pp), st("m", pp), st("m", pp)), [td])
                td = dvstep(lambda v, pp=par2: v.tensor_sub(
                    st("var", pp), st("q2", pp), st("mm", pp)), [td])
                td = dvstep(lambda v, pp=par2: v.tensor_scalar(
                    st("varp", pp), st("var", pp), 1e-5, None, op0=ALU.add), [td])
                td = dvstep(lambda v, pp=par2: v.reciprocal(
                    st("rp", pp), st("varp", pp)), [td])
                td = dvstep(lambda v, pp=par2: v.tensor_mul(
                    st("qq", pp), st("var", pp), st("rp", pp)), [td])
                td = dvstep(lambda v, pp=par2: v.tensor_scalar(
                    st("qq2", pp), st("qq", pp), 1e-6, None, op0=ALU.add), [td])
                td = dvstep(lambda v, pp=par2: v.reciprocal(
                    st("rp2", pp), st("qq2", pp)), [td])
                tsq1 = S.emit("scalar", lambda s, pp=par2: s.sqrt(
                    st("rstd", pp), st("rp", pp)), waits=[td], inc="act",
                    force=("act",))
                tsq2 = S.emit("scalar", lambda s, pp=par2: s.sqrt(
                    st("rstd2", pp), st("rp2", pp)), waits=[td], inc="act",
                    force=("act",))
                t_nmr = dvstep(lambda v, pp=par2: v.tensor_scalar(
                    st("nmr", pp), st("m", pp), st("rstd", pp), -1.0,
                    op0=ALU.mult, op1=ALU.mult), [tsq1])
                t_on = dvstep(lambda v, pp=par2: v.tensor_scalar(
                    st("onep", pp), st("rstd", pp), 1.0, None, op0=ALU.add), [tsq1])

                wxn = xnb[par2].begin_write()
                txn = S.emit("scalar", lambda s, pp=par2: s.activation(
                    xn_buf(pp), xstage_p(pp), AF.Identity, bias=st("nmr", pp),
                    scale=st("rstd", pp)),
                    waits=[t_nmr, tsq1] + wxn + (gate_par if i < 2 else []),
                    inc="act", force=("act",))
                xnb[par2].set_w(txn)
                t_xlast = txn
                if own:
                    tt = (row0 - (512 + p * 512)) // 128
                    tacc = S.emit("scalar", lambda s, pp=par2, t_=tt: s.activation(
                        acc[:, t_ * 2048:(t_ + 1) * 2048], xstage_p(pp), AF.Identity,
                        bias=st("nmr", pp), scale=st("onep", pp)),
                        waits=[t_on, t_nmr,
                               store_toks[0][tt] if p == 1 else None],
                        inc="act", force=("act",))
                    acc_last[p][tt] = tacc
                    t_xlast = tacc
                xsb.add_r(td)
                xsb.add_r(t_xlast)
                th = S.emit("scalar", lambda s, pp=par2: s.activation(
                    hst_buf(pp), xn_buf(pp), AF.Identity, scale=st("rstd2", pp)),
                    waits=[txn, tsq2], inc="act", force=("act",))
                hstb[par2].set_w(th)

                for half in range(2):
                    q = ptq_rot[ptq_i[0] % 2]
                    ptq_i[0] += 1
                    wq_ = q.begin_write()
                    tpe = None
                    for k in range(8):
                        f = half * 8 + k
                        tpe = S.emit("tensor", lambda t, f_=f, q_=q, k_=k, pp=par2:
                                     t.transpose(
                                         q_.ap[:, k_ * 128:(k_ + 1) * 128],
                                         hst_buf(pp)[:, f_ * 128:(f_ + 1) * 128],
                                         identb_s[:]),
                                     waits=([th, t_idb] + wq_ + gate_bigA)
                                     if k == 0 else [],
                                     inc="pe" if k == 7 else None)
                    q.set_w(tpe)
                    hstb[par2].add_r(tpe)
                    tev = None
                    for k in range(8):
                        f = half * 8 + k
                        tev = S.emit("vector", lambda v, f_=f, q_=q, k_=k, c_=hc:
                                     v.tensor_scalar(
                                         h_tile(f_, c_, 128),
                                         q_.ap[:, k_ * 128:(k_ + 1) * 128],
                                         normw_v[:, f_:f_ + 1], None, op0=ALU.mult),
                                     waits=[tpe] if k == 0 else [],
                                     inc="dve" if k == 7 else None)
                    q.add_r(tev)
                    h_ready.append(tev)
                if own:
                    oc = hc - (512 if p == 0 else 0)
                    for half in range(2):
                        q = ptq_rot[ptq_i[0] % 2]
                        ptq_i[0] += 1
                        wq_ = q.begin_write()
                        tpe = None
                        for k in range(8):
                            f = half * 8 + k
                            tpe = S.emit("tensor", lambda t, f_=f, q_=q, k_=k, pp=par2:
                                         t.transpose(
                                             q_.ap[:, k_ * 128:(k_ + 1) * 128],
                                             xn_buf(pp)[:, f_ * 128:(f_ + 1) * 128],
                                             identb_s[:]),
                                         waits=([txn, t_idb] + wq_ + gate_bigA)
                                         if k == 0 else [],
                                         inc="pe" if k == 7 else None)
                        q.set_w(tpe)
                        xnb[par2].add_r(tpe)
                        tev = None
                        for k in range(8):
                            f = half * 8 + k
                            tev = S.emit("vector", lambda v, f_=f, q_=q, k_=k, o_=oc:
                                         v.tensor_scalar(
                                             x0_tile(f_, o_, 128),
                                             q_.ap[:, k_ * 128:(k_ + 1) * 128],
                                             su_a[:, f_:f_ + 1], None, op0=ALU.mult),
                                         waits=[tpe, t1] if k == 0 else [],
                                         inc="dve" if k == 7 else None)
                        q.add_r(tev)
                        x0_ready.append(tev)

            x1_toks = []
            for f in range(NT):
                q = [pA, pB][f % 2]
                wq_ = q.begin_write()
                tpe = S.emit("tensor", lambda t, f_=f, q_=q: t.matmul(
                    q_.ap, h128_s[:], x0_tile(f_), start=True, stop=True),
                    waits=wq_ + x0_ready + [t_h128], inc="pe")
                q.set_w(tpe)
                tev = S.emit("vector", lambda v, f_=f, q_=q: v.tensor_copy(
                    x1b_tile(f_), q_.ap), waits=[tpe] + (gate_bigA if f < 2 else []),
                    inc="dve")
                q.add_r(tev)
                x1_toks.append(tev)
            cur_in, cur_out = x1b_tile, x0_tile
            stg_tok = x1_toks
            for s_i in range(4):
                newtoks = []
                for ho in range(2):
                    def bf_fn(v, s_=s_i, h_=ho, ci=cur_in, co=cur_out):
                        restv = (1 << s_) * 512
                        iv = ci(0, 0, 8192).rearrange("p (g t r) -> p g t r",
                                                      t=2, r=restv)
                        ov = co(0, 0, 8192).rearrange("p (g t r) -> p g t r",
                                                      t=2, r=restv)
                        if h_ == 0:
                            return v.tensor_add(ov[:, :, 0, :], iv[:, :, 0, :],
                                                iv[:, :, 1, :])
                        return v.tensor_sub(ov[:, :, 1, :], iv[:, :, 0, :],
                                            iv[:, :, 1, :])
                    tb = S.emit("vector", bf_fn,
                                waits=(stg_tok if ho == 0 else []),
                                inc="dve", force=("dve",))
                    newtoks.append(tb)
                stg_tok = newtoks
                cur_in, cur_out = cur_out, cur_in
            t_x1st = S.emit("sync", lambda sy: sy.dma_start(
                out=x1scr_d, in_=bigA[:, 24576:32768]),
                waits=stg_tok + [t_x1l_prev[0]], inc="cst")

            if p == 0:
                t_ropes = S.emit("sync", lambda sy: sy.dma_start(
                    out=ropes_f[:, 0:2048], in_=ropes_d[:, 0:2048]),
                    waits=[t_x1st], inc="cr")

                def cosv(c0, w):
                    return ropes_f[:, c0:c0 + w]

                def sinv(c0, w):
                    return ropes_f[:, 1024 + c0:1024 + c0 + w]
            else:
                t_ropes = S.emit("sync", lambda sy: sy.dma_start(
                    out=ropes_f[:, 0:1024], in_=ropes_d[:, 2048:3072]),
                    waits=[t_x1st], inc="cr")

                def cosv(c0, w):
                    return ropes_f[:, c0:c0 + w]

                def sinv(c0, w):
                    return ropes_f[:, 512 + c0:512 + c0 + w]

            if PHASES < 2:
                break
            own_c0 = 512 if p == 0 else 0
            ln_end = [T("act"), T("dve"), T("pe")]
            tlb = []
            for mi, (dst, c0) in enumerate([(lbq, 0), (lbk, 2048), (lbv, 4096)]):
                tt_ = S.emit("gpsimd", lambda g, d_=dst, c_=c0: g.dma_start(
                    out=d_, in_=loraB_d[:, c_:c_ + 2048]),
                    waits=ln_end if mi == 0 else [], inc="cl")
                tlb.append(tt_)
            t_lb = tlb[-1]

            s1_rot = [pA, pB, pC, pD]
            s1_jobs = [("q", s1q, 0, [(own_c0, 512, 0)]),
                       ("k", s1k, 256,
                        [(0, 512, 0), (512, 512, 512)] if p == 0 else [(0, 512, 0)]),
                       ("v", s1v, 512,
                        [(0, 512, 0), (512, 512, 512)] if p == 0 else [(0, 512, 0)])]
            s1_toks = {}
            ji = 0
            for name, dst, acol, groups in s1_jobs:
                for (hc0, w, dcol) in groups:
                    q = s1_rot[ji % 4]
                    ji += 1
                    wq_ = q.begin_write()
                    tpe = None
                    for d in range(NT):
                        tpe = S.emit("tensor", lambda t, d_=d, q_=q, a_=acol, h0=hc0,
                                     w_=w: t.matmul(
                                         q_.ap[0:16, 0:w_],
                                         loraA_s[:, a_ + d_ * 16: a_ + d_ * 16 + 16],
                                         h_tile(d_, h0, w_),
                                         start=(d_ == 0), stop=(d_ == 15)),
                                     waits=(wq_ + h_ready + [t_loraA])
                                     if d == 0 else [],
                                     inc="pe" if d == 15 else None)
                    q.set_w(tpe)
                    tev = S.emit("vector", lambda v, q_=q, d_=dst, c_=dcol, w_=w:
                                 v.tensor_scalar(d_[:, c_:c_ + w_],
                                                 q_.ap[0:16, 0:w_],
                                                 1.0 / 16.0, None, op0=ALU.mult),
                                 waits=[tpe] + (gate_par if ji <= 4 else []),
                                 inc="dve")
                    q.add_r(tev)
                    s1_toks.setdefault(name, []).append(tev)

            def rope_ops(buf_ap, cos_ap, sin_ap, t_ev):
                tr1 = S.emit("scalar", lambda s: s.mul(
                    rotb[0:64, :], buf_ap[64:128, :], -1.0),
                    waits=[t_ev, t_ropes] + rot_b.begin_write(),
                    inc="act", force=("act",))
                tr2 = S.emit("scalar", lambda s: s.copy(
                    rotb[64:128, :], buf_ap[0:64, :]), inc="act")
                rot_b.set_w(tr2)
                tm1 = S.emit("vector", lambda v: v.tensor_mul(buf_ap, buf_ap, cos_ap),
                             waits=[tr2], inc="dve")
                tm2 = S.emit("vector", lambda v: v.tensor_mul(rotb, rotb, sin_ap),
                             inc="dve", force=("dve",))
                tm3 = S.emit("vector", lambda v: v.tensor_add(buf_ap, buf_ap, rotb),
                             waits=[tm1, tm2], inc="dve", force=("dve",))
                rot_b.add_r(tm3)
                return tm3

            q_rope = [None] * 16
            k_rope = [[] for _ in range(16)]

            for mat_d, isQ in [(wqT_d, True), (wkT_d, False)]:
                for g in range(8):
                    sl, tw = wload(mat_d[g * 128:(g + 1) * 128, :])
                    for sub in range(2):
                        o = g * 2 + sub
                        if isQ:
                            segs = [(own_c0, 512, 0)]
                        else:
                            segs = ([(0, 512, 0), (512, 512, 512)]
                                    if p == 0 else [(0, 512, 0)])
                        for si, (hc0, w, kc0) in enumerate(segs):
                            q = [pA, pB][(o + si) % 2]
                            wq_ = q.begin_write()
                            tpe = None
                            for d in range(NT):
                                tpe = S.emit("tensor", lambda t, d_=d, q_=q, s_=sl,
                                             h0=hc0, w_=w, sub_=sub: t.matmul(
                                                 q_.ap[:, 0:w_],
                                                 wchunk(s_, d_, 256, sub_ * 128)[:, 0:128],
                                                 h_tile(d_, h0, w_),
                                                 start=(d_ == 0), stop=False),
                                             waits=(wq_ + [tw] + h_ready)
                                             if d == 0 else [], inc=None)
                            lb = lbq if isQ else lbk
                            s1buf = s1q if isQ else s1k
                            tpe = S.emit("tensor", lambda t, q_=q, o_=o, lb_=lb,
                                         s1_=s1buf, w_=w, kc=kc0: t.matmul(
                                             q_.ap[:, 0:w_],
                                             lb_[:, o_ * 128:(o_ + 1) * 128],
                                             s1_[:, kc:kc + w_],
                                             start=False, stop=True),
                                         waits=[t_lb] + s1_toks["q" if isQ else "k"],
                                         inc="pe")
                            q.set_w(tpe)
                            wslot[sl].add_r(tpe)
                            if isQ:
                                dst = q_tile(o)
                                tev = S.emit("vector", lambda v, d_=dst, q_=q:
                                             v.tensor_copy(d_, q_.ap),
                                             waits=[tpe, t_x1st], inc="dve")
                                q.add_r(tev)
                                q_rope[o] = rope_ops(dst, cosv(own_c0, 512),
                                                     sinv(own_c0, 512), tev)
                            else:
                                dst = kfm[:, o * 1024 + kc0: o * 1024 + kc0 + w]
                                war_gate = ([phase_end["attn_pe"]]
                                            if p == 1 else [])
                                tev = S.emit("vector", lambda v, d_=dst, q_=q:
                                             v.tensor_copy(d_, q_.ap),
                                             waits=[tpe] + war_gate, inc="dve")
                                q.add_r(tev)
                                k_rope[o].append(rope_ops(dst, cosv(kc0, w),
                                                          sinv(kc0, w), tev))

            nchunks = 8 if p == 0 else 4
            vp = [pA, pB, pC, pD]
            v_evac = {}
            vj = 0
            for g in range(8):
                sl, tw = wload(wvT_d[g * 128:(g + 1) * 128, :])
                for c in range(nchunks):
                    q = vp[vj % 4]
                    vj += 1
                    wq_ = q.begin_write()
                    col = 0
                    tpe = None
                    for d in range(NT):
                        tpe = S.emit("tensor", lambda t, d_=d, q_=q, s_=sl, c_=c,
                                     col_=col: t.matmul(
                                         q_.ap[:, col_:col_ + 256],
                                         h_tile(d_, c_ * 128, 128),
                                         wchunk(s_, d_, 256),
                                         start=(d_ == 0), stop=False),
                                     waits=(wq_ + [tw] + h_ready)
                                     if d == 0 else [], inc=None)
                    tpe = S.emit("tensor", lambda t, q_=q, g_=g, c_=c, col_=col:
                                 t.matmul(q_.ap[:, col_:col_ + 256],
                                          s1v[:, c_ * 128:(c_ + 1) * 128],
                                          lbv[:, g_ * 256:(g_ + 1) * 256],
                                          start=False, stop=True),
                                 waits=[t_lb] + s1_toks["v"], inc="pe")
                    q.set_w(tpe)
                    wslot[sl].add_r(tpe)
                    war_gate = [phase_end["attn_pe"]] if p == 1 else []
                    tev = S.emit("scalar", lambda s, q_=q, c_=c, g_=g, col_=col:
                                 s.copy(vtm[:, c_ * 2048 + g_ * 256:
                                            c_ * 2048 + (g_ + 1) * 256],
                                        q_.ap[:, col_:col_ + 256]),
                                 waits=[tpe] + war_gate, inc="act")
                    q.add_r(tev)
                    v_evac.setdefault(c, []).append(tev)

            qkv_pe_end = T("pe")
            if PHASES < 3:
                break

            t_masks = S.emit("sync", lambda sy: sy.dma_start(out=masks_s, in_=masks_d),
                             waits=[qkv_pe_end, T("dve"), T("act")], inc="cm")

            def kcols_for(qt):
                if p == 0:
                    sa = [(0, 512, qt * 128)]
                    sb = (512, 128, qt * 128 + 512)
                else:
                    jb = 512 - qt * 128
                    sa = [(0, jb, 512 + qt * 128)]
                    if qt > 0:
                        sa.append((jb, qt * 128, 0))
                    sb = (512, 128, qt * 128)
                return sa, sb

            def vchunk_for(qt, j5):
                if p == 0:
                    return qt + j5
                r = qt + j5
                return 4 + r if r < 4 else r - 4

            pA.r += pB.r
            pB.r = []
            pC.r += pD.r
            pD.r = []
            s_p = [pA, pC]
            av_p = [pE, pF]
            attn_done = []
            heads = [(qt, hh) for qt in range(4) for hh in range(16)]
            NH = len(heads)
            htok = {}

            def em_scores(gi):
                qt, hh = heads[gi]
                pi = gi % 2
                sa, sb = kcols_for(qt)
                qa = s_p[pi]
                wq_ = qa.begin_write()
                segs = sa + [sb]
                tsc = None
                for ii, (j0, w, kc) in enumerate(segs):
                    tsc = S.emit("tensor", lambda t, q_=qa, j_=j0, w_=w, kc_=kc,
                                 h_=hh, qt_=qt: t.matmul(
                                     q_.full[:, j_:j_ + w_],
                                     q_tile(h_, qt_ * 128, 128),
                                     kfm[:, h_ * 1024 + kc_: h_ * 1024 + kc_ + w_],
                                     start=True, stop=True),
                                 waits=(wq_ + [q_rope[hh]] + k_rope[hh])
                                 if ii == 0 else [],
                                 inc="pe" if ii == len(segs) - 1 else None)
                qa.set_w(tsc)
                htok[(gi, "s")] = tsc

            def em_softmax(gi):
                qt, hh = heads[gi]
                pi = gi % 2
                qa = s_p[pi]
                sm_b = smax_par[pi]
                e_b = eb[pi]
                wsm = sm_b.begin_write()
                mc = qt * 640 if p == 0 else 2560
                tmadd = S.emit("vector", lambda v, q_=qa, mc_=mc: v.tensor_add(
                    q_.full[:, 0:640], q_.full[:, 0:640], masks_s[:, mc_:mc_ + 640]),
                    waits=[htok[(gi, "s")], t_masks], inc="dve", force=("dve",))
                tm1 = S.emit("vector", lambda v, q_=qa, pp=pi: v.reduce_max(
                    smax_s[:, pp * 4: pp * 4 + 1], q_.full[:, 0:640], axis=AX.X,
                    negate=True), waits=[tmadd] + wsm, inc="dve", force=("dve",))
                we_ = e_b.begin_write()
                te1 = S.emit("scalar", lambda s, q_=qa, pp=pi: s.activation(
                    ebuf(pp)[:, 0:640], q_.full[:, 0:640], AF.Exp,
                    bias=smax_s[:, pp * 4: pp * 4 + 1],
                    accum_out=smax_s[:, pp * 4 + 2: pp * 4 + 3]),
                    waits=[tm1] + we_, inc="act")
                qa.add_r(te1)
                sm_b.add_r(te1)
                tri = S.emit("vector", lambda v, pp=pi: v.reciprocal(
                    smax_s[:, pp * 4 + 3: pp * 4 + 4],
                    smax_s[:, pp * 4 + 2: pp * 4 + 3]),
                    waits=[te1], inc="dve", force=("dve",))
                tnm = S.emit("vector", lambda v, pp=pi: v.tensor_scalar(
                    ebuf(pp)[:, 0:640], ebuf(pp)[:, 0:640],
                    smax_s[:, pp * 4 + 3: pp * 4 + 4], None, op0=ALU.mult),
                    waits=[tri], inc="dve", force=("dve",))
                e_b.set_w(tnm)
                sm_b.add_r(tnm)
                htok[(gi, "sm")] = tnm

            def em_transpose(gi):
                pi = gi % 2
                e_b, at_b = eb[pi], atb[pi]
                qx = [pT0, pT1][pi]
                wq_ = qx.begin_write()
                tpe = None
                for k in range(5):
                    tpe = S.emit("tensor", lambda t, q_=qx, k_=k, pp=pi: t.transpose(
                        q_.ap[:, k_ * 128:(k_ + 1) * 128],
                        ebuf(pp)[:, k_ * 128:(k_ + 1) * 128], identb_s[:]),
                        waits=([htok[(gi, "sm")]] + wq_) if k == 0 else [],
                        inc="pe" if k == 4 else None)
                qx.set_w(tpe)
                e_b.add_r(tpe)
                wat = at_b.begin_write()
                tat = S.emit("scalar", lambda s, q_=qx, pp=pi: s.copy(
                    atbuf(pp)[:, 0:640], q_.ap[:, 0:640]),
                    waits=[tpe] + wat, inc="act")
                qx.add_r(tat)
                at_b.set_w(tat)
                htok[(gi, "at")] = tat

            def em_av(gi):
                qt, hh = heads[gi]
                pi = gi % 2
                qv = av_p[pi]
                at_b = atb[pi]
                wq_ = qv.begin_write()
                tpe = None
                for j5 in range(5):
                    c = vchunk_for(qt, j5)
                    tpe = S.emit("tensor", lambda t, q_=qv, j_=j5, c_=c, h_=hh,
                                 pp=pi: t.matmul(
                                     q_.ap[:, 0:128],
                                     vtm[:, c_ * 2048 + h_ * 128:
                                         c_ * 2048 + (h_ + 1) * 128],
                                     atbuf(pp)[:, j_ * 128:(j_ + 1) * 128],
                                     start=(j_ == 0), stop=(j_ == 4)),
                                 waits=(wq_ + [htok[(gi, "at")]] +
                                        v_evac.get(c, [])) if j5 == 0 else [],
                                 inc="pe" if j5 == 4 else None)
                qv.set_w(tpe)
                at_b.add_r(tpe)
                tev = S.emit("scalar", lambda s, q_=qv, h_=hh, qt_=qt: s.copy(
                    ao_tile(h_, qt_ * 128, 128), q_.ap[:, 0:128]),
                    waits=[tpe, t_x1st, t_ropes], inc="act")
                qv.add_r(tev)
                attn_done.append(tev)

            for gi in range(NH + 2):
                if HEADS_ONLY is not None and gi < NH and gi not in HEADS_ONLY:
                    continue
                if gi < NH:
                    em_scores(gi)
                    em_softmax(gi)
                if ATTN_SUB >= 2 and 1 <= gi <= NH:
                    em_transpose(gi - 1)
                if ATTN_SUB >= 3 and gi >= 2:
                    em_av(gi - 2)
                if ATTN_BARRIER:
                    toks = [T("pe"), T("dve"), T("act")]
                    for e_ in ("tensor", "vector", "scalar"):
                        S.emit(e_, None, waits=toks, force=("pe", "dve", "act"))

            phase_end["attn_pe"] = T("pe")
            if PHASES < 4:
                break

            t_oB = S.emit("gpsimd", lambda g: g.dma_start(
                out=oBt, in_=loraB_d[:, 6144:8192]), waits=[qkv_pe_end], inc="cl")
            q = pC
            wq_ = q.begin_write()
            tpe = None
            for d in range(NT):
                tpe = S.emit("tensor", lambda t, d_=d, q_=q: t.matmul(
                    q_.ap[0:16, 0:512], loraA_s[:, 768 + d_ * 16: 768 + d_ * 16 + 16],
                    ao_tile(d_), start=(d_ == 0), stop=(d_ == 15)),
                    waits=(wq_ + attn_done) if d == 0 else [],
                    inc="pe" if d == 15 else None)
            q.set_w(tpe)
            t_s1o = S.emit("vector", lambda v, q_=q: v.tensor_scalar(
                s1o[:, 0:512], q_.ap[0:16, 0:512], 1.0 / 16.0, None, op0=ALU.mult),
                waits=[tpe], inc="dve")
            q.add_r(t_s1o)

            def emit_oevac(q, tpe, c):
                osb = ost_b2[c % 2]
                wo_ = osb.begin_write()
                tev = S.emit("scalar", lambda s, q_=q, pp=c % 2: s.copy(
                    ostage_p(pp), q_.ap), waits=[tpe] + wo_, inc="act")
                q.add_r(tev)
                osb.set_w(tev)
                last_add = None
                for tt in range(4):
                    qtr = [pE, pF][tt % 2]
                    wq2 = qtr.begin_write()
                    tpt = S.emit("tensor", lambda t, q_=qtr, t_=tt, pp=c % 2:
                                 t.transpose(
                                     q_.ap, ostage_p(pp)[:, t_ * 128:(t_ + 1) * 128],
                                     identf_s[:]),
                                 waits=[tev, t_idf] + wq2, inc="pe")
                    qtr.set_w(tpt)
                    tadd = S.emit("vector", lambda v, q_=qtr, t_=tt, c_=c:
                                  v.tensor_add(
                                      acc[:, t_ * 2048 + c_ * 128:
                                          t_ * 2048 + (c_ + 1) * 128],
                                      acc[:, t_ * 2048 + c_ * 128:
                                          t_ * 2048 + (c_ + 1) * 128],
                                      q_.ap),
                                  waits=[tpt, acc_last[p][tt]], inc="dve")
                    qtr.add_r(tadd)
                    acc_last[p][tt] = tadd
                    last_add = tadd
                osb.add_r(last_add)

            o_pending = None
            for g in range(8):
                sl, tw = wload(woT_d[g * 128:(g + 1) * 128, :])
                for sub in range(2):
                    c = g * 2 + sub
                    q = [pA, pB][c % 2]
                    wq_ = q.begin_write()
                    tpe = None
                    for d in range(NT):
                        tpe = S.emit("tensor", lambda t, d_=d, q_=q, s_=sl,
                                     sub_=sub: t.matmul(
                                         q_.ap,
                                         wchunk(s_, d_, 256, sub_ * 128)[:, 0:128],
                                         ao_tile(d_), start=(d_ == 0), stop=False),
                                     waits=(wq_ + [tw] + attn_done)
                                     if d == 0 else [], inc=None)
                    tpe = S.emit("tensor", lambda t, q_=q, c_=c: t.matmul(
                        q_.ap, oBt[:, c_ * 128:(c_ + 1) * 128], s1o[:, 0:512],
                        start=False, stop=True), waits=[t_oB, t_s1o], inc="pe")
                    q.set_w(tpe)
                    wslot[sl].add_r(tpe)
                    if o_pending is not None:
                        emit_oevac(*o_pending)
                    o_pending = (q, tpe, c)
            emit_oevac(*o_pending)

            phase_end["oproj_pe"] = T("pe")
            phase_end["oproj_dve"] = T("dve")
            phase_end["oproj_act"] = T("act")
            if PHASES < 5:
                break

            t_x1l = S.emit("sync", lambda sy: sy.dma_start(out=x1, in_=x1scr_d),
                           waits=[t_x1st, phase_end["oproj_pe"],
                                  phase_end["oproj_dve"], phase_end["oproj_act"]],
                           inc="x1l")
            wring[0] = 4
            wload_i[0] = wload_i[0] - (wload_i[0] % 2)
            vtm_gate = [phase_end["attn_pe"]]
            t_x1l_prev[0] = t_x1l
            z_toks = [None] * 64
            zh_tok = [None] * 64
            slot_ap = [z_tile(i) for i in range(64)]
            bspare = {"vector": spare_bf_ap, "gpsimd": spare_bf_ap2}
            bchain = {"vector": None, "gpsimd": None}
            bcnt = [0]
            pend_bf = []

            def emit_h128(mh):
                qh = [pE, pF][mh % 2]
                wqh = qh.begin_write()
                tph = S.emit("tensor", lambda t, m_=mh, q_=qh: t.matmul(
                    q_.full, h128_s[:], z_tile(m_), start=True, stop=True),
                    waits=wqh + [z_toks[mh]], inc="pe")
                qh.set_w(tph)
                tze = S.emit("scalar", lambda s, m_=mh, q_=qh: s.copy(
                    z_tile(m_), q_.full), waits=[tph], inc="act")
                qh.add_r(tze)
                zh_tok[mh] = tze
                pend_bf.append(mh)

            ready_pairs = []

            def queue_butterflies(mt):
                for s_i in range(6):
                    blk = 1 << (s_i + 1)
                    if (mt + 1) % blk != 0:
                        break
                    base = mt + 1 - blk
                    half = 1 << s_i
                    for k in range(half):
                        ready_pairs.append((base + k, base + k + half))

            def emit_pairs(maxn):
                n = 0
                while ready_pairs and n < maxn:
                    ia, ib = ready_pairs.pop(0)
                    eng = "vector" if bcnt[0] % 2 == 0 else "gpsimd"
                    lbl = "dve" if eng == "vector" else "gp"
                    bcnt[0] += 1
                    a_ap = slot_ap[ia]
                    b_ap = slot_ap[ib]
                    sp_ap = bspare[eng]
                    tsub = S.emit(eng,
                                  lambda v, a=a_ap, b=b_ap, sp=sp_ap:
                                  v.tensor_sub(sp, a, b),
                                  waits=[zh_tok[ia], zh_tok[ib], bchain[eng]],
                                  inc=lbl, force=("dve", "gp"))
                    tadd = S.emit(eng,
                                  lambda v, a=a_ap, b=b_ap:
                                  v.tensor_add(a, a, b),
                                  waits=[tsub], inc=lbl, force=("dve", "gp"))
                    slot_ap[ib] = sp_ap
                    bspare[eng] = b_ap
                    zh_tok[ia] = tadd
                    zh_tok[ib] = tsub
                    bchain[eng] = tadd
                    n += 1
            for mg in range(32):
                sl_u, tw_u = wload(wupT_d[mg * 128:(mg + 1) * 128, :],
                                   extra=(vtm_gate if mg < 2 else ()))
                sl_g, tw_g = wload(wgT_d[mg * 128:(mg + 1) * 128, :],
                                   extra=(vtm_gate if mg < 2 else ()))
                for sub in range(2):
                    mt = mg * 2 + sub
                    pu = [pA, pC][mt % 2]
                    pg = [pB, pD][mt % 2]
                    wq_ = pu.begin_write()
                    tpe = None
                    for d in range(NT):
                        tpe = S.emit("tensor", lambda t, d_=d, q_=pu, s_=sl_u,
                                     sub_=sub: t.matmul(
                                         q_.ap,
                                         wchunk(s_, d_, 256, sub_ * 128)[:, 0:128],
                                         x1_tile(d_), start=(d_ == 0),
                                         stop=(d_ == 15)),
                                     waits=(wq_ + [tw_u, t_x1l])
                                     if d == 0 else [],
                                     inc="pe" if d == 15 else None)
                    pu.set_w(tpe)
                    wslot[sl_u].add_r(tpe)
                    wu_ = ub[mt % 2].begin_write()
                    tu = S.emit("vector", lambda v, q_=pu, m_=mt, pp=mt % 2:
                                v.tensor_scalar(ubuf(pp), q_.ap,
                                                wscup_su[:, m_:m_ + 1], None,
                                                op0=ALU.mult),
                                waits=[tpe, t_dv] + wu_, inc="dve")
                    pu.add_r(tu)
                    ub[mt % 2].set_w(tu)
                    wq_ = pg.begin_write()
                    tpe = None
                    for d in range(NT):
                        tpe = S.emit("tensor", lambda t, d_=d, q_=pg, s_=sl_g,
                                     sub_=sub: t.matmul(
                                         q_.ap,
                                         wchunk(s_, d_, 256, sub_ * 128)[:, 0:128],
                                         x1_tile(d_), start=(d_ == 0),
                                         stop=(d_ == 15)),
                                     waits=(wq_ + [tw_g, t_x1l])
                                     if d == 0 else [],
                                     inc="pe" if d == 15 else None)
                    pg.set_w(tpe)
                    wslot[sl_g].add_r(tpe)
                    wg_ = gb[mt % 2].begin_write()
                    tg = S.emit("scalar", lambda s, q_=pg, m_=mt, pp=mt % 2:
                                s.activation(gbuf(pp), q_.ap, AF.Silu,
                                             scale=wscg_v[:, m_:m_ + 1]),
                                waits=[tpe] + wg_, inc="act")
                    pg.add_r(tg)
                    gb[mt % 2].set_w(tg)
                    zgate = ([phase_end["oproj_pe"]] if mt >= 32
                             else [qkv_pe_end])
                    tz = S.emit("vector", lambda v, m_=mt, pp=mt % 2: v.tensor_mul(
                        z_tile(m_), gbuf(pp), ubuf(pp)),
                        waits=[tu, tg] + zgate, inc="dve", force=("dve",))
                    ub[mt % 2].add_r(tz)
                    gb[mt % 2].add_r(tz)
                    z_toks[mt] = tz
                    if mt >= 1:
                        emit_h128(mt - 1)
                    if mt == 63:
                        emit_h128(63)
                    for mtq in list(pend_bf):
                        if zh_tok[mtq] is None:
                            break
                        pend_bf.remove(mtq)
                        queue_butterflies(mtq)
                    emit_pairs(4)
            emit_pairs(10 ** 9)
            phase_end["ug_pe"] = T("pe")
            for sp in (spare_bf_ap, spare_bf_ap2):
                for jx in range(64):
                    if slot_ap[jx] is sp:
                        free_slot = (bspare["vector"]
                                     if bspare["vector"] is not sp and
                                     bspare["vector"] is not spare_bf_ap and
                                     bspare["vector"] is not spare_bf_ap2
                                     else bspare["gpsimd"])
                        assert free_slot is not spare_bf_ap and free_slot is not spare_bf_ap2
                        tmv = S.emit("vector", lambda v, d_=free_slot, s_=sp:
                                     v.tensor_copy(d_, s_),
                                     waits=[zh_tok[jx], bchain["vector"],
                                            bchain["gpsimd"]],
                                     inc="dve", force=("dve",))
                        if bspare["vector"] is free_slot:
                            bspare["vector"] = sp
                        else:
                            bspare["gpsimd"] = sp
                        slot_ap[jx] = free_slot
                        zh_tok[jx] = tmv
                        bchain["vector"] = tmv
                        break
            zh_fin = zh_tok

            for cs in range(8):
                pair = [pA, pB] if cs % 2 == 0 else [pC, pD]

                def dload(li, cs_=cs):
                    return wload(wdT_d[(cs_ * 4 + li) * 128:
                                       (cs_ * 4 + li + 1) * 128, :], 4096)
                loads = [dload(0), dload(1), dload(2)]
                wq_ = pair[0].begin_write() + pair[1].begin_write()
                tpe = None
                for j in range(64):
                    sl, tw = loads[j // 16]
                    jj = j % 16
                    for ct in range(2):
                        q = pair[ct]
                        rhs_ap = slot_ap[j]
                        tpe = S.emit("tensor", lambda t, q_=q, s_=sl, jj_=jj,
                                     ct_=ct, j_=j, r_=rhs_ap: t.matmul(
                                         q_.ap,
                                         wslot_ap(s_, jj_ * 256 + ct_ * 128,
                                                  jj_ * 256 + ct_ * 128 + 128),
                                         r_,
                                         start=(j_ == 0), stop=(j_ == 63)),
                                     waits=((wq_ + [tw, zh_fin[j]])
                                            if (j == 0 and ct == 0) else
                                            ([tw, zh_fin[j]] if ct == 0 else [])),
                                     inc="pe" if (jj == 15 and ct == 1) else None)
                        if ct == 1 and jj == 15:
                            wslot[sl].add_r(tpe)
                    if jj == 15 and j // 16 + 3 < 4:
                        loads.append(dload(j // 16 + 3))
                for ct in range(2):
                    pair[ct].set_w(tpe)
                for ct in range(2):
                    c = cs * 2 + ct
                    q = pair[ct]
                    osb = ost_b2[c % 2]
                    wd_ = osb.begin_write()
                    tev = S.emit("scalar", lambda s, q_=q, c_=c, pp=c % 2: s.mul(
                        ostage_p(pp), q_.ap, wscdp_v[:, c_:c_ + 1]),
                        waits=[tpe] + wd_, inc="act")
                    q.add_r(tev)
                    osb.set_w(tev)
                    last_add = None
                    for tt in range(4):
                        qtr = [pE, pF][tt % 2]
                        wq2 = qtr.begin_write()
                        tpt = S.emit("tensor", lambda t, q_=qtr, t_=tt, pp=c % 2:
                                     t.transpose(
                                         q_.ap, ostage_p(pp)[:, t_ * 128:(t_ + 1) * 128],
                                         identf_s[:]),
                                     waits=[tev, t_idf] + wq2, inc="pe")
                        qtr.set_w(tpt)
                        tadd = S.emit("vector", lambda v, q_=qtr, t_=tt, c_=c:
                                      v.tensor_add(
                                          acc[:, t_ * 2048 + c_ * 128:
                                              t_ * 2048 + (c_ + 1) * 128],
                                          acc[:, t_ * 2048 + c_ * 128:
                                              t_ * 2048 + (c_ + 1) * 128],
                                          q_.ap),
                                      waits=[tpt, acc_last[p][tt]], inc="dve")
                        qtr.add_r(tadd)
                        acc_last[p][tt] = tadd
                        last_add = tadd
                    osb.add_r(last_add)

            phase_end["ffn_pe"] = T("pe")
            phase_end["ffn_dve"] = T("dve")
            phase_end["ffn_act"] = T("act")
            wring[0] = 2
            wload_i[0] = wload_i[0] - (wload_i[0] % 2)

            if PHASES < 6 and p == 0:
                for tt in range(4):
                    store_toks[p][tt] = S.emit("sync", lambda sy, t_=tt, p_=p:
                                               sy.dma_start(
                                                   out=out_d[p_ * 512 + t_ * 128:
                                                             p_ * 512 + (t_ + 1) * 128, :],
                                                   in_=acc[:, t_ * 2048:(t_ + 1) * 2048]),
                                               waits=[acc_last[p][tt]], inc="cst")
                break
            for tt in range(4):
                store_toks[p][tt] = S.emit("sync", lambda sy, t_=tt, p_=p:
                                           sy.dma_start(
                                               out=out_d[p_ * 512 + t_ * 128:
                                                         p_ * 512 + (t_ + 1) * 128, :],
                                               in_=acc[:, t_ * 2048:(t_ + 1) * 2048]),
                                           waits=[acc_last[p][tt]], inc="cst")

        tap_map = {
            "h": bigA[:, 0:16384], "qreg": bigA[:, 16384:24576],
            "ao": bigA[:, 24576:32768], "kfm": kfm, "vtm": vtm,
            "x1": x1, "z": bigA, "acc": acc, "x1scr": x1scr_d, "par": par,
        }
        for name, shape, dt in DEBUG_SPEC:
            src = tap_map[name]
            S.emit("sync", lambda sy, s_=src, d_=dbg_d[name]: sy.dma_start(
                out=d_, in_=s_),
                waits=[T("pe"), T("dve"), T("act"), T("gp")], inc="cst")

        S.emit("sync", None, waits=[T("cst")])

        def replay(eng_name):
            def f(eng):
                for (waits, fn, inc, amt) in S.ops[eng_name]:
                    for (lbl, val) in waits:
                        eng.wait_ge(sems[lbl], val)
                    if fn is None:
                        continue
                    inst = fn(eng)
                    if inc is not None:
                        inst.then_inc(sems[inc], amt)
            return f

        block.sync(replay("sync"))
        block.gpsimd(replay("gpsimd"))
        block.tensor(replay("tensor"))
        block.vector(replay("vector"))
        block.scalar(replay("scalar"))

    return nc



def _tile_proj(wT, ng, gw):
    d_in, n_out = wT.shape
    nc_ = d_in // 128
    return np.ascontiguousarray(
        wT.reshape(nc_, 128, ng, gw).transpose(2, 1, 0, 3).reshape(ng * 128, nc_ * gw))


def _tile_down(wT):
    arr = wT.reshape(4, 16, 128, 8, 256)
    return np.ascontiguousarray(
        arr.transpose(3, 0, 2, 1, 4).reshape(4 * 8 * 128, 16 * 256))


def make_in_maps(inputs):
    x = np.ascontiguousarray(np.asarray(inputs["L0_attn_x"], np.float32))
    wqT = _tile_proj(np.asarray(inputs["L0_attn_wq"], np.float32).T, 8, 256)
    wkT = _tile_proj(np.asarray(inputs["L0_attn_wk"], np.float32).T, 8, 256)
    wvT = _tile_proj(np.asarray(inputs["L0_attn_wv"], np.float32).T, 8, 256)
    woT = _tile_proj(np.asarray(inputs["L0_attn_wo"], np.float32).T, 8, 256)
    wupT = _tile_proj(np.asarray(inputs["L0_ffn_W_up"], np.float32).T, 32, 256)
    wgT = _tile_proj(np.asarray(inputs["L0_ffn_W_gate"], np.float32).T, 32, 256)
    wdT = _tile_down(np.asarray(inputs["L0_ffn_W_down"], np.float32).T)
    loraA = _pack_loraA(inputs["L0_attn_qA"], inputs["L0_attn_kA"],
                        inputs["L0_attn_vA"], inputs["L0_attn_oA"])
    loraB = _pack_loraB(inputs["L0_attn_qB"], inputs["L0_attn_kB"],
                        inputs["L0_attn_vB"], inputs["L0_attn_oB"])
    wsc = np.asarray(inputs["L0_ffn_Wscale_ug"], np.float32)
    vecs = np.concatenate([
        _pack_cols(np.asarray(inputs["L0_attn_norm_w"], np.float32), 16),
        _pack_cols(np.asarray(inputs["L0_ffn_SU_ug"], np.float32), 16),
        _pack_cols(wsc[:8192], 64),
        _pack_cols(wsc[8192:], 64),
        _pack_cols(np.asarray(inputs["L0_ffn_SU_dp"], np.float32), 64),
        _pack_cols(np.asarray(inputs["L0_ffn_Wscale_dp"], np.float32), 16),
    ], axis=1).astype(np.float32)
    h128 = _h128_bf16()
    identb = np.eye(128, dtype=BF_NP)
    identf = np.eye(128, dtype=np.float32)

    in_maps = []
    for c in range(NCORES):
        beta, gamma = c // 4, c % 4
        s = gamma * 1024
        xs = np.zeros((1536, D), np.float32)
        lo = s - 512
        src_lo = max(lo, 0)
        xs[src_lo - lo:, :] = x[beta, src_lo: s + 1024, :]
        in_maps.append({
            "xs": xs, "wqT": wqT, "wkT": wkT, "wvT": wvT, "woT": woT,
            "wupT": wupT, "wgT": wgT, "wdT": wdT,
            "loraA": loraA, "loraB": loraB,
            "masks": _masks_bf16(gamma), "ropes": _ropes_f32(gamma),
            "vecs": vecs, "h128": h128, "identb": identb, "identf": identf,
        })
    return in_maps


def kernel(**inputs):
    in_maps = make_in_maps(inputs)
    nc = build_nc()
    res = run_bass_kernel_spmd(nc, in_maps, core_ids=list(range(NCORES)))
    outs = [np.asarray(res.results[c]["out"], np.float32) for c in range(NCORES)]
    return np.stack([np.concatenate(outs[:4], axis=0),
                     np.concatenate(outs[4:], axis=0)])


# version 09
# speedup vs baseline: 1.6250x; 1.1198x over previous
import numpy as np
import ml_dtypes
import concourse.bass as bass
import concourse.mybir as mybir
from concourse.bass_utils import run_bass_kernel_spmd

F32 = mybir.dt.float32
BF16 = mybir.dt.bfloat16
AF = mybir.ActivationFunctionType
ALU = mybir.AluOpType
AX = mybir.AxisListType
BF_NP = ml_dtypes.bfloat16

D = 2048
NT = 16
TP = 512
NCORES = 8

OWNER = {
    "cx": "sync", "cst": "sync", "x1l": "sync", "cc": "sync", "cr": "sync",
    "cm": "sync", "ctr": "sync", "cts": "sync", "cx2": "sync",
    "cl": "gpsimd", "wa": "gpsimd", "wb": "gpsimd", "wc": "gpsimd",
    "wd": "gpsimd",
    "pe": "tensor", "dve": "vector", "act": "scalar", "gp": "gpsimd",
}
DMA_LBL = {"cx", "cx2", "cst", "x1l", "cl", "wa", "wb", "wc", "wd", "cc", "cr", "cm", "ctr", "cts"}

DEBUG_SPEC = []
PHASES = 99
ATTN_SUB = 3
TR_LIMIT = 9999
TR_EVAC = True
ATTN_BARRIER = False
HEADS_ONLY = None
SCORES_RAW = False


class Sch:
    ENGS = ("sync", "gpsimd", "tensor", "vector", "scalar")

    def __init__(self):
        self.ops = {e: [] for e in self.ENGS}
        self.cnt = {k: 0 for k in OWNER}

    def tok(self, lbl):
        return (lbl, self.cnt[lbl])

    def emit(self, eng, fn, waits=(), inc=None, force=()):
        wmax = {}
        for t in waits:
            if t is None:
                continue
            lbl, val = t[0], t[1]
            if val <= 0:
                continue
            if lbl in DMA_LBL:
                val = self.cnt[lbl]
                if val <= 0:
                    continue
            elif OWNER[lbl] == eng and lbl not in force:
                continue
            if wmax.get(lbl, 0) < val:
                wmax[lbl] = val
        amt = 0
        if inc is not None:
            amt = 16 if inc in DMA_LBL else 1
            self.cnt[inc] += amt
        self.ops[eng].append((sorted(wmax.items()), fn, inc, amt))
        return (inc, self.cnt[inc]) if inc is not None else None


class PBuf:
    def __init__(self):
        self.w = None
        self.r = []
        self.ap = None

    def begin_write(self):
        w = list(self.r)
        self.r = []
        return w

    def set_w(self, tok):
        self.w = tok

    def add_r(self, tok):
        if tok is not None:
            self.r.append(tok)



def _popcount_mat():
    i = np.arange(128)
    a = i[:, None] & i[None, :]
    pc = np.zeros_like(a)
    v = a.copy()
    while v.any():
        pc += v & 1
        v >>= 1
    return pc


def _h128_bf16():
    return np.where(_popcount_mat() % 2 == 0, 1.0, -1.0).astype(BF_NP)


def _masks_bf16(gamma):
    NEG = np.float32(-1e30)
    a = np.arange(128)[:, None]
    j = np.arange(512)[None, :]
    std = (j >= a + 1)
    b = np.arange(128)[None, :]
    caus = (b <= a)
    out = np.full((128, 3200), NEG, np.float32)
    for qt in range(4):
        m = std.copy()
        if gamma == 0:
            m &= (j >= 512 - qt * 128)
        out[:, qt * 640: qt * 640 + 512] = np.where(m, 0.0, NEG)
        out[:, qt * 640 + 512: qt * 640 + 640] = np.where(caus, 0.0, NEG)
    out[:, 2560:3072] = np.where(std, 0.0, NEG)
    out[:, 3072:3200] = np.where(caus, 0.0, NEG)
    return out.astype(BF_NP)


def _ropes_f32(gamma):
    half = 64
    theta = 1.0 / 10000.0 ** (np.arange(half, dtype=np.float64) / half)
    th = np.concatenate([theta, theta])[:, None]
    scale = 128.0 ** -0.25
    out = np.zeros((128, 3072), np.float32)
    g0 = gamma * 1024
    t0 = np.arange(g0 - 512, g0 + 512, dtype=np.float64)[None, :]
    c0 = np.cos(th * t0) * scale
    s0 = np.sin(th * t0) * scale
    neg = (t0[0] < 0)
    c0[:, neg] = 0.0
    s0[:, neg] = 0.0
    out[:, 0:1024] = c0
    out[:, 1024:2048] = s0
    t1 = np.arange(g0 + 512, g0 + 1024, dtype=np.float64)[None, :]
    out[:, 2048:2560] = np.cos(th * t1) * scale
    out[:, 2560:3072] = np.sin(th * t1) * scale
    return out


def _pack_cols(v, n):
    return np.ascontiguousarray(v.reshape(n, 128).T)


def _pack_loraA(qA, kA, vA, oA):
    outs = []
    for A in (qA, kA, vA, oA):
        AT = np.asarray(A, np.float32).T.reshape(16, 128, 16).transpose(1, 0, 2).reshape(128, 256)
        outs.append(AT)
    return np.ascontiguousarray(np.concatenate(outs, axis=1))


def _pack_loraB(qB, kB, vB, oB):
    return np.ascontiguousarray(np.concatenate(
        [np.asarray(m, np.float32).T for m in (qB, kB, vB, oB)], axis=1))



def build_nc():
    nc = bass.Bass("TRN2", target_bir_lowering=False)

    xs_d = nc.dram_tensor("xs", [1536, D], F32, kind="ExternalInput").ap()
    wqT_d = nc.dram_tensor("wqT", [1024, 4096], F32, kind="ExternalInput").ap()
    wkT_d = nc.dram_tensor("wkT", [1024, 4096], F32, kind="ExternalInput").ap()
    wvT_d = nc.dram_tensor("wvT", [1024, 4096], F32, kind="ExternalInput").ap()
    woT_d = nc.dram_tensor("woT", [1024, 4096], F32, kind="ExternalInput").ap()
    wupT_d = nc.dram_tensor("wupT", [4096, 4096], F32, kind="ExternalInput").ap()
    wgT_d = nc.dram_tensor("wgT", [4096, 4096], F32, kind="ExternalInput").ap()
    wdT_d = nc.dram_tensor("wdT", [4096, 4096], F32, kind="ExternalInput").ap()
    loraA_d = nc.dram_tensor("loraA", [128, 1024], F32, kind="ExternalInput").ap()
    loraB_d = nc.dram_tensor("loraB", [16, 8192], F32, kind="ExternalInput").ap()
    masks_d = nc.dram_tensor("masks", [128, 3200], BF16, kind="ExternalInput").ap()
    ropes_d = nc.dram_tensor("ropes", [128, 3072], F32, kind="ExternalInput").ap()
    vecs_d = nc.dram_tensor("vecs", [128, 240], F32, kind="ExternalInput").ap()
    h128_d = nc.dram_tensor("h128", [128, 128], BF16, kind="ExternalInput").ap()
    identb_d = nc.dram_tensor("identb", [128, 128], BF16, kind="ExternalInput").ap()
    identf_d = nc.dram_tensor("identf", [128, 128], F32, kind="ExternalInput").ap()
    out_d = nc.dram_tensor("out", [1024, D], F32, kind="ExternalOutput").ap()
    x1scr_d = nc.dram_tensor("x1scr", [128, 8192], BF16, kind="Internal").ap()
    dbg_d = {}
    for name, shape, dt in DEBUG_SPEC:
        dbg_d[name] = nc.dram_tensor("dbg_" + name, list(shape), dt,
                                     kind="ExternalOutput").ap()

    S = Sch()
    T = S.tok

    from contextlib import ExitStack
    es = ExitStack()
    with es:
        acc_t = es.enter_context(nc.sbuf_tensor("acc", [128, 8192], F32))
        kfm_t = es.enter_context(nc.sbuf_tensor("kfm", [128, 16384], BF16))
        vtm_t = es.enter_context(nc.sbuf_tensor("vtm", [128, 16384], BF16))
        bigA_t = es.enter_context(nc.sbuf_tensor("bigA", [128, 32768], BF16))
        par_t = es.enter_context(nc.sbuf_tensor("parena", [128, 12288], BF16))
        war_t = es.enter_context(nc.sbuf_tensor("warena", [128, 8192], BF16))
        loraA_s = es.enter_context(nc.sbuf_tensor("loraA_s", [128, 1024], BF16))
        h128_s = es.enter_context(nc.sbuf_tensor("h128_s", [128, 128], BF16))
        identb_s = es.enter_context(nc.sbuf_tensor("identb_s", [128, 128], BF16))
        identf_s = es.enter_context(nc.sbuf_tensor("identf_s", [128, 128], F32))
        vecs_s = es.enter_context(nc.sbuf_tensor("vecs_s", [128, 240], F32))
        dvecs_s = es.enter_context(nc.sbuf_tensor("dvecs", [128, 144], F32))
        stats_s = es.enter_context(nc.sbuf_tensor("stats", [128, 32], F32))
        smax_s = es.enter_context(nc.sbuf_tensor("smax", [128, 8], F32))
        pAB_t = es.enter_context(nc.psum_tensor("pAB", [128, 1024], F32))
        pCD_t = es.enter_context(nc.psum_tensor("pCD", [128, 1024], F32))
        pT_t = es.enter_context(nc.psum_tensor("pT", [128, 2048], BF16))
        pE_t = es.enter_context(nc.psum_tensor("pE", [128, 512], F32))
        pF_t = es.enter_context(nc.psum_tensor("pF", [128, 512], F32))
        s_cx = es.enter_context(nc.semaphore("s_cx"))
        s_cc = es.enter_context(nc.semaphore("s_cc"))
        s_cr = es.enter_context(nc.semaphore("s_cr"))
        s_cm = es.enter_context(nc.semaphore("s_cm"))
        s_ctr = es.enter_context(nc.semaphore("s_ctr"))
        s_cts = es.enter_context(nc.semaphore("s_cts"))
        s_cx2 = es.enter_context(nc.semaphore("s_cx2"))
        s_cst = es.enter_context(nc.semaphore("s_cst"))
        s_x1l = es.enter_context(nc.semaphore("s_x1l"))
        s_cl = es.enter_context(nc.semaphore("s_cl"))
        s_wa = es.enter_context(nc.semaphore("s_wa"))
        s_wb = es.enter_context(nc.semaphore("s_wb"))
        s_wc = es.enter_context(nc.semaphore("s_wc"))
        s_wd = es.enter_context(nc.semaphore("s_wd"))
        s_pe = es.enter_context(nc.semaphore("s_pe"))
        s_dve = es.enter_context(nc.semaphore("s_dve"))
        s_act = es.enter_context(nc.semaphore("s_act"))
        s_gp = es.enter_context(nc.semaphore("s_gp"))
        block = es.enter_context(nc.Block())

        sems = {"cx": s_cx, "cst": s_cst, "x1l": s_x1l, "cl": s_cl, "wa": s_wa,
                "wb": s_wb, "wc": s_wc, "wd": s_wd, "pe": s_pe, "dve": s_dve, "act": s_act, "gp": s_gp,
                "cc": s_cc, "cr": s_cr, "cm": s_cm, "ctr": s_ctr, "cts": s_cts, "cx2": s_cx2}

        acc = acc_t[:]
        kfm = kfm_t[:]
        vtm = vtm_t[:]
        bigA = bigA_t[:]
        par = par_t[:]
        war = war_t[:]

        def h_tile(f, c0, w):
            return bigA[:, f * 1024 + c0: f * 1024 + c0 + w]

        def q_tile(o, c0=0, w=TP):
            return bigA[:, 16384 + o * 512 + c0: 16384 + o * 512 + c0 + w]

        def ao_tile(o, c0=0, w=TP):
            return bigA[:, 24576 + o * 512 + c0: 24576 + o * 512 + c0 + w]

        def z_tile(m, c0=0, w=TP):
            return bigA[:, m * 512 + c0: m * 512 + c0 + w]

        ropes_f = bigA[:, 24576:32768].bitcast(F32)
        x0_tile = q_tile
        x1b_tile = ao_tile

        xstage0 = par[:, 0:4096].bitcast(F32)
        xstage1 = vtm[:, 0:4096].bitcast(F32)

        def xstage_p(pp):
            return xstage0 if pp == 0 else xstage1

        def xn_buf(p):
            return par[:, 4096 + p * 2048: 6144 + p * 2048]

        def hst_buf(p):
            return par[:, 8192 + p * 2048: 10240 + p * 2048]

        lbq = par[0:16, 0:2048]
        lbk = par[0:16, 2048:4096]
        lbv = par[0:16, 4096:6144]
        s1q = par[0:16, 6144:6656]
        s1k = par[0:16, 6656:7680]
        s1v = par[0:16, 7680:8704]
        rotb = par[:, 8704:9216]
        masks_s = par[:, 0:3200]

        def ebuf(p):
            return par[:, 3200 + p * 640: 3840 + p * 640]

        def atbuf(p):
            return par[:, 4480 + p * 640: 5120 + p * 640]

        s1o = par[0:16, 5760:6272]
        oBt = par[0:16, 6272:8320]
        def ostage_p(pp):
            return par[:, 10240 + pp * 1024: 11264 + pp * 1024].bitcast(F32)
        x1 = par[:, 0:8192]

        def x1_tile(f):
            return par[:, f * 512:(f + 1) * 512]

        def ubuf(p):
            return par[:, 8192 + p * 512: 8704 + p * 512]

        def gbuf(p):
            return par[:, 9216 + p * 512: 9728 + p * 512]

        spare_bf_ap = par[:, 10240:10752]
        spare_bf_ap2 = par[:, 10752:11264]

        normw_v = vecs_s[:, 0:16]
        suug_v = vecs_s[:, 16:32]
        wscup_v = vecs_s[:, 32:96]
        wscg_v = vecs_s[:, 96:160]
        sudp_v = vecs_s[:, 160:224]
        wscdp_v = vecs_s[:, 224:240]
        su_a = dvecs_s[:, 0:16]
        su_b = dvecs_s[:, 16:80]
        wscup_su = dvecs_s[:, 80:144]

        ST = {n: i for i, n in enumerate(
            ["ssum", "ssq", "m", "q2", "mm", "var", "varp", "rp", "qq", "qq2",
             "rp2", "rstd", "rstd2", "nmr", "onep"])}

        def st(n, p):
            c = ST[n] * 2 + p
            return stats_s[:, c:c + 1]

        pA, pB, pC, pD = PBuf(), PBuf(), PBuf(), PBuf()
        pE, pF = PBuf(), PBuf()
        pT0, pT1 = PBuf(), PBuf()
        pA.ap, pB.ap = pAB_t[:, 0:512], pAB_t[:, 512:1024]
        pC.ap, pD.ap = pCD_t[:, 0:512], pCD_t[:, 512:1024]
        pA.full, pC.full = pAB_t[:], pCD_t[:]
        pE.ap, pF.ap = pE_t[:, 0:128], pF_t[:, 0:128]
        pE.full, pF.full = pE_t[:], pF_t[:]
        pT0.ap, pT1.ap = pT_t[:, 0:1024], pT_t[:, 1024:2048]

        xstage_b2 = [PBuf(), PBuf()]
        xnb = [PBuf(), PBuf()]
        hstb = [PBuf(), PBuf()]
        wslot = [PBuf(), PBuf(), PBuf(), PBuf()]
        wslot_lbl = ["wa", "wb", "wc", "wd"]
        rot_b = PBuf()
        eb = [PBuf(), PBuf()]
        atb = [PBuf(), PBuf()]
        ub = [PBuf(), PBuf()]
        gb = [PBuf(), PBuf()]
        stat_par = [PBuf(), PBuf()]
        smax_par = [PBuf(), PBuf()]
        ost_b2 = [PBuf(), PBuf()]

        wload_i = [0]
        wring = [2]

        def wslot_ap(s, a, b2):
            if s < 2:
                return war[:, s * 4096 + a: s * 4096 + b2]
            return vtm[:, (s - 2) * 4096 + a: (s - 2) * 4096 + b2]

        def wload(in_ap, nelem=4096, extra=()):
            s = wload_i[0] % wring[0]
            wload_i[0] += 1
            b = wslot[s]
            out_ap = wslot_ap(s, 0, nelem)
            waits = b.begin_write() + list(extra)
            t = S.emit("gpsimd",
                       lambda g, o=out_ap, i=in_ap: g.dma_start(out=o, in_=i),
                       waits=waits, inc=wslot_lbl[s])
            b.set_w(t)
            return s, t

        def wchunk(s, d, w, off=0):
            return wslot_ap(s, d * w + off, d * w + w)

        taps = {}

        t_vecs = S.emit("sync", lambda sy: sy.dma_start(out=vecs_s[:], in_=vecs_d), inc="cc")
        t_h128 = S.emit("sync", lambda sy: sy.dma_start(out=h128_s[:], in_=h128_d), inc="cc")
        t_idb = S.emit("sync", lambda sy: sy.dma_start(out=identb_s[:], in_=identb_d), inc="cc")
        t_idf = S.emit("sync", lambda sy: sy.dma_start(out=identf_s[:], in_=identf_d), inc="cc")
        t_loraA = S.emit("gpsimd", lambda g: g.dma_start(out=loraA_s[:], in_=loraA_d), inc="cl")

        t1 = S.emit("vector", lambda v: v.tensor_scalar(
            su_a, suug_v, float(1.0 / np.sqrt(2048.0)), None, op0=ALU.mult),
            waits=[t_vecs], inc="dve")
        t2 = S.emit("vector", lambda v: v.tensor_scalar(
            su_b, sudp_v, float(1.0 / np.sqrt(8192.0)), None, op0=ALU.mult),
            inc="dve")
        t_dv = S.emit("vector", lambda v: v.tensor_mul(wscup_su, wscup_v, su_b),
                      waits=[t2], force=("dve",), inc="dve")


        acc_last = [[None] * 4, [None] * 4]
        store_toks = [[None] * 4, [None] * 4]
        phase_end = {}
        t_x1l_prev = [None]

        for p in range(2):
            n_tiles = 8 if p == 0 else 4
            tile_rows0 = 0 if p == 0 else 1024
            h_ready = []
            x0_ready = []
            gate_bigA = []
            gate_par = []
            if p == 1:
                gate_bigA = [phase_end["ffn_pe"], phase_end["ffn_dve"],
                             phase_end["ffn_act"]]
                gate_par = gate_bigA

            ptq_rot = [pT0, pT1]
            ptq_i = [0]

            for i in range(n_tiles):
                par2 = i % 2
                row0 = tile_rows0 + i * 128
                own = (p == 1) or (i >= 4)
                hc = row0 - tile_rows0
                xsb = xstage_b2[par2]
                gx = []
                if p == 1 and i < 2:
                    gx = ([phase_end["ug_pe"]] if par2 == 0
                          else [phase_end["ffn_pe"]])
                waits = xsb.begin_write() + gx
                tx = S.emit("sync", lambda sy, r=row0, pp=par2: sy.dma_start(
                    out=xstage_p(pp), in_=xs_d[r:r + 128, :]), waits=waits,
                    inc="cx" if par2 == 0 else "cx2")
                xsb.set_w(tx)

                sb = stat_par[par2]
                wstat = sb.begin_write()
                t_sum = S.emit("vector", lambda v, pp=par2: v.reduce_sum(
                    st("ssum", pp), xstage_p(pp), axis=AX.X),
                    waits=[tx] + wstat, inc="dve", force=("dve",))
                t_sq = S.emit("scalar", lambda s, pp=par2: s.activation(
                    hst_buf(pp), xstage_p(pp), AF.Square, accum_out=st("ssq", pp)),
                    waits=[tx] + wstat + hstb[par2].begin_write() +
                          (gate_par if i < 2 else []),
                    inc="act", force=("act",))

                def dvstep(fn, waits=()):
                    return S.emit("vector", fn, waits=list(waits), inc="dve",
                                  force=("dve",))

                td = dvstep(lambda v, pp=par2: v.tensor_scalar(
                    st("m", pp), st("ssum", pp), 1.0 / 2048.0, None, op0=ALU.mult),
                    [t_sum])
                td = dvstep(lambda v, pp=par2: v.tensor_scalar(
                    st("q2", pp), st("ssq", pp), 1.0 / 2048.0, None, op0=ALU.mult),
                    [td, t_sq])
                td = dvstep(lambda v, pp=par2: v.tensor_mul(
                    st("mm", pp), st("m", pp), st("m", pp)), [td])
                td = dvstep(lambda v, pp=par2: v.tensor_sub(
                    st("var", pp), st("q2", pp), st("mm", pp)), [td])
                td = dvstep(lambda v, pp=par2: v.tensor_scalar(
                    st("varp", pp), st("var", pp), 1e-5, None, op0=ALU.add), [td])
                td = dvstep(lambda v, pp=par2: v.reciprocal(
                    st("rp", pp), st("varp", pp)), [td])
                td = dvstep(lambda v, pp=par2: v.tensor_mul(
                    st("qq", pp), st("var", pp), st("rp", pp)), [td])
                td = dvstep(lambda v, pp=par2: v.tensor_scalar(
                    st("qq2", pp), st("qq", pp), 1e-6, None, op0=ALU.add), [td])
                td = dvstep(lambda v, pp=par2: v.reciprocal(
                    st("rp2", pp), st("qq2", pp)), [td])
                tsq1 = S.emit("scalar", lambda s, pp=par2: s.sqrt(
                    st("rstd", pp), st("rp", pp)), waits=[td], inc="act",
                    force=("act",))
                tsq2 = S.emit("scalar", lambda s, pp=par2: s.sqrt(
                    st("rstd2", pp), st("rp2", pp)), waits=[td], inc="act",
                    force=("act",))
                t_nmr = dvstep(lambda v, pp=par2: v.tensor_scalar(
                    st("nmr", pp), st("m", pp), st("rstd", pp), -1.0,
                    op0=ALU.mult, op1=ALU.mult), [tsq1])
                t_on = dvstep(lambda v, pp=par2: v.tensor_scalar(
                    st("onep", pp), st("rstd", pp), 1.0, None, op0=ALU.add), [tsq1])

                wxn = xnb[par2].begin_write()
                txn = S.emit("scalar", lambda s, pp=par2: s.activation(
                    xn_buf(pp), xstage_p(pp), AF.Identity, bias=st("nmr", pp),
                    scale=st("rstd", pp)),
                    waits=[t_nmr, tsq1] + wxn + (gate_par if i < 2 else []),
                    inc="act", force=("act",))
                xnb[par2].set_w(txn)
                t_xlast = txn
                if own:
                    tt = (row0 - (512 + p * 512)) // 128
                    tacc = S.emit("scalar", lambda s, pp=par2, t_=tt: s.activation(
                        acc[:, t_ * 2048:(t_ + 1) * 2048], xstage_p(pp), AF.Identity,
                        bias=st("nmr", pp), scale=st("onep", pp)),
                        waits=[t_on, t_nmr,
                               store_toks[0][tt] if p == 1 else None],
                        inc="act", force=("act",))
                    acc_last[p][tt] = tacc
                    t_xlast = tacc
                xsb.add_r(td)
                xsb.add_r(t_xlast)
                th = S.emit("scalar", lambda s, pp=par2: s.activation(
                    hst_buf(pp), xn_buf(pp), AF.Identity, scale=st("rstd2", pp)),
                    waits=[txn, tsq2], inc="act", force=("act",))
                hstb[par2].set_w(th)

                for half in range(2):
                    q = ptq_rot[ptq_i[0] % 2]
                    ptq_i[0] += 1
                    wq_ = q.begin_write()
                    tpe = None
                    for k in range(8):
                        f = half * 8 + k
                        tpe = S.emit("tensor", lambda t, f_=f, q_=q, k_=k, pp=par2:
                                     t.transpose(
                                         q_.ap[:, k_ * 128:(k_ + 1) * 128],
                                         hst_buf(pp)[:, f_ * 128:(f_ + 1) * 128],
                                         identb_s[:]),
                                     waits=([th, t_idb] + wq_ + gate_bigA)
                                     if k == 0 else [],
                                     inc="pe" if k == 7 else None)
                    q.set_w(tpe)
                    hstb[par2].add_r(tpe)
                    tev = None
                    for k in range(8):
                        f = half * 8 + k
                        tev = S.emit("vector", lambda v, f_=f, q_=q, k_=k, c_=hc:
                                     v.tensor_scalar(
                                         h_tile(f_, c_, 128),
                                         q_.ap[:, k_ * 128:(k_ + 1) * 128],
                                         normw_v[:, f_:f_ + 1], None, op0=ALU.mult),
                                     waits=[tpe] if k == 0 else [],
                                     inc="dve" if k == 7 else None)
                    q.add_r(tev)
                    h_ready.append(tev)
                if own:
                    oc = hc - (512 if p == 0 else 0)
                    for half in range(2):
                        q = ptq_rot[ptq_i[0] % 2]
                        ptq_i[0] += 1
                        wq_ = q.begin_write()
                        tpe = None
                        for k in range(8):
                            f = half * 8 + k
                            tpe = S.emit("tensor", lambda t, f_=f, q_=q, k_=k, pp=par2:
                                         t.transpose(
                                             q_.ap[:, k_ * 128:(k_ + 1) * 128],
                                             xn_buf(pp)[:, f_ * 128:(f_ + 1) * 128],
                                             identb_s[:]),
                                         waits=([txn, t_idb] + wq_ + gate_bigA)
                                         if k == 0 else [],
                                         inc="pe" if k == 7 else None)
                        q.set_w(tpe)
                        xnb[par2].add_r(tpe)
                        tev = None
                        for k in range(8):
                            f = half * 8 + k
                            tev = S.emit("vector", lambda v, f_=f, q_=q, k_=k, o_=oc:
                                         v.tensor_scalar(
                                             x0_tile(f_, o_, 128),
                                             q_.ap[:, k_ * 128:(k_ + 1) * 128],
                                             su_a[:, f_:f_ + 1], None, op0=ALU.mult),
                                         waits=[tpe, t1] if k == 0 else [],
                                         inc="dve" if k == 7 else None)
                        q.add_r(tev)
                        x0_ready.append(tev)

            x1_toks = []
            for f in range(NT):
                q = [pA, pB][f % 2]
                wq_ = q.begin_write()
                tpe = S.emit("tensor", lambda t, f_=f, q_=q: t.matmul(
                    q_.ap, h128_s[:], x0_tile(f_), start=True, stop=True),
                    waits=wq_ + x0_ready + [t_h128], inc="pe")
                q.set_w(tpe)
                tev = S.emit("vector", lambda v, f_=f, q_=q: v.tensor_copy(
                    x1b_tile(f_), q_.ap), waits=[tpe] + (gate_bigA if f < 2 else []),
                    inc="dve")
                q.add_r(tev)
                x1_toks.append(tev)
            cur_in, cur_out = x1b_tile, x0_tile
            stg_tok = x1_toks
            for s_i in range(4):
                newtoks = []
                for ho in range(2):
                    def bf_fn(v, s_=s_i, h_=ho, ci=cur_in, co=cur_out):
                        restv = (1 << s_) * 512
                        iv = ci(0, 0, 8192).rearrange("p (g t r) -> p g t r",
                                                      t=2, r=restv)
                        ov = co(0, 0, 8192).rearrange("p (g t r) -> p g t r",
                                                      t=2, r=restv)
                        if h_ == 0:
                            return v.tensor_add(ov[:, :, 0, :], iv[:, :, 0, :],
                                                iv[:, :, 1, :])
                        return v.tensor_sub(ov[:, :, 1, :], iv[:, :, 0, :],
                                            iv[:, :, 1, :])
                    tb = S.emit("vector", bf_fn,
                                waits=(stg_tok if ho == 0 else []),
                                inc="dve", force=("dve",))
                    newtoks.append(tb)
                stg_tok = newtoks
                cur_in, cur_out = cur_out, cur_in
            t_x1st = S.emit("sync", lambda sy: sy.dma_start(
                out=x1scr_d, in_=bigA[:, 24576:32768]),
                waits=stg_tok + [t_x1l_prev[0]], inc="cst")

            if p == 0:
                t_ropes = S.emit("sync", lambda sy: sy.dma_start(
                    out=ropes_f[:, 0:2048], in_=ropes_d[:, 0:2048]),
                    waits=[t_x1st], inc="cr")

                def cosv(c0, w):
                    return ropes_f[:, c0:c0 + w]

                def sinv(c0, w):
                    return ropes_f[:, 1024 + c0:1024 + c0 + w]
            else:
                t_ropes = S.emit("sync", lambda sy: sy.dma_start(
                    out=ropes_f[:, 0:1024], in_=ropes_d[:, 2048:3072]),
                    waits=[t_x1st], inc="cr")

                def cosv(c0, w):
                    return ropes_f[:, c0:c0 + w]

                def sinv(c0, w):
                    return ropes_f[:, 512 + c0:512 + c0 + w]

            if PHASES < 2:
                break
            own_c0 = 512 if p == 0 else 0
            ln_end = [T("act"), T("dve"), T("pe")]
            tlb = []
            for mi, (dst, c0) in enumerate([(lbq, 0), (lbk, 2048), (lbv, 4096)]):
                tt_ = S.emit("gpsimd", lambda g, d_=dst, c_=c0: g.dma_start(
                    out=d_, in_=loraB_d[:, c_:c_ + 2048]),
                    waits=ln_end if mi == 0 else [], inc="cl")
                tlb.append(tt_)
            t_lb = tlb[-1]

            s1_rot = [pA, pB, pC, pD]
            s1_jobs = [("q", s1q, 0, [(own_c0, 512, 0)]),
                       ("k", s1k, 256,
                        [(0, 512, 0), (512, 512, 512)] if p == 0 else [(0, 512, 0)]),
                       ("v", s1v, 512,
                        [(0, 512, 0), (512, 512, 512)] if p == 0 else [(0, 512, 0)])]
            s1_toks = {}
            ji = 0
            for name, dst, acol, groups in s1_jobs:
                for (hc0, w, dcol) in groups:
                    q = s1_rot[ji % 4]
                    ji += 1
                    wq_ = q.begin_write()
                    tpe = None
                    for d in range(NT):
                        tpe = S.emit("tensor", lambda t, d_=d, q_=q, a_=acol, h0=hc0,
                                     w_=w: t.matmul(
                                         q_.ap[0:16, 0:w_],
                                         loraA_s[:, a_ + d_ * 16: a_ + d_ * 16 + 16],
                                         h_tile(d_, h0, w_),
                                         start=(d_ == 0), stop=(d_ == 15)),
                                     waits=(wq_ + h_ready + [t_loraA])
                                     if d == 0 else [],
                                     inc="pe" if d == 15 else None)
                    q.set_w(tpe)
                    tev = S.emit("vector", lambda v, q_=q, d_=dst, c_=dcol, w_=w:
                                 v.tensor_scalar(d_[:, c_:c_ + w_],
                                                 q_.ap[0:16, 0:w_],
                                                 1.0 / 16.0, None, op0=ALU.mult),
                                 waits=[tpe] + (gate_par if ji <= 4 else []),
                                 inc="dve")
                    q.add_r(tev)
                    s1_toks.setdefault(name, []).append(tev)

            def rope_ops(buf_ap, cos_ap, sin_ap, t_ev):
                tr1 = S.emit("scalar", lambda s: s.mul(
                    rotb[0:64, :], buf_ap[64:128, :], -1.0),
                    waits=[t_ev, t_ropes] + rot_b.begin_write(),
                    inc="act", force=("act",))
                tr2 = S.emit("scalar", lambda s: s.copy(
                    rotb[64:128, :], buf_ap[0:64, :]), inc="act")
                rot_b.set_w(tr2)
                tm1 = S.emit("vector", lambda v: v.tensor_mul(buf_ap, buf_ap, cos_ap),
                             waits=[tr2], inc="dve")
                tm2 = S.emit("vector", lambda v: v.tensor_mul(rotb, rotb, sin_ap),
                             inc="dve", force=("dve",))
                tm3 = S.emit("vector", lambda v: v.tensor_add(buf_ap, buf_ap, rotb),
                             waits=[tm1, tm2], inc="dve", force=("dve",))
                rot_b.add_r(tm3)
                return tm3

            q_rope = [None] * 16
            k_rope = [[] for _ in range(16)]

            for mat_d, isQ in [(wqT_d, True), (wkT_d, False)]:
                for g in range(8):
                    sl, tw = wload(mat_d[g * 128:(g + 1) * 128, :])
                    for sub in range(2):
                        o = g * 2 + sub
                        if isQ:
                            segs = [(own_c0, 512, 0)]
                        else:
                            segs = ([(0, 512, 0), (512, 512, 512)]
                                    if p == 0 else [(0, 512, 0)])
                        for si, (hc0, w, kc0) in enumerate(segs):
                            q = [pA, pB][(o + si) % 2]
                            wq_ = q.begin_write()
                            tpe = None
                            for d in range(NT):
                                tpe = S.emit("tensor", lambda t, d_=d, q_=q, s_=sl,
                                             h0=hc0, w_=w, sub_=sub: t.matmul(
                                                 q_.ap[:, 0:w_],
                                                 wchunk(s_, d_, 256, sub_ * 128)[:, 0:128],
                                                 h_tile(d_, h0, w_),
                                                 start=(d_ == 0), stop=False),
                                             waits=(wq_ + [tw] + h_ready)
                                             if d == 0 else [], inc=None)
                            lb = lbq if isQ else lbk
                            s1buf = s1q if isQ else s1k
                            tpe = S.emit("tensor", lambda t, q_=q, o_=o, lb_=lb,
                                         s1_=s1buf, w_=w, kc=kc0: t.matmul(
                                             q_.ap[:, 0:w_],
                                             lb_[:, o_ * 128:(o_ + 1) * 128],
                                             s1_[:, kc:kc + w_],
                                             start=False, stop=True),
                                         waits=[t_lb] + s1_toks["q" if isQ else "k"],
                                         inc="pe")
                            q.set_w(tpe)
                            wslot[sl].add_r(tpe)
                            if isQ:
                                dst = q_tile(o)
                                tev = S.emit("vector", lambda v, d_=dst, q_=q:
                                             v.tensor_copy(d_, q_.ap),
                                             waits=[tpe, t_x1st], inc="dve")
                                q.add_r(tev)
                                q_rope[o] = rope_ops(dst, cosv(own_c0, 512),
                                                     sinv(own_c0, 512), tev)
                            else:
                                dst = kfm[:, o * 1024 + kc0: o * 1024 + kc0 + w]
                                war_gate = ([phase_end["attn_pe"]]
                                            if p == 1 else [])
                                tev = S.emit("vector", lambda v, d_=dst, q_=q:
                                             v.tensor_copy(d_, q_.ap),
                                             waits=[tpe] + war_gate, inc="dve")
                                q.add_r(tev)
                                k_rope[o].append(rope_ops(dst, cosv(kc0, w),
                                                          sinv(kc0, w), tev))

            nchunks = 8 if p == 0 else 4
            vp = [pA, pB, pC, pD]
            v_evac = {}
            vj = 0
            for g in range(8):
                sl, tw = wload(wvT_d[g * 128:(g + 1) * 128, :])
                for c in range(nchunks):
                    q = vp[vj % 4]
                    vj += 1
                    wq_ = q.begin_write()
                    col = 0
                    tpe = None
                    for d in range(NT):
                        tpe = S.emit("tensor", lambda t, d_=d, q_=q, s_=sl, c_=c,
                                     col_=col: t.matmul(
                                         q_.ap[:, col_:col_ + 256],
                                         h_tile(d_, c_ * 128, 128),
                                         wchunk(s_, d_, 256),
                                         start=(d_ == 0), stop=False),
                                     waits=(wq_ + [tw] + h_ready)
                                     if d == 0 else [], inc=None)
                    tpe = S.emit("tensor", lambda t, q_=q, g_=g, c_=c, col_=col:
                                 t.matmul(q_.ap[:, col_:col_ + 256],
                                          s1v[:, c_ * 128:(c_ + 1) * 128],
                                          lbv[:, g_ * 256:(g_ + 1) * 256],
                                          start=False, stop=True),
                                 waits=[t_lb] + s1_toks["v"], inc="pe")
                    q.set_w(tpe)
                    wslot[sl].add_r(tpe)
                    war_gate = [phase_end["attn_pe"]] if p == 1 else []
                    tev = S.emit("scalar", lambda s, q_=q, c_=c, g_=g, col_=col:
                                 s.copy(vtm[:, c_ * 2048 + g_ * 256:
                                            c_ * 2048 + (g_ + 1) * 256],
                                        q_.ap[:, col_:col_ + 256]),
                                 waits=[tpe] + war_gate, inc="act")
                    q.add_r(tev)
                    v_evac.setdefault(c, []).append(tev)

            qkv_pe_end = T("pe")
            if PHASES < 3:
                break

            t_masks = S.emit("sync", lambda sy: sy.dma_start(out=masks_s, in_=masks_d),
                             waits=[qkv_pe_end, T("dve"), T("act")], inc="cm")

            def kcols_for(qt):
                if p == 0:
                    sa = [(0, 512, qt * 128)]
                    sb = (512, 128, qt * 128 + 512)
                else:
                    jb = 512 - qt * 128
                    sa = [(0, jb, 512 + qt * 128)]
                    if qt > 0:
                        sa.append((jb, qt * 128, 0))
                    sb = (512, 128, qt * 128)
                return sa, sb

            def vchunk_for(qt, j5):
                if p == 0:
                    return qt + j5
                r = qt + j5
                return 4 + r if r < 4 else r - 4

            pA.r += pB.r
            pB.r = []
            pC.r += pD.r
            pD.r = []
            s_p = [pA, pC]
            av_p = [pE, pF]
            attn_done = []
            heads = [(qt, hh) for qt in range(4) for hh in range(16)]
            NH = len(heads)
            htok = {}

            def em_scores(gi):
                qt, hh = heads[gi]
                pi = gi % 2
                sa, sb = kcols_for(qt)
                qa = s_p[pi]
                wq_ = qa.begin_write()
                segs = sa + [sb]
                tsc = None
                for ii, (j0, w, kc) in enumerate(segs):
                    tsc = S.emit("tensor", lambda t, q_=qa, j_=j0, w_=w, kc_=kc,
                                 h_=hh, qt_=qt: t.matmul(
                                     q_.full[:, j_:j_ + w_],
                                     q_tile(h_, qt_ * 128, 128),
                                     kfm[:, h_ * 1024 + kc_: h_ * 1024 + kc_ + w_],
                                     start=True, stop=True),
                                 waits=(wq_ + [q_rope[hh]] + k_rope[hh])
                                 if ii == 0 else [],
                                 inc="pe" if ii == len(segs) - 1 else None)
                qa.set_w(tsc)
                htok[(gi, "s")] = tsc

            def em_softmax(gi):
                qt, hh = heads[gi]
                pi = gi % 2
                qa = s_p[pi]
                sm_b = smax_par[pi]
                e_b = eb[pi]
                wsm = sm_b.begin_write()
                mc = qt * 640 if p == 0 else 2560
                tmadd = S.emit("vector", lambda v, q_=qa, mc_=mc: v.tensor_add(
                    q_.full[:, 0:640], q_.full[:, 0:640], masks_s[:, mc_:mc_ + 640]),
                    waits=[htok[(gi, "s")], t_masks], inc="dve", force=("dve",))
                tm1 = S.emit("vector", lambda v, q_=qa, pp=pi: v.reduce_max(
                    smax_s[:, pp * 4: pp * 4 + 1], q_.full[:, 0:640], axis=AX.X,
                    negate=True), waits=[tmadd] + wsm, inc="dve", force=("dve",))
                we_ = e_b.begin_write()
                te1 = S.emit("scalar", lambda s, q_=qa, pp=pi: s.activation(
                    ebuf(pp)[:, 0:640], q_.full[:, 0:640], AF.Exp,
                    bias=smax_s[:, pp * 4: pp * 4 + 1],
                    accum_out=smax_s[:, pp * 4 + 2: pp * 4 + 3]),
                    waits=[tm1] + we_, inc="act")
                qa.add_r(te1)
                sm_b.add_r(te1)
                tri = S.emit("vector", lambda v, pp=pi: v.reciprocal(
                    smax_s[:, pp * 4 + 3: pp * 4 + 4],
                    smax_s[:, pp * 4 + 2: pp * 4 + 3]),
                    waits=[te1], inc="dve", force=("dve",))
                tnm = S.emit("vector", lambda v, pp=pi: v.tensor_scalar(
                    ebuf(pp)[:, 0:640], ebuf(pp)[:, 0:640],
                    smax_s[:, pp * 4 + 3: pp * 4 + 4], None, op0=ALU.mult),
                    waits=[tri], inc="dve", force=("dve",))
                e_b.set_w(tnm)
                sm_b.add_r(tnm)
                htok[(gi, "sm")] = tnm

            def em_transpose(gi):
                pi = gi % 2
                e_b, at_b = eb[pi], atb[pi]
                qx = [pT0, pT1][pi]
                wq_ = qx.begin_write()
                tpe = None
                for k in range(5):
                    tpe = S.emit("tensor", lambda t, q_=qx, k_=k, pp=pi: t.transpose(
                        q_.ap[:, k_ * 128:(k_ + 1) * 128],
                        ebuf(pp)[:, k_ * 128:(k_ + 1) * 128], identb_s[:]),
                        waits=([htok[(gi, "sm")]] + wq_) if k == 0 else [],
                        inc="pe" if k == 4 else None)
                qx.set_w(tpe)
                e_b.add_r(tpe)
                wat = at_b.begin_write()
                tat = S.emit("scalar", lambda s, q_=qx, pp=pi: s.copy(
                    atbuf(pp)[:, 0:640], q_.ap[:, 0:640]),
                    waits=[tpe] + wat, inc="act")
                qx.add_r(tat)
                at_b.set_w(tat)
                htok[(gi, "at")] = tat

            def em_av(gi):
                qt, hh = heads[gi]
                pi = gi % 2
                qv = av_p[pi]
                at_b = atb[pi]
                wq_ = qv.begin_write()
                tpe = None
                for j5 in range(5):
                    c = vchunk_for(qt, j5)
                    tpe = S.emit("tensor", lambda t, q_=qv, j_=j5, c_=c, h_=hh,
                                 pp=pi: t.matmul(
                                     q_.ap[:, 0:128],
                                     vtm[:, c_ * 2048 + h_ * 128:
                                         c_ * 2048 + (h_ + 1) * 128],
                                     atbuf(pp)[:, j_ * 128:(j_ + 1) * 128],
                                     start=(j_ == 0), stop=(j_ == 4)),
                                 waits=(wq_ + [htok[(gi, "at")]] +
                                        v_evac.get(c, [])) if j5 == 0 else [],
                                 inc="pe" if j5 == 4 else None)
                qv.set_w(tpe)
                at_b.add_r(tpe)
                tev = S.emit("scalar", lambda s, q_=qv, h_=hh, qt_=qt: s.copy(
                    ao_tile(h_, qt_ * 128, 128), q_.ap[:, 0:128]),
                    waits=[tpe, t_x1st, t_ropes], inc="act")
                qv.add_r(tev)
                attn_done.append(tev)

            for gi in range(NH + 2):
                if HEADS_ONLY is not None and gi < NH and gi not in HEADS_ONLY:
                    continue
                if gi < NH:
                    em_scores(gi)
                    em_softmax(gi)
                if ATTN_SUB >= 2 and 1 <= gi <= NH:
                    em_transpose(gi - 1)
                if ATTN_SUB >= 3 and gi >= 2:
                    em_av(gi - 2)
                if ATTN_BARRIER:
                    toks = [T("pe"), T("dve"), T("act")]
                    for e_ in ("tensor", "vector", "scalar"):
                        S.emit(e_, None, waits=toks, force=("pe", "dve", "act"))

            phase_end["attn_pe"] = T("pe")
            if PHASES < 4:
                break

            t_oB = S.emit("gpsimd", lambda g: g.dma_start(
                out=oBt, in_=loraB_d[:, 6144:8192]), waits=[qkv_pe_end], inc="cl")
            q = pC
            wq_ = q.begin_write()
            tpe = None
            for d in range(NT):
                tpe = S.emit("tensor", lambda t, d_=d, q_=q: t.matmul(
                    q_.ap[0:16, 0:512], loraA_s[:, 768 + d_ * 16: 768 + d_ * 16 + 16],
                    ao_tile(d_), start=(d_ == 0), stop=(d_ == 15)),
                    waits=(wq_ + attn_done) if d == 0 else [],
                    inc="pe" if d == 15 else None)
            q.set_w(tpe)
            t_s1o = S.emit("vector", lambda v, q_=q: v.tensor_scalar(
                s1o[:, 0:512], q_.ap[0:16, 0:512], 1.0 / 16.0, None, op0=ALU.mult),
                waits=[tpe], inc="dve")
            q.add_r(t_s1o)

            def emit_oevac(q, tpe, c):
                osb = ost_b2[c % 2]
                wo_ = osb.begin_write()
                tev = S.emit("scalar", lambda s, q_=q, pp=c % 2: s.copy(
                    ostage_p(pp), q_.ap), waits=[tpe] + wo_, inc="act")
                q.add_r(tev)
                osb.set_w(tev)
                last_add = None
                for tt in range(4):
                    qtr = [pE, pF][tt % 2]
                    wq2 = qtr.begin_write()
                    tpt = S.emit("tensor", lambda t, q_=qtr, t_=tt, pp=c % 2:
                                 t.transpose(
                                     q_.ap, ostage_p(pp)[:, t_ * 128:(t_ + 1) * 128],
                                     identf_s[:]),
                                 waits=[tev, t_idf] + wq2, inc="pe")
                    qtr.set_w(tpt)
                    tadd = S.emit("vector", lambda v, q_=qtr, t_=tt, c_=c:
                                  v.tensor_add(
                                      acc[:, t_ * 2048 + c_ * 128:
                                          t_ * 2048 + (c_ + 1) * 128],
                                      acc[:, t_ * 2048 + c_ * 128:
                                          t_ * 2048 + (c_ + 1) * 128],
                                      q_.ap),
                                  waits=[tpt, acc_last[p][tt]], inc="dve")
                    qtr.add_r(tadd)
                    acc_last[p][tt] = tadd
                    last_add = tadd
                osb.add_r(last_add)

            o_pending = None
            for g in range(8):
                sl, tw = wload(woT_d[g * 128:(g + 1) * 128, :])
                for sub in range(2):
                    c = g * 2 + sub
                    q = [pA, pB][c % 2]
                    wq_ = q.begin_write()
                    tpe = None
                    for d in range(NT):
                        tpe = S.emit("tensor", lambda t, d_=d, q_=q, s_=sl,
                                     sub_=sub: t.matmul(
                                         q_.ap,
                                         wchunk(s_, d_, 256, sub_ * 128)[:, 0:128],
                                         ao_tile(d_), start=(d_ == 0), stop=False),
                                     waits=(wq_ + [tw] + attn_done)
                                     if d == 0 else [], inc=None)
                    tpe = S.emit("tensor", lambda t, q_=q, c_=c: t.matmul(
                        q_.ap, oBt[:, c_ * 128:(c_ + 1) * 128], s1o[:, 0:512],
                        start=False, stop=True), waits=[t_oB, t_s1o], inc="pe")
                    q.set_w(tpe)
                    wslot[sl].add_r(tpe)
                    if o_pending is not None:
                        emit_oevac(*o_pending)
                    o_pending = (q, tpe, c)
            emit_oevac(*o_pending)

            phase_end["oproj_pe"] = T("pe")
            phase_end["oproj_dve"] = T("dve")
            phase_end["oproj_act"] = T("act")
            if PHASES < 5:
                break

            t_x1l = S.emit("sync", lambda sy: sy.dma_start(out=x1, in_=x1scr_d),
                           waits=[t_x1st, phase_end["oproj_pe"],
                                  phase_end["oproj_dve"], phase_end["oproj_act"]],
                           inc="x1l")
            wring[0] = 4
            wload_i[0] = wload_i[0] - (wload_i[0] % 2)
            vtm_gate = [phase_end["attn_pe"]]
            t_x1l_prev[0] = t_x1l
            z_toks = [None] * 64
            zh_tok = [None] * 64
            slot_ap = [z_tile(i) for i in range(64)]
            bspare = {"vector": spare_bf_ap, "gpsimd": spare_bf_ap2}
            bchain = {"vector": None, "gpsimd": None}
            bcnt = [0]
            pend_bf = []

            def emit_h128(mh):
                qh = [pE, pF][mh % 2]
                wqh = qh.begin_write()
                tph = S.emit("tensor", lambda t, m_=mh, q_=qh: t.matmul(
                    q_.full, h128_s[:], z_tile(m_), start=True, stop=True),
                    waits=wqh + [z_toks[mh]], inc="pe")
                qh.set_w(tph)
                tze = S.emit("scalar", lambda s, m_=mh, q_=qh: s.copy(
                    z_tile(m_), q_.full), waits=[tph], inc="act")
                qh.add_r(tze)
                zh_tok[mh] = tze
                pend_bf.append(mh)

            ready_pairs = []

            def queue_butterflies(mt):
                for s_i in range(6):
                    blk = 1 << (s_i + 1)
                    if (mt + 1) % blk != 0:
                        break
                    base = mt + 1 - blk
                    half = 1 << s_i
                    for k in range(half):
                        ready_pairs.append((base + k, base + k + half))

            def emit_pairs(maxn, dve_only=False):
                n = 0
                while ready_pairs and n < maxn:
                    ia, ib = ready_pairs.pop(0)
                    eng = "vector" if (dve_only or bcnt[0] % 2 == 0) else "gpsimd"
                    lbl = "dve" if eng == "vector" else "gp"
                    bcnt[0] += 1
                    a_ap = slot_ap[ia]
                    b_ap = slot_ap[ib]
                    sp_ap = bspare[eng]
                    tsub = S.emit(eng,
                                  lambda v, a=a_ap, b=b_ap, sp=sp_ap:
                                  v.tensor_sub(sp, a, b),
                                  waits=[zh_tok[ia], zh_tok[ib], bchain[eng]],
                                  inc=lbl, force=("dve", "gp"))
                    tadd = S.emit(eng,
                                  lambda v, a=a_ap, b=b_ap:
                                  v.tensor_add(a, a, b),
                                  waits=[tsub], inc=lbl, force=("dve", "gp"))
                    slot_ap[ib] = sp_ap
                    bspare[eng] = b_ap
                    zh_tok[ia] = tadd
                    zh_tok[ib] = tsub
                    bchain[eng] = tadd
                    n += 1
            for mg in range(32):
                sl_u, tw_u = wload(wupT_d[mg * 128:(mg + 1) * 128, :],
                                   extra=(vtm_gate if mg < 2 else ()))
                sl_g, tw_g = wload(wgT_d[mg * 128:(mg + 1) * 128, :],
                                   extra=(vtm_gate if mg < 2 else ()))
                for sub in range(2):
                    mt = mg * 2 + sub
                    pu = [pA, pC][mt % 2]
                    pg = [pB, pD][mt % 2]
                    wq_ = pu.begin_write()
                    tpe = None
                    for d in range(NT):
                        tpe = S.emit("tensor", lambda t, d_=d, q_=pu, s_=sl_u,
                                     sub_=sub: t.matmul(
                                         q_.ap,
                                         wchunk(s_, d_, 256, sub_ * 128)[:, 0:128],
                                         x1_tile(d_), start=(d_ == 0),
                                         stop=(d_ == 15)),
                                     waits=(wq_ + [tw_u, t_x1l])
                                     if d == 0 else [],
                                     inc="pe" if d == 15 else None)
                    pu.set_w(tpe)
                    wslot[sl_u].add_r(tpe)
                    wu_ = ub[mt % 2].begin_write()
                    tu = S.emit("vector", lambda v, q_=pu, m_=mt, pp=mt % 2:
                                v.tensor_scalar(ubuf(pp), q_.ap,
                                                wscup_su[:, m_:m_ + 1], None,
                                                op0=ALU.mult),
                                waits=[tpe, t_dv] + wu_, inc="dve")
                    pu.add_r(tu)
                    ub[mt % 2].set_w(tu)
                    wq_ = pg.begin_write()
                    tpe = None
                    for d in range(NT):
                        tpe = S.emit("tensor", lambda t, d_=d, q_=pg, s_=sl_g,
                                     sub_=sub: t.matmul(
                                         q_.ap,
                                         wchunk(s_, d_, 256, sub_ * 128)[:, 0:128],
                                         x1_tile(d_), start=(d_ == 0),
                                         stop=(d_ == 15)),
                                     waits=(wq_ + [tw_g, t_x1l])
                                     if d == 0 else [],
                                     inc="pe" if d == 15 else None)
                    pg.set_w(tpe)
                    wslot[sl_g].add_r(tpe)
                    wg_ = gb[mt % 2].begin_write()
                    tg = S.emit("scalar", lambda s, q_=pg, m_=mt, pp=mt % 2:
                                s.activation(gbuf(pp), q_.ap, AF.Silu,
                                             scale=wscg_v[:, m_:m_ + 1]),
                                waits=[tpe] + wg_, inc="act")
                    pg.add_r(tg)
                    gb[mt % 2].set_w(tg)
                    zgate = ([phase_end["oproj_pe"]] if mt >= 32
                             else [qkv_pe_end])
                    tz = S.emit("vector", lambda v, m_=mt, pp=mt % 2: v.tensor_mul(
                        z_tile(m_), gbuf(pp), ubuf(pp)),
                        waits=[tu, tg] + zgate, inc="dve", force=("dve",))
                    ub[mt % 2].add_r(tz)
                    gb[mt % 2].add_r(tz)
                    z_toks[mt] = tz
                    if mt >= 1:
                        emit_h128(mt - 1)
                    if mt == 63:
                        emit_h128(63)
                    for mtq in list(pend_bf):
                        if zh_tok[mtq] is None:
                            break
                        pend_bf.remove(mtq)
                        queue_butterflies(mtq)
                    emit_pairs(3, dve_only=True)
            emit_pairs(10 ** 9)
            phase_end["ug_pe"] = T("pe")
            for sp in (spare_bf_ap, spare_bf_ap2):
                for jx in range(64):
                    if slot_ap[jx] is sp:
                        free_slot = (bspare["vector"]
                                     if bspare["vector"] is not sp and
                                     bspare["vector"] is not spare_bf_ap and
                                     bspare["vector"] is not spare_bf_ap2
                                     else bspare["gpsimd"])
                        assert free_slot is not spare_bf_ap and free_slot is not spare_bf_ap2
                        tmv = S.emit("vector", lambda v, d_=free_slot, s_=sp:
                                     v.tensor_copy(d_, s_),
                                     waits=[zh_tok[jx], bchain["vector"],
                                            bchain["gpsimd"]],
                                     inc="dve", force=("dve",))
                        if bspare["vector"] is free_slot:
                            bspare["vector"] = sp
                        else:
                            bspare["gpsimd"] = sp
                        slot_ap[jx] = free_slot
                        zh_tok[jx] = tmv
                        bchain["vector"] = tmv
                        break
            zh_fin = zh_tok

            for cs in range(8):
                pair = [pA, pB] if cs % 2 == 0 else [pC, pD]

                def dload(li, cs_=cs):
                    return wload(wdT_d[(cs_ * 4 + li) * 128:
                                       (cs_ * 4 + li + 1) * 128, :], 4096)
                loads = [dload(0), dload(1), dload(2)]
                wq_ = pair[0].begin_write() + pair[1].begin_write()
                tpe = None
                for k in range(64):
                    j = DOWN_JSEQ[k]
                    sl, tw = loads[k // 16]
                    jj = k % 16
                    for ct in range(2):
                        q = pair[ct]
                        rhs_ap = slot_ap[j]
                        tpe = S.emit("tensor", lambda t, q_=q, s_=sl, jj_=jj,
                                     ct_=ct, k_=k, r_=rhs_ap: t.matmul(
                                         q_.ap,
                                         wslot_ap(s_, jj_ * 256 + ct_ * 128,
                                                  jj_ * 256 + ct_ * 128 + 128),
                                         r_,
                                         start=(k_ == 0), stop=(k_ == 63)),
                                     waits=((wq_ + [tw, zh_fin[j]])
                                            if (k == 0 and ct == 0) else
                                            ([tw, zh_fin[j]] if ct == 0 else [])),
                                     inc="pe" if (jj == 15 and ct == 1) else None)
                        if ct == 1 and jj == 15:
                            wslot[sl].add_r(tpe)
                    if jj == 15 and k // 16 + 3 < 4:
                        loads.append(dload(k // 16 + 3))
                for ct in range(2):
                    pair[ct].set_w(tpe)
                for ct in range(2):
                    c = cs * 2 + ct
                    q = pair[ct]
                    osb = ost_b2[c % 2]
                    wd_ = osb.begin_write()
                    tev = S.emit("scalar", lambda s, q_=q, c_=c, pp=c % 2: s.mul(
                        ostage_p(pp), q_.ap, wscdp_v[:, c_:c_ + 1]),
                        waits=[tpe] + wd_, inc="act")
                    q.add_r(tev)
                    osb.set_w(tev)
                    last_add = None
                    for tt in range(4):
                        qtr = [pE, pF][tt % 2]
                        wq2 = qtr.begin_write()
                        tpt = S.emit("tensor", lambda t, q_=qtr, t_=tt, pp=c % 2:
                                     t.transpose(
                                         q_.ap, ostage_p(pp)[:, t_ * 128:(t_ + 1) * 128],
                                         identf_s[:]),
                                     waits=[tev, t_idf] + wq2, inc="pe")
                        qtr.set_w(tpt)
                        tadd = S.emit("vector", lambda v, q_=qtr, t_=tt, c_=c:
                                      v.tensor_add(
                                          acc[:, t_ * 2048 + c_ * 128:
                                              t_ * 2048 + (c_ + 1) * 128],
                                          acc[:, t_ * 2048 + c_ * 128:
                                              t_ * 2048 + (c_ + 1) * 128],
                                          q_.ap),
                                      waits=[tpt, acc_last[p][tt]], inc="dve")
                        qtr.add_r(tadd)
                        acc_last[p][tt] = tadd
                        last_add = tadd
                    osb.add_r(last_add)

            phase_end["ffn_pe"] = T("pe")
            phase_end["ffn_dve"] = T("dve")
            phase_end["ffn_act"] = T("act")
            wring[0] = 2
            wload_i[0] = wload_i[0] - (wload_i[0] % 2)

            if PHASES < 6 and p == 0:
                for tt in range(4):
                    store_toks[p][tt] = S.emit("sync", lambda sy, t_=tt, p_=p:
                                               sy.dma_start(
                                                   out=out_d[p_ * 512 + t_ * 128:
                                                             p_ * 512 + (t_ + 1) * 128, :],
                                                   in_=acc[:, t_ * 2048:(t_ + 1) * 2048]),
                                               waits=[acc_last[p][tt]], inc="cst")
                break
            for tt in range(4):
                store_toks[p][tt] = S.emit("sync", lambda sy, t_=tt, p_=p:
                                           sy.dma_start(
                                               out=out_d[p_ * 512 + t_ * 128:
                                                         p_ * 512 + (t_ + 1) * 128, :],
                                               in_=acc[:, t_ * 2048:(t_ + 1) * 2048]),
                                           waits=[acc_last[p][tt]], inc="cst")

        tap_map = {
            "h": bigA[:, 0:16384], "qreg": bigA[:, 16384:24576],
            "ao": bigA[:, 24576:32768], "kfm": kfm, "vtm": vtm,
            "x1": x1, "z": bigA, "acc": acc, "x1scr": x1scr_d, "par": par,
        }
        for name, shape, dt in DEBUG_SPEC:
            src = tap_map[name]
            S.emit("sync", lambda sy, s_=src, d_=dbg_d[name]: sy.dma_start(
                out=d_, in_=s_),
                waits=[T("pe"), T("dve"), T("act"), T("gp")], inc="cst")

        S.emit("sync", None, waits=[T("cst")])

        def replay(eng_name):
            def f(eng):
                for (waits, fn, inc, amt) in S.ops[eng_name]:
                    for (lbl, val) in waits:
                        eng.wait_ge(sems[lbl], val)
                    if fn is None:
                        continue
                    inst = fn(eng)
                    if inc is not None:
                        inst.then_inc(sems[inc], amt)
            return f

        block.sync(replay("sync"))
        block.gpsimd(replay("gpsimd"))
        block.tensor(replay("tensor"))
        block.vector(replay("vector"))
        block.scalar(replay("scalar"))

    return nc



def _tile_proj(wT, ng, gw):
    d_in, n_out = wT.shape
    nc_ = d_in // 128
    return np.ascontiguousarray(
        wT.reshape(nc_, 128, ng, gw).transpose(2, 1, 0, 3).reshape(ng * 128, nc_ * gw))


DOWN_JSEQ = [v for i in range(32) for v in (i, i + 32)]


def _tile_down(wT):
    arr = wT.reshape(64, 128, 8, 256)
    arr = arr[DOWN_JSEQ]
    arr = arr.transpose(2, 0, 1, 3)
    arr = arr.reshape(8, 4, 16, 128, 256)
    return np.ascontiguousarray(
        arr.transpose(0, 1, 3, 2, 4).reshape(8 * 4 * 128, 16 * 256))


def make_in_maps(inputs):
    x = np.ascontiguousarray(np.asarray(inputs["L0_attn_x"], np.float32))
    wqT = _tile_proj(np.asarray(inputs["L0_attn_wq"], np.float32).T, 8, 256)
    wkT = _tile_proj(np.asarray(inputs["L0_attn_wk"], np.float32).T, 8, 256)
    wvT = _tile_proj(np.asarray(inputs["L0_attn_wv"], np.float32).T, 8, 256)
    woT = _tile_proj(np.asarray(inputs["L0_attn_wo"], np.float32).T, 8, 256)
    wupT = _tile_proj(np.asarray(inputs["L0_ffn_W_up"], np.float32).T, 32, 256)
    wgT = _tile_proj(np.asarray(inputs["L0_ffn_W_gate"], np.float32).T, 32, 256)
    wdT = _tile_down(np.asarray(inputs["L0_ffn_W_down"], np.float32).T)
    loraA = _pack_loraA(inputs["L0_attn_qA"], inputs["L0_attn_kA"],
                        inputs["L0_attn_vA"], inputs["L0_attn_oA"])
    loraB = _pack_loraB(inputs["L0_attn_qB"], inputs["L0_attn_kB"],
                        inputs["L0_attn_vB"], inputs["L0_attn_oB"])
    wsc = np.asarray(inputs["L0_ffn_Wscale_ug"], np.float32)
    vecs = np.concatenate([
        _pack_cols(np.asarray(inputs["L0_attn_norm_w"], np.float32), 16),
        _pack_cols(np.asarray(inputs["L0_ffn_SU_ug"], np.float32), 16),
        _pack_cols(wsc[:8192], 64),
        _pack_cols(wsc[8192:], 64),
        _pack_cols(np.asarray(inputs["L0_ffn_SU_dp"], np.float32), 64),
        _pack_cols(np.asarray(inputs["L0_ffn_Wscale_dp"], np.float32), 16),
    ], axis=1).astype(np.float32)
    h128 = _h128_bf16()
    identb = np.eye(128, dtype=BF_NP)
    identf = np.eye(128, dtype=np.float32)

    in_maps = []
    for c in range(NCORES):
        beta, gamma = c // 4, c % 4
        s = gamma * 1024
        xs = np.zeros((1536, D), np.float32)
        lo = s - 512
        src_lo = max(lo, 0)
        xs[src_lo - lo:, :] = x[beta, src_lo: s + 1024, :]
        in_maps.append({
            "xs": xs, "wqT": wqT, "wkT": wkT, "wvT": wvT, "woT": woT,
            "wupT": wupT, "wgT": wgT, "wdT": wdT,
            "loraA": loraA, "loraB": loraB,
            "masks": _masks_bf16(gamma), "ropes": _ropes_f32(gamma),
            "vecs": vecs, "h128": h128, "identb": identb, "identf": identf,
        })
    return in_maps


def kernel(**inputs):
    in_maps = make_in_maps(inputs)
    nc = build_nc()
    res = run_bass_kernel_spmd(nc, in_maps, core_ids=list(range(NCORES)))
    outs = [np.asarray(res.results[c]["out"], np.float32) for c in range(NCORES)]
    return np.stack([np.concatenate(outs[:4], axis=0),
                     np.concatenate(outs[4:], axis=0)])


# version 10
# speedup vs baseline: 1.6404x; 1.0095x over previous
import numpy as np
import ml_dtypes
import concourse.bass as bass
import concourse.mybir as mybir
from concourse.bass_utils import run_bass_kernel_spmd

F32 = mybir.dt.float32
BF16 = mybir.dt.bfloat16
AF = mybir.ActivationFunctionType
ALU = mybir.AluOpType
AX = mybir.AxisListType
BF_NP = ml_dtypes.bfloat16

D = 2048
NT = 16
TP = 512
NCORES = 8

OWNER = {
    "cx": "sync", "cst": "sync", "x1l": "sync", "cc": "sync", "cr": "sync",
    "cm": "sync", "ctr": "sync", "cts": "sync", "cx2": "sync",
    "cl": "gpsimd", "wa": "gpsimd", "wb": "gpsimd", "wc": "gpsimd",
    "wd": "gpsimd",
    "pe": "tensor", "dve": "vector", "act": "scalar", "gp": "gpsimd",
}
DMA_LBL = {"cx", "cx2", "cst", "x1l", "cl", "wa", "wb", "wc", "wd", "cc", "cr", "cm", "ctr", "cts"}

DEBUG_SPEC = []
PHASES = 99
ATTN_SUB = 3
TR_LIMIT = 9999
TR_EVAC = True
ATTN_BARRIER = False
HEADS_ONLY = None
SCORES_RAW = False


class Sch:
    ENGS = ("sync", "gpsimd", "tensor", "vector", "scalar")

    def __init__(self):
        self.ops = {e: [] for e in self.ENGS}
        self.cnt = {k: 0 for k in OWNER}

    def tok(self, lbl):
        return (lbl, self.cnt[lbl])

    def emit(self, eng, fn, waits=(), inc=None, force=()):
        wmax = {}
        for t in waits:
            if t is None:
                continue
            lbl, val = t[0], t[1]
            if val <= 0:
                continue
            if lbl in DMA_LBL:
                val = self.cnt[lbl]
                if val <= 0:
                    continue
            elif OWNER[lbl] == eng and lbl not in force:
                continue
            if wmax.get(lbl, 0) < val:
                wmax[lbl] = val
        amt = 0
        if inc is not None:
            amt = 16 if inc in DMA_LBL else 1
            self.cnt[inc] += amt
        self.ops[eng].append((sorted(wmax.items()), fn, inc, amt))
        return (inc, self.cnt[inc]) if inc is not None else None


class PBuf:
    def __init__(self):
        self.w = None
        self.r = []
        self.ap = None

    def begin_write(self):
        w = list(self.r)
        self.r = []
        return w

    def set_w(self, tok):
        self.w = tok

    def add_r(self, tok):
        if tok is not None:
            self.r.append(tok)



def _popcount_mat():
    i = np.arange(128)
    a = i[:, None] & i[None, :]
    pc = np.zeros_like(a)
    v = a.copy()
    while v.any():
        pc += v & 1
        v >>= 1
    return pc


def _h128_bf16():
    return np.where(_popcount_mat() % 2 == 0, 1.0, -1.0).astype(BF_NP)


def _masks_bf16(gamma):
    NEG = np.float32(-1e30)
    a = np.arange(128)[:, None]
    j = np.arange(512)[None, :]
    std = (j >= a + 1)
    b = np.arange(128)[None, :]
    caus = (b <= a)
    out = np.full((128, 3200), NEG, np.float32)
    for qt in range(4):
        m = std.copy()
        if gamma == 0:
            m &= (j >= 512 - qt * 128)
        out[:, qt * 640: qt * 640 + 512] = np.where(m, 0.0, NEG)
        out[:, qt * 640 + 512: qt * 640 + 640] = np.where(caus, 0.0, NEG)
    out[:, 2560:3072] = np.where(std, 0.0, NEG)
    out[:, 3072:3200] = np.where(caus, 0.0, NEG)
    return out.astype(BF_NP)


def _ropes_f32(gamma):
    half = 64
    theta = 1.0 / 10000.0 ** (np.arange(half, dtype=np.float64) / half)
    th = np.concatenate([theta, theta])[:, None]
    scale = 128.0 ** -0.25
    out = np.zeros((128, 3072), np.float32)
    g0 = gamma * 1024
    t0 = np.arange(g0 - 512, g0 + 512, dtype=np.float64)[None, :]
    c0 = np.cos(th * t0) * scale
    s0 = np.sin(th * t0) * scale
    neg = (t0[0] < 0)
    c0[:, neg] = 0.0
    s0[:, neg] = 0.0
    out[:, 0:1024] = c0
    out[:, 1024:2048] = s0
    t1 = np.arange(g0 + 512, g0 + 1024, dtype=np.float64)[None, :]
    out[:, 2048:2560] = np.cos(th * t1) * scale
    out[:, 2560:3072] = np.sin(th * t1) * scale
    return out


def _pack_cols(v, n):
    return np.ascontiguousarray(v.reshape(n, 128).T)


def _pack_loraA(qA, kA, vA, oA):
    outs = []
    for A in (qA, kA, vA, oA):
        AT = np.asarray(A, np.float32).T.reshape(16, 128, 16).transpose(1, 0, 2).reshape(128, 256)
        outs.append(AT)
    return np.ascontiguousarray(np.concatenate(outs, axis=1))


def _pack_loraB(qB, kB, vB, oB):
    return np.ascontiguousarray(np.concatenate(
        [np.asarray(m, np.float32).T for m in (qB, kB, vB, oB)], axis=1))



def build_nc():
    nc = bass.Bass("TRN2", target_bir_lowering=False)

    xs_d = nc.dram_tensor("xs", [1536, D], F32, kind="ExternalInput").ap()
    wqT_d = nc.dram_tensor("wqT", [1024, 4096], F32, kind="ExternalInput").ap()
    wkT_d = nc.dram_tensor("wkT", [1024, 4096], F32, kind="ExternalInput").ap()
    wvT_d = nc.dram_tensor("wvT", [1024, 4096], F32, kind="ExternalInput").ap()
    woT_d = nc.dram_tensor("woT", [1024, 4096], F32, kind="ExternalInput").ap()
    wupT_d = nc.dram_tensor("wupT", [4096, 4096], F32, kind="ExternalInput").ap()
    wgT_d = nc.dram_tensor("wgT", [4096, 4096], F32, kind="ExternalInput").ap()
    wdT_d = nc.dram_tensor("wdT", [4096, 4096], F32, kind="ExternalInput").ap()
    loraA_d = nc.dram_tensor("loraA", [128, 1024], F32, kind="ExternalInput").ap()
    loraB_d = nc.dram_tensor("loraB", [16, 8192], F32, kind="ExternalInput").ap()
    masks_d = nc.dram_tensor("masks", [128, 3200], BF16, kind="ExternalInput").ap()
    ropes_d = nc.dram_tensor("ropes", [128, 3072], F32, kind="ExternalInput").ap()
    vecs_d = nc.dram_tensor("vecs", [128, 240], F32, kind="ExternalInput").ap()
    h128_d = nc.dram_tensor("h128", [128, 128], BF16, kind="ExternalInput").ap()
    identb_d = nc.dram_tensor("identb", [128, 128], BF16, kind="ExternalInput").ap()
    identf_d = nc.dram_tensor("identf", [128, 128], F32, kind="ExternalInput").ap()
    out_d = nc.dram_tensor("out", [1024, D], F32, kind="ExternalOutput").ap()
    x1scr_d = nc.dram_tensor("x1scr", [128, 8192], BF16, kind="Internal").ap()
    dbg_d = {}
    for name, shape, dt in DEBUG_SPEC:
        dbg_d[name] = nc.dram_tensor("dbg_" + name, list(shape), dt,
                                     kind="ExternalOutput").ap()

    S = Sch()
    T = S.tok

    from contextlib import ExitStack
    es = ExitStack()
    with es:
        acc_t = es.enter_context(nc.sbuf_tensor("acc", [128, 8192], F32))
        kfm_t = es.enter_context(nc.sbuf_tensor("kfm", [128, 16384], BF16))
        vtm_t = es.enter_context(nc.sbuf_tensor("vtm", [128, 16384], BF16))
        bigA_t = es.enter_context(nc.sbuf_tensor("bigA", [128, 32768], BF16))
        par_t = es.enter_context(nc.sbuf_tensor("parena", [128, 12288], BF16))
        war_t = es.enter_context(nc.sbuf_tensor("warena", [128, 8192], BF16))
        loraA_s = es.enter_context(nc.sbuf_tensor("loraA_s", [128, 1024], BF16))
        h128_s = es.enter_context(nc.sbuf_tensor("h128_s", [128, 128], BF16))
        identb_s = es.enter_context(nc.sbuf_tensor("identb_s", [128, 128], BF16))
        identf_s = es.enter_context(nc.sbuf_tensor("identf_s", [128, 128], F32))
        vecs_s = es.enter_context(nc.sbuf_tensor("vecs_s", [128, 240], F32))
        dvecs_s = es.enter_context(nc.sbuf_tensor("dvecs", [128, 144], F32))
        stats_s = es.enter_context(nc.sbuf_tensor("stats", [128, 32], F32))
        smax_s = es.enter_context(nc.sbuf_tensor("smax", [128, 8], F32))
        pAB_t = es.enter_context(nc.psum_tensor("pAB", [128, 1024], F32))
        pCD_t = es.enter_context(nc.psum_tensor("pCD", [128, 1024], F32))
        pT_t = es.enter_context(nc.psum_tensor("pT", [128, 2048], BF16))
        pE_t = es.enter_context(nc.psum_tensor("pE", [128, 512], F32))
        pF_t = es.enter_context(nc.psum_tensor("pF", [128, 512], F32))
        s_cx = es.enter_context(nc.semaphore("s_cx"))
        s_cc = es.enter_context(nc.semaphore("s_cc"))
        s_cr = es.enter_context(nc.semaphore("s_cr"))
        s_cm = es.enter_context(nc.semaphore("s_cm"))
        s_ctr = es.enter_context(nc.semaphore("s_ctr"))
        s_cts = es.enter_context(nc.semaphore("s_cts"))
        s_cx2 = es.enter_context(nc.semaphore("s_cx2"))
        s_cst = es.enter_context(nc.semaphore("s_cst"))
        s_x1l = es.enter_context(nc.semaphore("s_x1l"))
        s_cl = es.enter_context(nc.semaphore("s_cl"))
        s_wa = es.enter_context(nc.semaphore("s_wa"))
        s_wb = es.enter_context(nc.semaphore("s_wb"))
        s_wc = es.enter_context(nc.semaphore("s_wc"))
        s_wd = es.enter_context(nc.semaphore("s_wd"))
        s_pe = es.enter_context(nc.semaphore("s_pe"))
        s_dve = es.enter_context(nc.semaphore("s_dve"))
        s_act = es.enter_context(nc.semaphore("s_act"))
        s_gp = es.enter_context(nc.semaphore("s_gp"))
        block = es.enter_context(nc.Block())

        sems = {"cx": s_cx, "cst": s_cst, "x1l": s_x1l, "cl": s_cl, "wa": s_wa,
                "wb": s_wb, "wc": s_wc, "wd": s_wd, "pe": s_pe, "dve": s_dve, "act": s_act, "gp": s_gp,
                "cc": s_cc, "cr": s_cr, "cm": s_cm, "ctr": s_ctr, "cts": s_cts, "cx2": s_cx2}

        acc = acc_t[:]
        kfm = kfm_t[:]
        vtm = vtm_t[:]
        bigA = bigA_t[:]
        par = par_t[:]
        war = war_t[:]

        def h_tile(f, c0, w):
            return bigA[:, f * 1024 + c0: f * 1024 + c0 + w]

        def q_tile(o, c0=0, w=TP):
            return bigA[:, 16384 + o * 512 + c0: 16384 + o * 512 + c0 + w]

        def ao_tile(o, c0=0, w=TP):
            return bigA[:, 24576 + o * 512 + c0: 24576 + o * 512 + c0 + w]

        def z_tile(m, c0=0, w=TP):
            return bigA[:, m * 512 + c0: m * 512 + c0 + w]

        ropes_f = bigA[:, 24576:32768].bitcast(F32)
        x0_tile = q_tile
        x1b_tile = ao_tile

        xstage0 = par[:, 0:4096].bitcast(F32)
        xstage1 = vtm[:, 0:4096].bitcast(F32)

        def xstage_p(pp):
            return xstage0 if pp == 0 else xstage1

        def xn_buf(p):
            return par[:, 4096 + p * 2048: 6144 + p * 2048]

        def hst_buf(p):
            return par[:, 8192 + p * 2048: 10240 + p * 2048]

        lbq = par[0:16, 0:2048]
        lbk = par[0:16, 2048:4096]
        lbv = par[0:16, 4096:6144]
        s1q = par[0:16, 6144:6656]
        s1k = par[0:16, 6656:7680]
        s1v = par[0:16, 7680:8704]
        rotb = par[:, 8704:9216]
        masks_s = par[:, 0:3200]

        def ebuf(p):
            return par[:, 3200 + p * 640: 3840 + p * 640]

        def atbuf(p):
            return par[:, 4480 + p * 640: 5120 + p * 640]

        s1o = par[0:16, 5760:6272]
        oBt = par[0:16, 6272:8320]
        def ostage_p(pp):
            return par[:, 10240 + pp * 1024: 11264 + pp * 1024].bitcast(F32)
        x1 = par[:, 0:8192]

        def x1_tile(f):
            return par[:, f * 512:(f + 1) * 512]

        def ubuf(p):
            return par[:, 8192 + p * 512: 8704 + p * 512]

        def gbuf(p):
            return par[:, 9216 + p * 512: 9728 + p * 512]

        spare_bf_ap = par[:, 10240:10752]
        spare_bf_ap2 = par[:, 10752:11264]

        normw_v = vecs_s[:, 0:16]
        suug_v = vecs_s[:, 16:32]
        wscup_v = vecs_s[:, 32:96]
        wscg_v = vecs_s[:, 96:160]
        sudp_v = vecs_s[:, 160:224]
        wscdp_v = vecs_s[:, 224:240]
        su_a = dvecs_s[:, 0:16]
        su_b = dvecs_s[:, 16:80]
        wscup_su = dvecs_s[:, 80:144]

        ST = {n: i for i, n in enumerate(
            ["ssum", "ssq", "m", "q2", "mm", "var", "varp", "rp", "qq", "qq2",
             "rp2", "rstd", "rstd2", "nmr", "onep"])}

        def st(n, p):
            c = ST[n] * 2 + p
            return stats_s[:, c:c + 1]

        pA, pB, pC, pD = PBuf(), PBuf(), PBuf(), PBuf()
        pE, pF = PBuf(), PBuf()
        pT0, pT1 = PBuf(), PBuf()
        pA.ap, pB.ap = pAB_t[:, 0:512], pAB_t[:, 512:1024]
        pC.ap, pD.ap = pCD_t[:, 0:512], pCD_t[:, 512:1024]
        pA.full, pC.full = pAB_t[:], pCD_t[:]
        pE.ap, pF.ap = pE_t[:, 0:128], pF_t[:, 0:128]
        pE.full, pF.full = pE_t[:], pF_t[:]
        pT0.ap, pT1.ap = pT_t[:, 0:1024], pT_t[:, 1024:2048]

        xstage_b2 = [PBuf(), PBuf()]
        xnb = [PBuf(), PBuf()]
        hstb = [PBuf(), PBuf()]
        wslot = [PBuf(), PBuf(), PBuf(), PBuf()]
        wslot_lbl = ["wa", "wb", "wc", "wd"]
        rot_b = PBuf()
        eb = [PBuf(), PBuf()]
        atb = [PBuf(), PBuf()]
        ub = [PBuf(), PBuf()]
        gb = [PBuf(), PBuf()]
        stat_par = [PBuf(), PBuf()]
        smax_par = [PBuf(), PBuf()]
        ost_b2 = [PBuf(), PBuf()]

        wload_i = [0]
        wring = [2]

        def wslot_ap(s, a, b2):
            if s < 2:
                return war[:, s * 4096 + a: s * 4096 + b2]
            return vtm[:, (s - 2) * 4096 + a: (s - 2) * 4096 + b2]

        def wload(in_ap, nelem=4096, extra=()):
            s = wload_i[0] % wring[0]
            wload_i[0] += 1
            b = wslot[s]
            out_ap = wslot_ap(s, 0, nelem)
            waits = b.begin_write() + list(extra)
            t = S.emit("gpsimd",
                       lambda g, o=out_ap, i=in_ap: g.dma_start(out=o, in_=i),
                       waits=waits, inc=wslot_lbl[s])
            b.set_w(t)
            return s, t

        def wchunk(s, d, w, off=0):
            return wslot_ap(s, d * w + off, d * w + w)

        taps = {}

        t_vecs = S.emit("sync", lambda sy: sy.dma_start(out=vecs_s[:], in_=vecs_d), inc="cc")
        t_h128 = S.emit("sync", lambda sy: sy.dma_start(out=h128_s[:], in_=h128_d), inc="cc")
        t_idb = S.emit("sync", lambda sy: sy.dma_start(out=identb_s[:], in_=identb_d), inc="cc")
        t_idf = S.emit("sync", lambda sy: sy.dma_start(out=identf_s[:], in_=identf_d), inc="cc")
        t_loraA = S.emit("gpsimd", lambda g: g.dma_start(out=loraA_s[:], in_=loraA_d), inc="cl")

        t1 = S.emit("vector", lambda v: v.tensor_scalar(
            su_a, suug_v, float(1.0 / np.sqrt(2048.0)), None, op0=ALU.mult),
            waits=[t_vecs], inc="dve")
        t2 = S.emit("vector", lambda v: v.tensor_scalar(
            su_b, sudp_v, float(1.0 / np.sqrt(8192.0)), None, op0=ALU.mult),
            inc="dve")
        t_dv = S.emit("vector", lambda v: v.tensor_mul(wscup_su, wscup_v, su_b),
                      waits=[t2], force=("dve",), inc="dve")


        acc_last = [[None] * 4, [None] * 4]
        store_toks = [[None] * 4, [None] * 4]
        phase_end = {}
        t_x1l_prev = [None]

        for p in range(2):
            n_tiles = 8 if p == 0 else 4
            tile_rows0 = 0 if p == 0 else 1024
            h_ready = []
            x0_ready = []
            gate_bigA = []
            gate_par = []
            if p == 1:
                gate_bigA = [phase_end["ffn_pe"], phase_end["ffn_dve"],
                             phase_end["ffn_act"]]
                gate_par = gate_bigA

            ptq_rot = [pT0, pT1]
            ptq_i = [0]

            for i in range(n_tiles):
                par2 = i % 2
                row0 = tile_rows0 + i * 128
                own = (p == 1) or (i >= 4)
                hc = row0 - tile_rows0
                xsb = xstage_b2[par2]
                gx = []
                if p == 1 and i < 2:
                    gx = ([phase_end["ug_pe"]] if par2 == 0
                          else [phase_end["ffn_pe"]])
                waits = xsb.begin_write() + gx
                tx = S.emit("sync", lambda sy, r=row0, pp=par2: sy.dma_start(
                    out=xstage_p(pp), in_=xs_d[r:r + 128, :]), waits=waits,
                    inc="cx" if par2 == 0 else "cx2")
                xsb.set_w(tx)

                sb = stat_par[par2]
                wstat = sb.begin_write()
                t_sum = S.emit("vector", lambda v, pp=par2: v.reduce_sum(
                    st("ssum", pp), xstage_p(pp), axis=AX.X),
                    waits=[tx] + wstat, inc="dve", force=("dve",))
                t_sq = S.emit("scalar", lambda s, pp=par2: s.activation(
                    hst_buf(pp), xstage_p(pp), AF.Square, accum_out=st("ssq", pp)),
                    waits=[tx] + wstat + hstb[par2].begin_write() +
                          (gate_par if i < 2 else []),
                    inc="act", force=("act",))

                def dvstep(fn, waits=()):
                    return S.emit("vector", fn, waits=list(waits), inc="dve",
                                  force=("dve",))

                td = dvstep(lambda v, pp=par2: v.tensor_scalar(
                    st("m", pp), st("ssum", pp), 1.0 / 2048.0, None, op0=ALU.mult),
                    [t_sum])
                td = dvstep(lambda v, pp=par2: v.tensor_scalar(
                    st("q2", pp), st("ssq", pp), 1.0 / 2048.0, None, op0=ALU.mult),
                    [td, t_sq])
                td = dvstep(lambda v, pp=par2: v.tensor_mul(
                    st("mm", pp), st("m", pp), st("m", pp)), [td])
                td = dvstep(lambda v, pp=par2: v.tensor_sub(
                    st("var", pp), st("q2", pp), st("mm", pp)), [td])
                td = dvstep(lambda v, pp=par2: v.tensor_scalar(
                    st("varp", pp), st("var", pp), 1e-5, None, op0=ALU.add), [td])
                td = dvstep(lambda v, pp=par2: v.reciprocal(
                    st("rp", pp), st("varp", pp)), [td])
                td = dvstep(lambda v, pp=par2: v.tensor_mul(
                    st("qq", pp), st("var", pp), st("rp", pp)), [td])
                td = dvstep(lambda v, pp=par2: v.tensor_scalar(
                    st("qq2", pp), st("qq", pp), 1e-6, None, op0=ALU.add), [td])
                td = dvstep(lambda v, pp=par2: v.reciprocal(
                    st("rp2", pp), st("qq2", pp)), [td])
                tsq1 = S.emit("scalar", lambda s, pp=par2: s.sqrt(
                    st("rstd", pp), st("rp", pp)), waits=[td], inc="act",
                    force=("act",))
                tsq2 = S.emit("scalar", lambda s, pp=par2: s.sqrt(
                    st("rstd2", pp), st("rp2", pp)), waits=[td], inc="act",
                    force=("act",))
                t_nmr = dvstep(lambda v, pp=par2: v.tensor_scalar(
                    st("nmr", pp), st("m", pp), st("rstd", pp), -1.0,
                    op0=ALU.mult, op1=ALU.mult), [tsq1])
                t_on = dvstep(lambda v, pp=par2: v.tensor_scalar(
                    st("onep", pp), st("rstd", pp), 1.0, None, op0=ALU.add), [tsq1])

                wxn = xnb[par2].begin_write()
                txn = S.emit("scalar", lambda s, pp=par2: s.activation(
                    xn_buf(pp), xstage_p(pp), AF.Identity, bias=st("nmr", pp),
                    scale=st("rstd", pp)),
                    waits=[t_nmr, tsq1] + wxn + (gate_par if i < 2 else []),
                    inc="act", force=("act",))
                xnb[par2].set_w(txn)
                t_xlast = txn
                if own:
                    tt = (row0 - (512 + p * 512)) // 128
                    tacc = S.emit("scalar", lambda s, pp=par2, t_=tt: s.activation(
                        acc[:, t_ * 2048:(t_ + 1) * 2048], xstage_p(pp), AF.Identity,
                        bias=st("nmr", pp), scale=st("onep", pp)),
                        waits=[t_on, t_nmr,
                               store_toks[0][tt] if p == 1 else None],
                        inc="act", force=("act",))
                    acc_last[p][tt] = tacc
                    t_xlast = tacc
                xsb.add_r(td)
                xsb.add_r(t_xlast)
                th = S.emit("scalar", lambda s, pp=par2: s.activation(
                    hst_buf(pp), xn_buf(pp), AF.Identity, scale=st("rstd2", pp)),
                    waits=[txn, tsq2], inc="act", force=("act",))
                hstb[par2].set_w(th)

                for half in range(2):
                    q = ptq_rot[ptq_i[0] % 2]
                    ptq_i[0] += 1
                    wq_ = q.begin_write()
                    tpe = None
                    for k in range(8):
                        f = half * 8 + k
                        tpe = S.emit("tensor", lambda t, f_=f, q_=q, k_=k, pp=par2:
                                     t.transpose(
                                         q_.ap[:, k_ * 128:(k_ + 1) * 128],
                                         hst_buf(pp)[:, f_ * 128:(f_ + 1) * 128],
                                         identb_s[:]),
                                     waits=([th, t_idb] + wq_ + gate_bigA)
                                     if k == 0 else [],
                                     inc="pe" if k == 7 else None)
                    q.set_w(tpe)
                    hstb[par2].add_r(tpe)
                    tev = None
                    for k in range(8):
                        f = half * 8 + k
                        tev = S.emit("vector", lambda v, f_=f, q_=q, k_=k, c_=hc:
                                     v.tensor_scalar(
                                         h_tile(f_, c_, 128),
                                         q_.ap[:, k_ * 128:(k_ + 1) * 128],
                                         normw_v[:, f_:f_ + 1], None, op0=ALU.mult),
                                     waits=[tpe] if k == 0 else [],
                                     inc="dve" if k == 7 else None)
                    q.add_r(tev)
                    h_ready.append(tev)
                if own:
                    oc = hc - (512 if p == 0 else 0)
                    for half in range(2):
                        q = ptq_rot[ptq_i[0] % 2]
                        ptq_i[0] += 1
                        wq_ = q.begin_write()
                        tpe = None
                        for k in range(8):
                            f = half * 8 + k
                            tpe = S.emit("tensor", lambda t, f_=f, q_=q, k_=k, pp=par2:
                                         t.transpose(
                                             q_.ap[:, k_ * 128:(k_ + 1) * 128],
                                             xn_buf(pp)[:, f_ * 128:(f_ + 1) * 128],
                                             identb_s[:]),
                                         waits=([txn, t_idb] + wq_ + gate_bigA)
                                         if k == 0 else [],
                                         inc="pe" if k == 7 else None)
                        q.set_w(tpe)
                        xnb[par2].add_r(tpe)
                        tev = None
                        for k in range(8):
                            f = half * 8 + k
                            tev = S.emit("vector", lambda v, f_=f, q_=q, k_=k, o_=oc:
                                         v.tensor_scalar(
                                             x0_tile(f_, o_, 128),
                                             q_.ap[:, k_ * 128:(k_ + 1) * 128],
                                             su_a[:, f_:f_ + 1], None, op0=ALU.mult),
                                         waits=[tpe, t1] if k == 0 else [],
                                         inc="dve" if k == 7 else None)
                        q.add_r(tev)
                        x0_ready.append(tev)

            x1_toks = []
            for f in range(NT):
                q = [pA, pB][f % 2]
                wq_ = q.begin_write()
                tpe = S.emit("tensor", lambda t, f_=f, q_=q: t.matmul(
                    q_.ap, h128_s[:], x0_tile(f_), start=True, stop=True),
                    waits=wq_ + x0_ready + [t_h128], inc="pe")
                q.set_w(tpe)
                tev = S.emit("vector", lambda v, f_=f, q_=q: v.tensor_copy(
                    x1b_tile(f_), q_.ap), waits=[tpe] + (gate_bigA if f < 2 else []),
                    inc="dve")
                q.add_r(tev)
                x1_toks.append(tev)
            cur_in, cur_out = x1b_tile, x0_tile
            stg_tok = x1_toks
            for s_i in range(4):
                newtoks = []
                for ho in range(2):
                    def bf_fn(v, s_=s_i, h_=ho, ci=cur_in, co=cur_out):
                        restv = (1 << s_) * 512
                        iv = ci(0, 0, 8192).rearrange("p (g t r) -> p g t r",
                                                      t=2, r=restv)
                        ov = co(0, 0, 8192).rearrange("p (g t r) -> p g t r",
                                                      t=2, r=restv)
                        if h_ == 0:
                            return v.tensor_add(ov[:, :, 0, :], iv[:, :, 0, :],
                                                iv[:, :, 1, :])
                        return v.tensor_sub(ov[:, :, 1, :], iv[:, :, 0, :],
                                            iv[:, :, 1, :])
                    tb = S.emit("vector", bf_fn,
                                waits=(stg_tok if ho == 0 else []),
                                inc="dve", force=("dve",))
                    newtoks.append(tb)
                stg_tok = newtoks
                cur_in, cur_out = cur_out, cur_in
            t_x1st = S.emit("sync", lambda sy: sy.dma_start(
                out=x1scr_d, in_=bigA[:, 24576:32768]),
                waits=stg_tok + [t_x1l_prev[0]], inc="cst")

            if p == 0:
                t_ropes = S.emit("sync", lambda sy: sy.dma_start(
                    out=ropes_f[:, 0:2048], in_=ropes_d[:, 0:2048]),
                    waits=[t_x1st], inc="cr")

                def cosv(c0, w):
                    return ropes_f[:, c0:c0 + w]

                def sinv(c0, w):
                    return ropes_f[:, 1024 + c0:1024 + c0 + w]
            else:
                t_ropes = S.emit("sync", lambda sy: sy.dma_start(
                    out=ropes_f[:, 0:1024], in_=ropes_d[:, 2048:3072]),
                    waits=[t_x1st], inc="cr")

                def cosv(c0, w):
                    return ropes_f[:, c0:c0 + w]

                def sinv(c0, w):
                    return ropes_f[:, 512 + c0:512 + c0 + w]

            if PHASES < 2:
                break
            own_c0 = 512 if p == 0 else 0
            ln_end = [T("act"), T("dve"), T("pe")]
            tlb = []
            for mi, (dst, c0) in enumerate([(lbq, 0), (lbk, 2048), (lbv, 4096)]):
                tt_ = S.emit("gpsimd", lambda g, d_=dst, c_=c0: g.dma_start(
                    out=d_, in_=loraB_d[:, c_:c_ + 2048]),
                    waits=ln_end if mi == 0 else [], inc="cl")
                tlb.append(tt_)
            t_lb = tlb[-1]

            s1_rot = [pA, pB, pC, pD]
            s1_jobs = [("q", s1q, 0, [(own_c0, 512, 0)]),
                       ("k", s1k, 256,
                        [(0, 512, 0), (512, 512, 512)] if p == 0 else [(0, 512, 0)]),
                       ("v", s1v, 512,
                        [(0, 512, 0), (512, 512, 512)] if p == 0 else [(0, 512, 0)])]
            s1_toks = {}
            ji = 0
            for name, dst, acol, groups in s1_jobs:
                for (hc0, w, dcol) in groups:
                    q = s1_rot[ji % 4]
                    ji += 1
                    wq_ = q.begin_write()
                    tpe = None
                    for d in range(NT):
                        tpe = S.emit("tensor", lambda t, d_=d, q_=q, a_=acol, h0=hc0,
                                     w_=w: t.matmul(
                                         q_.ap[0:16, 0:w_],
                                         loraA_s[:, a_ + d_ * 16: a_ + d_ * 16 + 16],
                                         h_tile(d_, h0, w_),
                                         start=(d_ == 0), stop=(d_ == 15)),
                                     waits=(wq_ + h_ready + [t_loraA])
                                     if d == 0 else [],
                                     inc="pe" if d == 15 else None)
                    q.set_w(tpe)
                    tev = S.emit("vector", lambda v, q_=q, d_=dst, c_=dcol, w_=w:
                                 v.tensor_scalar(d_[:, c_:c_ + w_],
                                                 q_.ap[0:16, 0:w_],
                                                 1.0 / 16.0, None, op0=ALU.mult),
                                 waits=[tpe] + (gate_par if ji <= 4 else []),
                                 inc="dve")
                    q.add_r(tev)
                    s1_toks.setdefault(name, []).append(tev)

            def rope_ops(buf_ap, cos_ap, sin_ap, t_ev):
                tr1 = S.emit("scalar", lambda s: s.mul(
                    rotb[0:64, :], buf_ap[64:128, :], -1.0),
                    waits=[t_ev, t_ropes] + rot_b.begin_write(),
                    inc="act", force=("act",))
                tr2 = S.emit("scalar", lambda s: s.copy(
                    rotb[64:128, :], buf_ap[0:64, :]), inc="act")
                rot_b.set_w(tr2)
                tm1 = S.emit("vector", lambda v: v.tensor_mul(buf_ap, buf_ap, cos_ap),
                             waits=[tr2], inc="dve")
                tm2 = S.emit("vector", lambda v: v.tensor_mul(rotb, rotb, sin_ap),
                             inc="dve", force=("dve",))
                tm3 = S.emit("vector", lambda v: v.tensor_add(buf_ap, buf_ap, rotb),
                             waits=[tm1, tm2], inc="dve", force=("dve",))
                rot_b.add_r(tm3)
                return tm3

            q_rope = [None] * 16
            k_rope = [[] for _ in range(16)]

            for mat_d, isQ in [(wqT_d, True), (wkT_d, False)]:
                for g in range(8):
                    sl, tw = wload(mat_d[g * 128:(g + 1) * 128, :])
                    for sub in range(2):
                        o = g * 2 + sub
                        if isQ:
                            segs = [(own_c0, 512, 0)]
                        else:
                            segs = ([(0, 512, 0), (512, 512, 512)]
                                    if p == 0 else [(0, 512, 0)])
                        for si, (hc0, w, kc0) in enumerate(segs):
                            q = [pA, pB][(o + si) % 2]
                            wq_ = q.begin_write()
                            tpe = None
                            for d in range(NT):
                                tpe = S.emit("tensor", lambda t, d_=d, q_=q, s_=sl,
                                             h0=hc0, w_=w, sub_=sub: t.matmul(
                                                 q_.ap[:, 0:w_],
                                                 wchunk(s_, d_, 256, sub_ * 128)[:, 0:128],
                                                 h_tile(d_, h0, w_),
                                                 start=(d_ == 0), stop=False),
                                             waits=(wq_ + [tw] + h_ready)
                                             if d == 0 else [], inc=None)
                            lb = lbq if isQ else lbk
                            s1buf = s1q if isQ else s1k
                            tpe = S.emit("tensor", lambda t, q_=q, o_=o, lb_=lb,
                                         s1_=s1buf, w_=w, kc=kc0: t.matmul(
                                             q_.ap[:, 0:w_],
                                             lb_[:, o_ * 128:(o_ + 1) * 128],
                                             s1_[:, kc:kc + w_],
                                             start=False, stop=True),
                                         waits=[t_lb] + s1_toks["q" if isQ else "k"],
                                         inc="pe")
                            q.set_w(tpe)
                            wslot[sl].add_r(tpe)
                            if isQ:
                                dst = q_tile(o)
                                tev = S.emit("vector", lambda v, d_=dst, q_=q:
                                             v.tensor_copy(d_, q_.ap),
                                             waits=[tpe, t_x1st], inc="dve")
                                q.add_r(tev)
                                q_rope[o] = rope_ops(dst, cosv(own_c0, 512),
                                                     sinv(own_c0, 512), tev)
                            else:
                                dst = kfm[:, o * 1024 + kc0: o * 1024 + kc0 + w]
                                war_gate = ([phase_end["attn_pe"]]
                                            if p == 1 else [])
                                tev = S.emit("vector", lambda v, d_=dst, q_=q:
                                             v.tensor_copy(d_, q_.ap),
                                             waits=[tpe] + war_gate, inc="dve")
                                q.add_r(tev)
                                k_rope[o].append(rope_ops(dst, cosv(kc0, w),
                                                          sinv(kc0, w), tev))

            nchunks = 8 if p == 0 else 4
            vp = [pA, pB, pC, pD]
            v_evac = {}
            vj = 0
            for g in range(8):
                sl, tw = wload(wvT_d[g * 128:(g + 1) * 128, :])
                for c in range(nchunks):
                    q = vp[vj % 4]
                    vj += 1
                    wq_ = q.begin_write()
                    col = 0
                    tpe = None
                    for d in range(NT):
                        tpe = S.emit("tensor", lambda t, d_=d, q_=q, s_=sl, c_=c,
                                     col_=col: t.matmul(
                                         q_.ap[:, col_:col_ + 256],
                                         h_tile(d_, c_ * 128, 128),
                                         wchunk(s_, d_, 256),
                                         start=(d_ == 0), stop=False),
                                     waits=(wq_ + [tw] + h_ready)
                                     if d == 0 else [], inc=None)
                    tpe = S.emit("tensor", lambda t, q_=q, g_=g, c_=c, col_=col:
                                 t.matmul(q_.ap[:, col_:col_ + 256],
                                          s1v[:, c_ * 128:(c_ + 1) * 128],
                                          lbv[:, g_ * 256:(g_ + 1) * 256],
                                          start=False, stop=True),
                                 waits=[t_lb] + s1_toks["v"], inc="pe")
                    q.set_w(tpe)
                    wslot[sl].add_r(tpe)
                    war_gate = [phase_end["attn_pe"]] if p == 1 else []
                    tev = S.emit("scalar", lambda s, q_=q, c_=c, g_=g, col_=col:
                                 s.copy(vtm[:, c_ * 2048 + g_ * 256:
                                            c_ * 2048 + (g_ + 1) * 256],
                                        q_.ap[:, col_:col_ + 256]),
                                 waits=[tpe] + war_gate, inc="act")
                    q.add_r(tev)
                    v_evac.setdefault(c, []).append(tev)

            qkv_pe_end = T("pe")
            if PHASES < 3:
                break

            t_masks = S.emit("sync", lambda sy: sy.dma_start(out=masks_s, in_=masks_d),
                             waits=[qkv_pe_end, T("dve"), T("act")], inc="cm")

            def kcols_for(qt):
                if p == 0:
                    sa = [(0, 512, qt * 128)]
                    sb = (512, 128, qt * 128 + 512)
                else:
                    jb = 512 - qt * 128
                    sa = [(0, jb, 512 + qt * 128)]
                    if qt > 0:
                        sa.append((jb, qt * 128, 0))
                    sb = (512, 128, qt * 128)
                return sa, sb

            def vchunk_for(qt, j5):
                if p == 0:
                    return qt + j5
                r = qt + j5
                return 4 + r if r < 4 else r - 4

            pA.r += pB.r
            pB.r = []
            pC.r += pD.r
            pD.r = []
            s_p = [pA, pC]
            av_p = [pE, pF]
            attn_done = []
            heads = [(qt, hh) for qt in range(4) for hh in range(16)]
            NH = len(heads)
            htok = {}

            def em_scores(gi):
                qt, hh = heads[gi]
                pi = gi % 2
                sa, sb = kcols_for(qt)
                qa = s_p[pi]
                wq_ = qa.begin_write()
                segs = sa + [sb]
                tsc = None
                for ii, (j0, w, kc) in enumerate(segs):
                    tsc = S.emit("tensor", lambda t, q_=qa, j_=j0, w_=w, kc_=kc,
                                 h_=hh, qt_=qt: t.matmul(
                                     q_.full[:, j_:j_ + w_],
                                     q_tile(h_, qt_ * 128, 128),
                                     kfm[:, h_ * 1024 + kc_: h_ * 1024 + kc_ + w_],
                                     start=True, stop=True),
                                 waits=(wq_ + [q_rope[hh]] + k_rope[hh])
                                 if ii == 0 else [],
                                 inc="pe" if ii == len(segs) - 1 else None)
                qa.set_w(tsc)
                htok[(gi, "s")] = tsc

            def em_softmax(gi):
                qt, hh = heads[gi]
                pi = gi % 2
                qa = s_p[pi]
                sm_b = smax_par[pi]
                e_b = eb[pi]
                wsm = sm_b.begin_write()
                mc = qt * 640 if p == 0 else 2560
                tmadd = S.emit("vector", lambda v, q_=qa, mc_=mc: v.tensor_add(
                    q_.full[:, 0:640], q_.full[:, 0:640], masks_s[:, mc_:mc_ + 640]),
                    waits=[htok[(gi, "s")], t_masks], inc="dve", force=("dve",))
                tm1 = S.emit("vector", lambda v, q_=qa, pp=pi: v.reduce_max(
                    smax_s[:, pp * 4: pp * 4 + 1], q_.full[:, 0:640], axis=AX.X,
                    negate=True), waits=[tmadd] + wsm, inc="dve", force=("dve",))
                we_ = e_b.begin_write()
                te1 = S.emit("scalar", lambda s, q_=qa, pp=pi: s.activation(
                    ebuf(pp)[:, 0:640], q_.full[:, 0:640], AF.Exp,
                    bias=smax_s[:, pp * 4: pp * 4 + 1],
                    accum_out=smax_s[:, pp * 4 + 2: pp * 4 + 3]),
                    waits=[tm1] + we_, inc="act")
                qa.add_r(te1)
                sm_b.add_r(te1)
                tri = S.emit("vector", lambda v, pp=pi: v.reciprocal(
                    smax_s[:, pp * 4 + 3: pp * 4 + 4],
                    smax_s[:, pp * 4 + 2: pp * 4 + 3]),
                    waits=[te1], inc="dve", force=("dve",))
                tnm = S.emit("vector", lambda v, pp=pi: v.tensor_scalar(
                    ebuf(pp)[:, 0:640], ebuf(pp)[:, 0:640],
                    smax_s[:, pp * 4 + 3: pp * 4 + 4], None, op0=ALU.mult),
                    waits=[tri], inc="dve", force=("dve",))
                e_b.set_w(tnm)
                sm_b.add_r(tnm)
                htok[(gi, "sm")] = tnm

            def em_transpose(gi):
                pi = gi % 2
                e_b, at_b = eb[pi], atb[pi]
                qx = [pT0, pT1][pi]
                wq_ = qx.begin_write()
                tpe = None
                for k in range(5):
                    tpe = S.emit("tensor", lambda t, q_=qx, k_=k, pp=pi: t.transpose(
                        q_.ap[:, k_ * 128:(k_ + 1) * 128],
                        ebuf(pp)[:, k_ * 128:(k_ + 1) * 128], identb_s[:]),
                        waits=([htok[(gi, "sm")]] + wq_) if k == 0 else [],
                        inc="pe" if k == 4 else None)
                qx.set_w(tpe)
                e_b.add_r(tpe)
                wat = at_b.begin_write()
                tat = S.emit("scalar", lambda s, q_=qx, pp=pi: s.copy(
                    atbuf(pp)[:, 0:640], q_.ap[:, 0:640]),
                    waits=[tpe] + wat, inc="act")
                qx.add_r(tat)
                at_b.set_w(tat)
                htok[(gi, "at")] = tat

            def em_av(gi):
                qt, hh = heads[gi]
                pi = gi % 2
                qv = av_p[pi]
                at_b = atb[pi]
                wq_ = qv.begin_write()
                tpe = None
                for j5 in range(5):
                    c = vchunk_for(qt, j5)
                    tpe = S.emit("tensor", lambda t, q_=qv, j_=j5, c_=c, h_=hh,
                                 pp=pi: t.matmul(
                                     q_.ap[:, 0:128],
                                     vtm[:, c_ * 2048 + h_ * 128:
                                         c_ * 2048 + (h_ + 1) * 128],
                                     atbuf(pp)[:, j_ * 128:(j_ + 1) * 128],
                                     start=(j_ == 0), stop=(j_ == 4)),
                                 waits=(wq_ + [htok[(gi, "at")]] +
                                        v_evac.get(c, [])) if j5 == 0 else [],
                                 inc="pe" if j5 == 4 else None)
                qv.set_w(tpe)
                at_b.add_r(tpe)
                tev = S.emit("scalar", lambda s, q_=qv, h_=hh, qt_=qt: s.copy(
                    ao_tile(h_, qt_ * 128, 128), q_.ap[:, 0:128]),
                    waits=[tpe, t_x1st, t_ropes], inc="act")
                qv.add_r(tev)
                attn_done.append(tev)

            for gi in range(NH + 2):
                if HEADS_ONLY is not None and gi < NH and gi not in HEADS_ONLY:
                    continue
                if gi < NH:
                    em_scores(gi)
                    em_softmax(gi)
                if ATTN_SUB >= 2 and 1 <= gi <= NH:
                    em_transpose(gi - 1)
                if ATTN_SUB >= 3 and gi >= 2:
                    em_av(gi - 2)
                if ATTN_BARRIER:
                    toks = [T("pe"), T("dve"), T("act")]
                    for e_ in ("tensor", "vector", "scalar"):
                        S.emit(e_, None, waits=toks, force=("pe", "dve", "act"))

            phase_end["attn_pe"] = T("pe")
            if PHASES < 4:
                break

            t_oB = S.emit("gpsimd", lambda g: g.dma_start(
                out=oBt, in_=loraB_d[:, 6144:8192]), waits=[qkv_pe_end], inc="cl")
            q = pC
            wq_ = q.begin_write()
            tpe = None
            for d in range(NT):
                tpe = S.emit("tensor", lambda t, d_=d, q_=q: t.matmul(
                    q_.ap[0:16, 0:512], loraA_s[:, 768 + d_ * 16: 768 + d_ * 16 + 16],
                    ao_tile(d_), start=(d_ == 0), stop=(d_ == 15)),
                    waits=(wq_ + attn_done) if d == 0 else [],
                    inc="pe" if d == 15 else None)
            q.set_w(tpe)
            t_s1o = S.emit("vector", lambda v, q_=q: v.tensor_scalar(
                s1o[:, 0:512], q_.ap[0:16, 0:512], 1.0 / 16.0, None, op0=ALU.mult),
                waits=[tpe], inc="dve")
            q.add_r(t_s1o)

            def emit_oevac(q, tpe, c):
                osb = ost_b2[c % 2]
                wo_ = osb.begin_write()
                tev = S.emit("scalar", lambda s, q_=q, pp=c % 2: s.copy(
                    ostage_p(pp), q_.ap), waits=[tpe] + wo_, inc="act")
                q.add_r(tev)
                osb.set_w(tev)
                last_add = None
                for tt in range(4):
                    qtr = [pE, pF][tt % 2]
                    wq2 = qtr.begin_write()
                    tpt = S.emit("tensor", lambda t, q_=qtr, t_=tt, pp=c % 2:
                                 t.transpose(
                                     q_.ap, ostage_p(pp)[:, t_ * 128:(t_ + 1) * 128],
                                     identf_s[:]),
                                 waits=[tev, t_idf] + wq2, inc="pe")
                    qtr.set_w(tpt)
                    tadd = S.emit("vector", lambda v, q_=qtr, t_=tt, c_=c:
                                  v.tensor_add(
                                      acc[:, t_ * 2048 + c_ * 128:
                                          t_ * 2048 + (c_ + 1) * 128],
                                      acc[:, t_ * 2048 + c_ * 128:
                                          t_ * 2048 + (c_ + 1) * 128],
                                      q_.ap),
                                  waits=[tpt, acc_last[p][tt]], inc="dve")
                    qtr.add_r(tadd)
                    acc_last[p][tt] = tadd
                    last_add = tadd
                osb.add_r(last_add)

            o_pending = None
            for g in range(8):
                sl, tw = wload(woT_d[g * 128:(g + 1) * 128, :])
                for sub in range(2):
                    c = g * 2 + sub
                    q = [pA, pB][c % 2]
                    wq_ = q.begin_write()
                    tpe = None
                    for d in range(NT):
                        tpe = S.emit("tensor", lambda t, d_=d, q_=q, s_=sl,
                                     sub_=sub: t.matmul(
                                         q_.ap,
                                         wchunk(s_, d_, 256, sub_ * 128)[:, 0:128],
                                         ao_tile(d_), start=(d_ == 0), stop=False),
                                     waits=(wq_ + [tw] + attn_done)
                                     if d == 0 else [], inc=None)
                    tpe = S.emit("tensor", lambda t, q_=q, c_=c: t.matmul(
                        q_.ap, oBt[:, c_ * 128:(c_ + 1) * 128], s1o[:, 0:512],
                        start=False, stop=True), waits=[t_oB, t_s1o], inc="pe")
                    q.set_w(tpe)
                    wslot[sl].add_r(tpe)
                    if o_pending is not None:
                        emit_oevac(*o_pending)
                    o_pending = (q, tpe, c)
            emit_oevac(*o_pending)

            phase_end["oproj_pe"] = T("pe")
            phase_end["oproj_dve"] = T("dve")
            phase_end["oproj_act"] = T("act")
            if PHASES < 5:
                break

            t_x1l = S.emit("sync", lambda sy: sy.dma_start(out=x1, in_=x1scr_d),
                           waits=[t_x1st, phase_end["oproj_pe"],
                                  phase_end["oproj_dve"], phase_end["oproj_act"]],
                           inc="x1l")
            wring[0] = 4
            wload_i[0] = wload_i[0] - (wload_i[0] % 2)
            vtm_gate = [phase_end["attn_pe"]]
            t_x1l_prev[0] = t_x1l
            z_toks = [None] * 64
            zh_tok = [None] * 64
            slot_ap = [z_tile(i) for i in range(64)]
            bspare = {"vector": spare_bf_ap, "gpsimd": spare_bf_ap2}
            bchain = {"vector": None, "gpsimd": None}
            bcnt = [0]
            pend_bf = []

            def emit_h128(mh):
                qh = [pE, pF][mh % 2]
                wqh = qh.begin_write()
                tph = S.emit("tensor", lambda t, m_=mh, q_=qh: t.matmul(
                    q_.full, h128_s[:], z_tile(m_), start=True, stop=True),
                    waits=wqh + [z_toks[mh]], inc="pe")
                qh.set_w(tph)
                tze = S.emit("scalar", lambda s, m_=mh, q_=qh: s.copy(
                    z_tile(m_), q_.full), waits=[tph], inc="act")
                qh.add_r(tze)
                zh_tok[mh] = tze
                pend_bf.append(mh)

            ready_pairs = []

            def queue_butterflies(mt):
                for s_i in range(6):
                    blk = 1 << (s_i + 1)
                    if (mt + 1) % blk != 0:
                        break
                    base = mt + 1 - blk
                    half = 1 << s_i
                    for k in range(half):
                        ready_pairs.append((base + k, base + k + half))

            def emit_pairs(maxn, dve_only=False):
                n = 0
                while ready_pairs and n < maxn:
                    ia, ib = ready_pairs.pop(0)
                    eng = "vector" if (dve_only or bcnt[0] % 2 == 0) else "gpsimd"
                    lbl = "dve" if eng == "vector" else "gp"
                    bcnt[0] += 1
                    a_ap = slot_ap[ia]
                    b_ap = slot_ap[ib]
                    sp_ap = bspare[eng]
                    tsub = S.emit(eng,
                                  lambda v, a=a_ap, b=b_ap, sp=sp_ap:
                                  v.tensor_sub(sp, a, b),
                                  waits=[zh_tok[ia], zh_tok[ib], bchain[eng]],
                                  inc=lbl, force=("dve", "gp"))
                    tadd = S.emit(eng,
                                  lambda v, a=a_ap, b=b_ap:
                                  v.tensor_add(a, a, b),
                                  waits=[tsub], inc=lbl, force=("dve", "gp"))
                    slot_ap[ib] = sp_ap
                    bspare[eng] = b_ap
                    zh_tok[ia] = tadd
                    zh_tok[ib] = tsub
                    bchain[eng] = tadd
                    n += 1
            for mg in range(32):
                sl_u, tw_u = wload(wupT_d[mg * 128:(mg + 1) * 128, :],
                                   extra=(vtm_gate if mg < 2 else ()))
                sl_g, tw_g = wload(wgT_d[mg * 128:(mg + 1) * 128, :],
                                   extra=(vtm_gate if mg < 2 else ()))
                for sub in range(2):
                    mt = mg * 2 + sub
                    pu = [pA, pC][mt % 2]
                    pg = [pB, pD][mt % 2]
                    wq_ = pu.begin_write()
                    tpe = None
                    for d in range(NT):
                        tpe = S.emit("tensor", lambda t, d_=d, q_=pu, s_=sl_u,
                                     sub_=sub: t.matmul(
                                         q_.ap,
                                         wchunk(s_, d_, 256, sub_ * 128)[:, 0:128],
                                         x1_tile(d_), start=(d_ == 0),
                                         stop=(d_ == 15)),
                                     waits=(wq_ + [tw_u, t_x1l])
                                     if d == 0 else [],
                                     inc="pe" if d == 15 else None)
                    pu.set_w(tpe)
                    wslot[sl_u].add_r(tpe)
                    wu_ = ub[mt % 2].begin_write()
                    tu = S.emit("vector", lambda v, q_=pu, m_=mt, pp=mt % 2:
                                v.tensor_scalar(ubuf(pp), q_.ap,
                                                wscup_su[:, m_:m_ + 1], None,
                                                op0=ALU.mult),
                                waits=[tpe, t_dv] + wu_, inc="dve")
                    pu.add_r(tu)
                    ub[mt % 2].set_w(tu)
                    wq_ = pg.begin_write()
                    tpe = None
                    for d in range(NT):
                        tpe = S.emit("tensor", lambda t, d_=d, q_=pg, s_=sl_g,
                                     sub_=sub: t.matmul(
                                         q_.ap,
                                         wchunk(s_, d_, 256, sub_ * 128)[:, 0:128],
                                         x1_tile(d_), start=(d_ == 0),
                                         stop=(d_ == 15)),
                                     waits=(wq_ + [tw_g, t_x1l])
                                     if d == 0 else [],
                                     inc="pe" if d == 15 else None)
                    pg.set_w(tpe)
                    wslot[sl_g].add_r(tpe)
                    wg_ = gb[mt % 2].begin_write()
                    tg = S.emit("scalar", lambda s, q_=pg, m_=mt, pp=mt % 2:
                                s.activation(gbuf(pp), q_.ap, AF.Silu,
                                             scale=wscg_v[:, m_:m_ + 1]),
                                waits=[tpe] + wg_, inc="act")
                    pg.add_r(tg)
                    gb[mt % 2].set_w(tg)
                    zgate = ([phase_end["oproj_pe"]] if mt >= 32
                             else [qkv_pe_end])
                    tz = S.emit("vector", lambda v, m_=mt, pp=mt % 2: v.tensor_mul(
                        z_tile(m_), gbuf(pp), ubuf(pp)),
                        waits=[tu, tg] + zgate, inc="dve", force=("dve",))
                    ub[mt % 2].add_r(tz)
                    gb[mt % 2].add_r(tz)
                    z_toks[mt] = tz
                    if mt >= 1:
                        emit_h128(mt - 1)
                    if mt == 63:
                        emit_h128(63)
                    for mtq in list(pend_bf):
                        if zh_tok[mtq] is None:
                            break
                        pend_bf.remove(mtq)
                        queue_butterflies(mtq)
                    emit_pairs(3, dve_only=True)
            if len(ready_pairs) == 63:
                head, s4s, s5s = ready_pairs[:15], ready_pairs[15:31], ready_pairs[31:]
                inter = []
                for k in range(16):
                    inter.append(s4s[k])
                    inter.append(s5s[k])
                    inter.append(s5s[16 + k])
                ready_pairs[:] = head + inter
            def dload_cs(li, cs_):
                return wload(wdT_d[(cs_ * 4 + li) * 128:
                                   (cs_ * 4 + li + 1) * 128, :], 4096)
            loads_cs0 = [dload_cs(0, 0), dload_cs(1, 0), dload_cs(2, 0)]
            emit_pairs(10 ** 9)
            phase_end["ug_pe"] = T("pe")
            for sp in (spare_bf_ap, spare_bf_ap2):
                for jx in range(64):
                    if slot_ap[jx] is sp:
                        free_slot = (bspare["vector"]
                                     if bspare["vector"] is not sp and
                                     bspare["vector"] is not spare_bf_ap and
                                     bspare["vector"] is not spare_bf_ap2
                                     else bspare["gpsimd"])
                        assert free_slot is not spare_bf_ap and free_slot is not spare_bf_ap2
                        tmv = S.emit("vector", lambda v, d_=free_slot, s_=sp:
                                     v.tensor_copy(d_, s_),
                                     waits=[zh_tok[jx], bchain["vector"],
                                            bchain["gpsimd"]],
                                     inc="dve", force=("dve",))
                        if bspare["vector"] is free_slot:
                            bspare["vector"] = sp
                        else:
                            bspare["gpsimd"] = sp
                        slot_ap[jx] = free_slot
                        zh_tok[jx] = tmv
                        bchain["vector"] = tmv
                        break
            zh_fin = zh_tok

            for cs in range(8):
                pair = [pA, pB] if cs % 2 == 0 else [pC, pD]

                def dload(li, cs_=cs):
                    return dload_cs(li, cs_)
                loads = loads_cs0 if cs == 0 else [dload(0), dload(1), dload(2)]
                wq_ = pair[0].begin_write() + pair[1].begin_write()
                tpe = None
                for k in range(64):
                    j = DOWN_JSEQ[k]
                    sl, tw = loads[k // 16]
                    jj = k % 16
                    for ct in range(2):
                        q = pair[ct]
                        rhs_ap = slot_ap[j]
                        tpe = S.emit("tensor", lambda t, q_=q, s_=sl, jj_=jj,
                                     ct_=ct, k_=k, r_=rhs_ap: t.matmul(
                                         q_.ap,
                                         wslot_ap(s_, jj_ * 256 + ct_ * 128,
                                                  jj_ * 256 + ct_ * 128 + 128),
                                         r_,
                                         start=(k_ == 0), stop=(k_ == 63)),
                                     waits=((wq_ + [tw, zh_fin[j]])
                                            if (k == 0 and ct == 0) else
                                            ([tw, zh_fin[j]] if ct == 0 else [])),
                                     inc="pe" if (jj == 15 and ct == 1) else None)
                        if ct == 1 and jj == 15:
                            wslot[sl].add_r(tpe)
                    if jj == 15 and k // 16 + 3 < 4:
                        loads.append(dload(k // 16 + 3))
                for ct in range(2):
                    pair[ct].set_w(tpe)
                for ct in range(2):
                    c = cs * 2 + ct
                    q = pair[ct]
                    osb = ost_b2[c % 2]
                    wd_ = osb.begin_write()
                    tev = S.emit("scalar", lambda s, q_=q, c_=c, pp=c % 2: s.mul(
                        ostage_p(pp), q_.ap, wscdp_v[:, c_:c_ + 1]),
                        waits=[tpe] + wd_, inc="act")
                    q.add_r(tev)
                    osb.set_w(tev)
                    last_add = None
                    for tt in range(4):
                        qtr = [pE, pF][tt % 2]
                        wq2 = qtr.begin_write()
                        tpt = S.emit("tensor", lambda t, q_=qtr, t_=tt, pp=c % 2:
                                     t.transpose(
                                         q_.ap, ostage_p(pp)[:, t_ * 128:(t_ + 1) * 128],
                                         identf_s[:]),
                                     waits=[tev, t_idf] + wq2, inc="pe")
                        qtr.set_w(tpt)
                        tadd = S.emit("vector", lambda v, q_=qtr, t_=tt, c_=c:
                                      v.tensor_add(
                                          acc[:, t_ * 2048 + c_ * 128:
                                              t_ * 2048 + (c_ + 1) * 128],
                                          acc[:, t_ * 2048 + c_ * 128:
                                              t_ * 2048 + (c_ + 1) * 128],
                                          q_.ap),
                                      waits=[tpt, acc_last[p][tt]], inc="dve")
                        qtr.add_r(tadd)
                        acc_last[p][tt] = tadd
                        last_add = tadd
                    osb.add_r(last_add)

            phase_end["ffn_pe"] = T("pe")
            phase_end["ffn_dve"] = T("dve")
            phase_end["ffn_act"] = T("act")
            wring[0] = 2
            wload_i[0] = wload_i[0] - (wload_i[0] % 2)

            if PHASES < 6 and p == 0:
                for tt in range(4):
                    store_toks[p][tt] = S.emit("sync", lambda sy, t_=tt, p_=p:
                                               sy.dma_start(
                                                   out=out_d[p_ * 512 + t_ * 128:
                                                             p_ * 512 + (t_ + 1) * 128, :],
                                                   in_=acc[:, t_ * 2048:(t_ + 1) * 2048]),
                                               waits=[acc_last[p][tt]], inc="cst")
                break
            for tt in range(4):
                store_toks[p][tt] = S.emit("sync", lambda sy, t_=tt, p_=p:
                                           sy.dma_start(
                                               out=out_d[p_ * 512 + t_ * 128:
                                                         p_ * 512 + (t_ + 1) * 128, :],
                                               in_=acc[:, t_ * 2048:(t_ + 1) * 2048]),
                                           waits=[acc_last[p][tt]], inc="cst")

        tap_map = {
            "h": bigA[:, 0:16384], "qreg": bigA[:, 16384:24576],
            "ao": bigA[:, 24576:32768], "kfm": kfm, "vtm": vtm,
            "x1": x1, "z": bigA, "acc": acc, "x1scr": x1scr_d, "par": par,
        }
        for name, shape, dt in DEBUG_SPEC:
            src = tap_map[name]
            S.emit("sync", lambda sy, s_=src, d_=dbg_d[name]: sy.dma_start(
                out=d_, in_=s_),
                waits=[T("pe"), T("dve"), T("act"), T("gp")], inc="cst")

        S.emit("sync", None, waits=[T("cst")])

        def replay(eng_name):
            def f(eng):
                for (waits, fn, inc, amt) in S.ops[eng_name]:
                    for (lbl, val) in waits:
                        eng.wait_ge(sems[lbl], val)
                    if fn is None:
                        continue
                    inst = fn(eng)
                    if inc is not None:
                        inst.then_inc(sems[inc], amt)
            return f

        block.sync(replay("sync"))
        block.gpsimd(replay("gpsimd"))
        block.tensor(replay("tensor"))
        block.vector(replay("vector"))
        block.scalar(replay("scalar"))

    return nc



def _tile_proj(wT, ng, gw):
    d_in, n_out = wT.shape
    nc_ = d_in // 128
    return np.ascontiguousarray(
        wT.reshape(nc_, 128, ng, gw).transpose(2, 1, 0, 3).reshape(ng * 128, nc_ * gw))


DOWN_JSEQ = [v for i in range(32) for v in (i, i + 32)]


def _tile_down(wT):
    arr = wT.reshape(64, 128, 8, 256)
    arr = arr[DOWN_JSEQ]
    arr = arr.transpose(2, 0, 1, 3)
    arr = arr.reshape(8, 4, 16, 128, 256)
    return np.ascontiguousarray(
        arr.transpose(0, 1, 3, 2, 4).reshape(8 * 4 * 128, 16 * 256))


def make_in_maps(inputs):
    x = np.ascontiguousarray(np.asarray(inputs["L0_attn_x"], np.float32))
    wqT = _tile_proj(np.asarray(inputs["L0_attn_wq"], np.float32).T, 8, 256)
    wkT = _tile_proj(np.asarray(inputs["L0_attn_wk"], np.float32).T, 8, 256)
    wvT = _tile_proj(np.asarray(inputs["L0_attn_wv"], np.float32).T, 8, 256)
    woT = _tile_proj(np.asarray(inputs["L0_attn_wo"], np.float32).T, 8, 256)
    wupT = _tile_proj(np.asarray(inputs["L0_ffn_W_up"], np.float32).T, 32, 256)
    wgT = _tile_proj(np.asarray(inputs["L0_ffn_W_gate"], np.float32).T, 32, 256)
    wdT = _tile_down(np.asarray(inputs["L0_ffn_W_down"], np.float32).T)
    loraA = _pack_loraA(inputs["L0_attn_qA"], inputs["L0_attn_kA"],
                        inputs["L0_attn_vA"], inputs["L0_attn_oA"])
    loraB = _pack_loraB(inputs["L0_attn_qB"], inputs["L0_attn_kB"],
                        inputs["L0_attn_vB"], inputs["L0_attn_oB"])
    wsc = np.asarray(inputs["L0_ffn_Wscale_ug"], np.float32)
    vecs = np.concatenate([
        _pack_cols(np.asarray(inputs["L0_attn_norm_w"], np.float32), 16),
        _pack_cols(np.asarray(inputs["L0_ffn_SU_ug"], np.float32), 16),
        _pack_cols(wsc[:8192], 64),
        _pack_cols(wsc[8192:], 64),
        _pack_cols(np.asarray(inputs["L0_ffn_SU_dp"], np.float32), 64),
        _pack_cols(np.asarray(inputs["L0_ffn_Wscale_dp"], np.float32), 16),
    ], axis=1).astype(np.float32)
    h128 = _h128_bf16()
    identb = np.eye(128, dtype=BF_NP)
    identf = np.eye(128, dtype=np.float32)

    in_maps = []
    for c in range(NCORES):
        beta, gamma = c // 4, c % 4
        s = gamma * 1024
        xs = np.zeros((1536, D), np.float32)
        lo = s - 512
        src_lo = max(lo, 0)
        xs[src_lo - lo:, :] = x[beta, src_lo: s + 1024, :]
        in_maps.append({
            "xs": xs, "wqT": wqT, "wkT": wkT, "wvT": wvT, "woT": woT,
            "wupT": wupT, "wgT": wgT, "wdT": wdT,
            "loraA": loraA, "loraB": loraB,
            "masks": _masks_bf16(gamma), "ropes": _ropes_f32(gamma),
            "vecs": vecs, "h128": h128, "identb": identb, "identf": identf,
        })
    return in_maps


def kernel(**inputs):
    in_maps = make_in_maps(inputs)
    nc = build_nc()
    res = run_bass_kernel_spmd(nc, in_maps, core_ids=list(range(NCORES)))
    outs = [np.asarray(res.results[c]["out"], np.float32) for c in range(NCORES)]
    return np.stack([np.concatenate(outs[:4], axis=0),
                     np.concatenate(outs[4:], axis=0)])
